# version 24
# speedup vs baseline: 1.0512x; 1.0293x over previous
import contextlib
import numpy as np
import ml_dtypes
import concourse.bass as bass
import concourse.mybir as mybir
from concourse.bass import IndirectOffsetOnAxis
from concourse.bass_utils import run_bass_kernel_spmd
from concourse.aot_env import allow_ici_all_to_all
from concourse.hw_specs import get_activation_tables

F32 = mybir.dt.float32
BF16 = mybir.dt.bfloat16
I32 = mybir.dt.int32
U32 = mybir.dt.uint32
AF = mybir.ActivationFunctionType
ALU = mybir.AluOpType
AX = mybir.AxisListType

NCORES = 8
TPC = 2048
DM = 1024
NCT = DM // 128
TCH = 512
NCH = TPC // TCH
NTT = TPC // 128
GROUPS4 = [[0, 1, 2, 3], [4, 5, 6, 7]]
GROUPS8 = [[0, 1, 2, 3, 4, 5, 6, 7]]
BFNP = np.dtype(ml_dtypes.bfloat16)


class Sem:
    __slots__ = ("h", "n")

    def __init__(self, h):
        self.h = h
        self.n = 0

    def inc(self, instr, k=1):
        instr.then_inc(self.h, k)
        self.n += k
        return instr

    def dma(self, instr):
        return self.inc(instr, 16)

    def inc16(self, instr):
        return self.inc(instr, 16)


def load_act_set(nc, eng, set_name):
    tabs = list(get_activation_tables(nc.m.arch).keys())
    idx = tabs.index(set_name)
    return eng.add_instruction(mybir.InstLoadActFuncSet(
        name=nc.get_next_instruction_name(), act_func_set_id=idx, ins=[], outs=[]))



def _finish(nc, es, *stacks):
    for st_ in stacks:
        if st_ is not None:
            st_.close()
    es.close()
    return nc

def build_l0(dbg=(), stop_after="all", races=True, fuse=None):
    if fuse is None:
        nc = bass.Bass("TRN2", target_bir_lowering=False, detect_race_conditions=races)
    else:
        nc = fuse["nc"]

    x_d = nc.dram_tensor("x", [TPC, DM], F32, kind="ExternalInput").ap()
    wq_d = nc.dram_tensor("wq_d", [512, DM], F32, kind="ExternalInput").ap()
    bq_d = nc.dram_tensor("bq_d", [4, 128], F32, kind="ExternalInput").ap()
    wqk_d = nc.dram_tensor("wqk_d", [DM, 512], F32, kind="ExternalInput").ap()
    bqk_d = nc.dram_tensor("bqk_d", [8, 128], F32, kind="ExternalInput").ap()
    wkv_d = nc.dram_tensor("wkv_d", [64, DM], F32, kind="ExternalInput").ap()
    bkv_d = nc.dram_tensor("bkv_d", [1, 64], F32, kind="ExternalInput").ap()
    wvu_d = nc.dram_tensor("wvu_d", [DM, 64], F32, kind="ExternalInput").ap()
    bvu_d = nc.dram_tensor("bvu_d", [8, 128], F32, kind="ExternalInput").ap()
    wpr_d = nc.dram_tensor("wpr_d", [DM, DM], F32, kind="ExternalInput").ap()
    bpr_d = nc.dram_tensor("bpr_d", [8, 128], F32, kind="ExternalInput").ap()
    gam_d = nc.dram_tensor("gam_d", [8, 128], F32, kind="ExternalInput").ap()
    bet_d = nc.dram_tensor("bet_d", [8, 128], F32, kind="ExternalInput").ap()
    f1q_d = nc.dram_tensor("f1q_d", [DM, 4096], I32, kind="ExternalInput").ap()
    f1s_d = nc.dram_tensor("f1s_d", [8, 4096], F32, kind="ExternalInput").ap()
    f1b_d = nc.dram_tensor("f1b_d", [32, 128], F32, kind="ExternalInput").ap()
    f2q_d = nc.dram_tensor("f2q_d", [4096, DM], I32, kind="ExternalInput").ap()
    f2s_d = nc.dram_tensor("f2s_d", [32, DM], F32, kind="ExternalInput").ap()
    f2b_d = nc.dram_tensor("f2b_d", [8, 128], F32, kind="ExternalInput").ap()

    if fuse is None:
        x1_d = nc.dram_tensor("x1", [TPC, DM], F32, kind="ExternalOutput").ap()
    else:
        x1_d = nc.dram_tensor("x1", [TPC, DM], F32).ap()
        fuse["x1_d"] = x1_d
    dout = {}
    for name, shape, dt in [
        ("xn", [128, NCT * TPC], F32), ("h", [128, NCT * TPC], F32),
        ("g0", [TPC, DM], F32), ("q", [128, NCT * TPC], F32),
        ("ckv", [8192, 64], F32), ("s0", [TPC, DM], F32),
        ("beff", [128, DM], F32), ("mb", [128, 8 * 1024], F32),
        ("yac", [128, 4 * DM], F32), ("ckv2x", [128, 8192], F32),
    ]:
        if name in dbg:
            dout[name] = nc.dram_tensor("dbg_" + name, shape, dt, kind="ExternalOutput").ap()

    ckv_bo = nc.dram_tensor("ckv_bo", [TPC, 64], F32)
    ckv_fu = nc.dram_tensor("ckv_fu", [8192, 64], F32)
    g0_dr = nc.dram_tensor("g0_dr", [TPC, DM], F32)
    y1_dr = nc.dram_tensor("y1_dr", [4096, TPC], BF16)
    w1b_dr = nc.dram_tensor("w1b_dr", [DM, 4096], BF16)
    w2b_dr = nc.dram_tensor("w2b_dr", [4096, DM], BF16)

    es = contextlib.ExitStack()
    E = es.enter_context
    with es:
        ident = E(nc.sbuf_tensor("ident", [128, 128], F32))
        ones_c = E(nc.sbuf_tensor("ones_c", [128, 1], F32))
        ones_r = E(nc.sbuf_tensor("ones_r", [1, 128], F32))
        negK = E(nc.sbuf_tensor("negK", [128, 1], F32))
        qfm = E(nc.sbuf_tensor("qfm", [128, NCT * TPC], BF16))
        rows = E(nc.sbuf_tensor("rows", [1, 4 * TCH], F32))
        sqs = E(nc.sbuf_tensor("sqs", [128, TCH], F32))
        prm = E(nc.sbuf_tensor("prm", [128, 96], F32))
        ps = E(nc.psum_tensor("ps", [128, 4096], F32))

        PRM_GAM, PRM_BET, PRM_F1B, PRM_F2B, PRM_BQ, PRM_BQK, PRM_BKV, PRM_BPR, PRM_BVU = \
            0, 8, 16, 48, 56, 60, 68, 69, 77

        SE = E if fuse is None else (lambda s: fuse["E"](s))
        ld = Sem(SE(nc.semaphore("a_ld")))
        st = Sem(SE(nc.semaphore("a_st")))
        cm = Sem(SE(nc.semaphore("a_cm")))
        tp = Sem(SE(nc.semaphore("a_tp")))
        av = Sem(SE(nc.semaphore("a_av")))
        gm = Sem(SE(nc.semaphore("a_gm")))
        sc_s = Sem(SE(nc.semaphore("a_sc_s")))
        ex_s = Sem(SE(nc.semaphore("a_ex_s")))
        pv_s = Sem(SE(nc.semaphore("a_pv_s")))
        cc1 = Sem(SE(nc.semaphore("a_cc1")))

        block = E(nc.Block())

        @block.gpsimd
        def _(g):
            g.memset(ident[:], 0.0)
            g.affine_select(out=ident[:], in_=ident[:], compare_op=ALU.not_equal,
                            fill=1.0, base=0, pattern=[[-1, 128]], channel_multiplier=1)
            g.memset(ones_c[:], 1.0)
            g.memset(ones_r[:], 1.0)
            gm.inc(g.memset(negK[:], -20.0))

        @block.sync
        def _(s):
            for src, col, n in [(gam_d, PRM_GAM, 8), (bet_d, PRM_BET, 8),
                                (f1b_d, PRM_F1B, 32), (f2b_d, PRM_F2B, 8),
                                (bq_d, PRM_BQ, 4), (bqk_d, PRM_BQK, 8),
                                (bpr_d, PRM_BPR, 8), (bvu_d, PRM_BVU, 8)]:
                for r in range(n):
                    ld.dma(s.dma_start(out=prm[:, col + r:col + r + 1], in_=src[r:r + 1, :]))
            ld.dma(s.dma_start(out=prm[0:64, PRM_BKV:PRM_BKV + 1], in_=bkv_d[0:1, :]))
        p0_ld = ld.n

        esA = contextlib.ExitStack()
        with esA:
            EA = esA.enter_context
            xfm = EA(nc.sbuf_tensor("xfm", [128, NCT * TPC], F32))
            hb = EA(nc.sbuf_tensor("hb", [128, NCT * TPC], BF16))
            xt = EA(nc.sbuf_tensor("xt", [128, DM], F32))
            esA1 = contextlib.ExitStack()
            EA1 = esA1.enter_context
            xnb = EA1(nc.sbuf_tensor("xnb", [128, NCT * TCH], BF16))
            cqb = EA1(nc.sbuf_tensor("cqb", [128, 4 * TCH], BF16))
            wql = EA1(nc.sbuf_tensor("wql", [128, 32 * 128], BF16))
            wkl = EA1(nc.sbuf_tensor("wkl", [128, 32 * 128], BF16))
            wvl = EA1(nc.sbuf_tensor("wvl", [128, 8 * 64], BF16))
            ckvf = EA1(nc.sbuf_tensor("ckvf", [64, TPC], F32))
            ckvt = EA1(nc.sbuf_tensor("ckvt", [128, 4 * 64], F32))

            for tt in range(NTT):
                @block.sync
                def _(s, tt=tt):
                    if tt == 0:
                        s.wait_ge(gm.h, 1)
                        s.wait_ge(ld.h, p0_ld)
                    else:
                        s.wait_ge(cm.h, cm.n)
                    ld.dma(s.dma_start(out=xt[:], in_=x_d[tt * 128:(tt + 1) * 128, :]))

                @block.tensor
                def _(t, tt=tt):
                    t.wait_ge(ld.h, ld.n)
                    for ct in range(NCT):
                        tp.inc(t.transpose(ps[:, ct * 512:ct * 512 + 128],
                                           xt[:, ct * 128:(ct + 1) * 128], ident[:]))

                @block.vector
                def _(v, tt=tt):
                    v.wait_ge(tp.h, tp.n)
                    last = None
                    for ct in range(NCT):
                        last = v.tensor_copy(
                            out=xfm[:, ct * TPC + tt * 128:ct * TPC + (tt + 1) * 128],
                            in_=ps[:, ct * 512:ct * 512 + 128])
                    cm.inc(last)

            @block.scalar
            def _(s):
                load_act_set(nc, s, "natural_log_exp_and_others")
                av.inc(s.nop())

            r_mu = rows[0:1, 0:TCH]
            r_ms = rows[0:1, TCH:2 * TCH]
            r_va = rows[0:1, 2 * TCH:3 * TCH]
            r_rs = rows[0:1, 3 * TCH:4 * TCH]
            ps_sum = ps[0:1, 1024:1024 + TCH]
            ps_sq = ps[0:1, 1536:1536 + TCH]
            ps_mu = ps[:, 2048:2048 + TCH]
            ps_rs = ps[:, 2560:2560 + TCH]

            def emit_ln(affine):
                for tc in range(NCH):
                    c0 = tc * TCH

                    for ct in range(NCT):
                        @block.scalar
                        def _(s, ct=ct, c0=c0):
                            s.wait_ge(cm.h, cm.n)
                            if ct > 0:
                                s.wait_ge(tp.h, tp.n)
                            av.inc(s.activation(out=sqs[:],
                                                in_=xfm[:, ct * TPC + c0:ct * TPC + c0 + TCH],
                                                func=AF.Square))

                        @block.tensor
                        def _(t, ct=ct, c0=c0):
                            t.wait_ge(av.h, av.n)
                            t.matmul(ps_sum, ones_c[:],
                                     xfm[:, ct * TPC + c0:ct * TPC + c0 + TCH],
                                     start=(ct == 0), stop=(ct == NCT - 1))
                            tp.inc(t.matmul(ps_sq, ones_c[:], sqs[:],
                                            start=(ct == 0), stop=(ct == NCT - 1)))

                    @block.vector
                    def _(v, c0=c0):
                        v.wait_ge(tp.h, tp.n)
                        v.tensor_scalar(out=r_mu, in0=ps_sum, scalar1=1.0 / DM,
                                        scalar2=None, op0=ALU.mult)
                        v.tensor_scalar(out=r_ms, in0=ps_sq, scalar1=1.0 / DM,
                                        scalar2=None, op0=ALU.mult)
                        v.tensor_mul(out=r_va, in0=r_mu, in1=r_mu)
                        v.tensor_sub(out=r_va, in0=r_ms, in1=r_va)
                        cm.inc(v.tensor_scalar_add(out=r_va, in0=r_va, scalar1=1e-5))

                    @block.scalar
                    def _(s):
                        s.wait_ge(cm.h, cm.n)
                        s.activation(out=r_rs, in_=r_va, func=AF.Ln)
                        av.inc(s.activation(out=r_rs, in_=r_rs, func=AF.Exp, scale=-0.5))

                    @block.tensor
                    def _(t):
                        t.wait_ge(av.h, av.n)
                        t.matmul(ps_mu, ones_r[:], r_mu, start=True, stop=True)
                        tp.inc(t.matmul(ps_rs, ones_r[:], r_rs, start=True, stop=True))

                    @block.vector
                    def _(v, c0=c0):
                        v.wait_ge(tp.h, tp.n)
                        last = None
                        for ct in range(NCT):
                            sl = xfm[:, ct * TPC + c0:ct * TPC + c0 + TCH]
                            if not affine:
                                v.tensor_sub(out=sl, in0=sl, in1=ps_mu)
                                last = v.tensor_mul(out=sl, in0=sl, in1=ps_rs)
                            else:
                                v.tensor_sub(out=sqs[:], in0=sl, in1=ps_mu)
                                v.tensor_mul(out=sqs[:], in0=sqs[:], in1=ps_rs)
                                v.tensor_scalar(out=sqs[:], in0=sqs[:],
                                                scalar1=prm[:, PRM_GAM + ct:PRM_GAM + ct + 1],
                                                scalar2=None, op0=ALU.mult)
                                last = v.tensor_scalar(
                                    out=hb[:, ct * TPC + c0:ct * TPC + c0 + TCH],
                                    in0=sqs[:],
                                    scalar1=prm[:, PRM_BET + ct:PRM_BET + ct + 1],
                                    scalar2=None, op0=ALU.add)
                        cm.inc(last)

            emit_ln(False)
            _stop_ln0 = (stop_after == "ln0")

            if "xn" in dout:
                @block.gpsimd
                def _(g):
                    g.wait_ge(cm.h, cm.n)
                    st.dma(g.dma_start(out=dout["xn"], in_=xfm[:]))

            if stop_after == "ln0":
                @block.gpsimd
                def _(g):
                    g.wait_ge(st.h, st.n)
                return _finish(nc, es, esA1, esA)
            emit_ln(True)

            if "h" in dout:
                for ct in range(NCT):
                    for tc in range(NCH):
                        @block.vector
                        def _(v, ct=ct, tc=tc):
                            v.wait_ge(st.h, st.n)
                            cm.inc(v.tensor_copy(out=sqs[:],
                                                 in_=hb[:, ct * TPC + tc * TCH:ct * TPC + tc * TCH + TCH]))

                        @block.gpsimd
                        def _(g, ct=ct, tc=tc):
                            g.wait_ge(cm.h, cm.n)
                            st.dma(g.dma_start(
                                out=dout["h"][:, ct * TPC + tc * TCH:ct * TPC + tc * TCH + TCH],
                                in_=sqs[:]))

            if stop_after == "ln":
                @block.gpsimd
                def _(g):
                    g.wait_ge(st.h, st.n)
                return _finish(nc, es, esA1, esA)
            def emit_wT(src, nrows, ncols, dst, dst_block):
                nrt = (nrows + 127) // 128
                for i in range(nrt):
                    rpart = min(128, nrows - i * 128)
                    @block.sync
                    def _(s, i=i, rpart=rpart):
                        s.wait_ge(cm.h, cm.n)
                        ld.dma(s.dma_start(out=xt[0:rpart, 0:ncols],
                                           in_=src[i * 128:i * 128 + rpart, :]))

                    @block.tensor
                    def _(t, i=i, rpart=rpart):
                        t.wait_ge(ld.h, ld.n)
                        for ct in range(ncols // 128):
                            tp.inc(t.transpose(ps[:, (ct % 8) * 512:(ct % 8) * 512 + rpart],
                                               xt[0:rpart, ct * 128:(ct + 1) * 128],
                                               ident[0:rpart, 0:rpart]))

                    @block.vector
                    def _(v, i=i, rpart=rpart):
                        v.wait_ge(tp.h, tp.n)
                        last = None
                        for ct in range(ncols // 128):
                            off = dst_block(i, ct)
                            last = v.tensor_copy(out=dst[:, off:off + rpart],
                                                 in_=ps[:, (ct % 8) * 512:(ct % 8) * 512 + rpart])
                        cm.inc(last)

            emit_wT(wq_d, 512, DM, wql, lambda i, ct: (ct * 4 + i) * 128)
            emit_wT(wqk_d, DM, 512, wkl, lambda i, ct: (ct * 8 + i) * 128)
            emit_wT(wkv_d, 64, DM, wvl, lambda i, ct: ct * 64)

            for tc in range(NCH):
                c0 = tc * TCH

                @block.vector
                def _(v, c0=c0):
                    v.wait_ge(cm.h, cm.n)
                    last = None
                    for ct in range(NCT):
                        last = v.tensor_copy(out=xnb[:, ct * TCH:(ct + 1) * TCH],
                                             in_=xfm[:, ct * TPC + c0:ct * TPC + c0 + TCH])
                    cm.inc(last)

                @block.tensor
                def _(t, c0=c0):
                    t.wait_ge(cm.h, cm.n)
                    last = None
                    for qlt in range(4):
                        for ct in range(NCT):
                            last = t.matmul(ps[:, qlt * 512:qlt * 512 + TCH],
                                            wql[:, (ct * 4 + qlt) * 128:(ct * 4 + qlt + 1) * 128],
                                            xnb[:, ct * TCH:(ct + 1) * TCH],
                                            start=(ct == 0), stop=(ct == NCT - 1))
                    tp.inc(last)

                @block.vector
                def _(v):
                    v.wait_ge(tp.h, tp.n)
                    last = None
                    for qlt in range(4):
                        last = v.tensor_scalar(out=cqb[:, qlt * TCH:(qlt + 1) * TCH],
                                               in0=ps[:, qlt * 512:qlt * 512 + TCH],
                                               scalar1=prm[:, PRM_BQ + qlt:PRM_BQ + qlt + 1],
                                               scalar2=None, op0=ALU.add)
                    cm.inc(last)

                @block.tensor
                def _(t, c0=c0):
                    t.wait_ge(cm.h, cm.n)
                    last = None
                    for ft in range(NCT):
                        for qlt in range(4):
                            last = t.matmul(ps[:, ft * 512:ft * 512 + TCH],
                                            wkl[:, (qlt * 8 + ft) * 128:(qlt * 8 + ft + 1) * 128],
                                            cqb[:, qlt * TCH:(qlt + 1) * TCH],
                                            start=(qlt == 0), stop=(qlt == 3))
                    tp.inc(last)

                @block.vector
                def _(v, c0=c0):
                    v.wait_ge(tp.h, tp.n)
                    last = None
                    for ft in range(NCT):
                        last = v.tensor_scalar(out=qfm[:, ft * TPC + c0:ft * TPC + c0 + TCH],
                                               in0=ps[:, ft * 512:ft * 512 + TCH],
                                               scalar1=prm[:, PRM_BQK + ft:PRM_BQK + ft + 1],
                                               scalar2=None, op0=ALU.add)
                    cm.inc(last)

                @block.tensor
                def _(t, c0=c0):
                    t.wait_ge(cm.h, cm.n)
                    last = None
                    for ct in range(NCT):
                        last = t.matmul(ps[0:64, 0:TCH],
                                        wvl[:, ct * 64:(ct + 1) * 64],
                                        xnb[:, ct * TCH:(ct + 1) * TCH],
                                        start=(ct == 0), stop=(ct == NCT - 1))
                    tp.inc(last)

                @block.vector
                def _(v, c0=c0):
                    v.wait_ge(tp.h, tp.n)
                    cm.inc(v.tensor_scalar(out=ckvf[:, c0:c0 + TCH],
                                           in0=ps[0:64, 0:TCH],
                                           scalar1=prm[0:64, PRM_BKV:PRM_BKV + 1],
                                           scalar2=None, op0=ALU.add))

                @block.tensor
                def _(t, tc=tc, c0=c0):
                    t.wait_ge(cm.h, cm.n)
                    last = None
                    for j in range(4):
                        last = t.transpose(ps[:, 3072 + j * 128:3072 + j * 128 + 64],
                                           ckvf[0:64, c0 + j * 128:c0 + (j + 1) * 128],
                                           ident[0:64, 0:64])
                    tp.inc(last)

                @block.vector
                def _(v, tc=tc):
                    v.wait_ge(tp.h, tp.n)
                    v.wait_ge(st.h, st.n)
                    last = None
                    for j in range(4):
                        last = v.tensor_copy(out=ckvt[:, j * 64:(j + 1) * 64],
                                             in_=ps[:, 3072 + j * 128:3072 + j * 128 + 64])
                    cm.inc(last)

                @block.gpsimd
                def _(g, tc=tc):
                    g.wait_ge(cm.h, cm.n)
                    for j in range(4):
                        st.dma(g.dma_start(
                            out=ckv_bo.ap()[(tc * 4 + j) * 128:(tc * 4 + j + 1) * 128, :],
                            in_=ckvt[:, j * 64:(j + 1) * 64]))

            @block.gpsimd
            def _(g):
                g.wait_ge(st.h, st.n)
                cc1.inc(g.collective_compute(
                    "AllGather", ALU.bypass, replica_groups=GROUPS4,
                    ins=[ckv_bo.ap().opt()], outs=[ckv_fu.ap().opt()]))

            if "q" in dout:
                for ct in range(NCT):
                    for tc in range(NCH):
                        @block.vector
                        def _(v, ct=ct, tc=tc):
                            v.wait_ge(st.h, st.n)
                            cm.inc(v.tensor_copy(out=sqs[:],
                                                 in_=qfm[:, ct * TPC + tc * TCH:ct * TPC + tc * TCH + TCH]))

                        @block.gpsimd
                        def _(g, ct=ct, tc=tc):
                            g.wait_ge(cm.h, cm.n)
                            st.dma(g.dma_start(
                                out=dout["q"][:, ct * TPC + tc * TCH:ct * TPC + tc * TCH + TCH],
                                in_=sqs[:]))
            if "ckv" in dout:
                @block.gpsimd
                def _(g):
                    g.wait_ge(cc1.h, 1)
                    st.dma(g.dma_start(out=dout["ckv"], in_=ckv_fu.ap()))

            if stop_after == "qckv":
                @block.gpsimd
                def _(g):
                    g.wait_ge(cc1.h, 1)
                    g.wait_ge(st.h, st.n)
                return _finish(nc, es, esA1, esA)
            esA1.close()
            esA2 = contextlib.ExitStack()
            EA2 = esA2.enter_context
            qraw = EA2(nc.sbuf_tensor("qraw", [128, 2048], I32))
            qflt = EA2(nc.sbuf_tensor("qflt", [128, 2048], F32))
            wbf = EA2(nc.sbuf_tensor("wbf", [128, 2048], BF16))
            wstr = EA2(nc.sbuf_tensor("wstr", [128, 32 * 128], BF16))
            ysb = EA2(nc.sbuf_tensor("ysb", [128, 8 * TCH], BF16))
            vsc = EA2(nc.sbuf_tensor("vsc", [128, TCH], F32))
            tsc = EA2(nc.sbuf_tensor("tsc", [128, TCH], F32))
            gsc = EA2(nc.sbuf_tensor("gsc", [128, TCH], F32))
            ybf = EA2(nc.sbuf_tensor("ybf", [128, 2 * TCH], BF16))
            gtk = EA2(nc.sbuf_tensor("gtk", [128, 4 * DM], F32))

            @block.scalar
            def _(s):
                s.wait_ge(cm.h, cm.n)
                load_act_set(nc, s, "exp_and_others")
                av.inc(s.nop())

            def emit_dequant(q_src, s_src, w_dst, nkt, ncol, hist):
                W = min(2048, ncol)
                qld, bld, cast_av, stt_cm, wst_st = hist
                for kt in range(nkt):
                    for cc_ in range(ncol // W):
                        i = len(qld)

                        @block.sync
                        def _(s, kt=kt, cc_=cc_,
                              qfree=(cast_av[i - 1] if i >= 1 else 0),
                              rfree=(bld[i - 1] if i >= 1 else 0)):
                            if qfree:
                                s.wait_ge(av.h, qfree)
                                s.wait_ge(tp.h, rfree)
                            else:
                                s.wait_ge(av.h, av.n)
                                s.wait_ge(st.h, st.n)
                            ld.dma(s.dma_start(out=qraw[:, 0:W],
                                               in_=q_src[kt * 128:(kt + 1) * 128,
                                                         cc_ * W:(cc_ + 1) * W]))
                            ld.dma(s.dma_start(out=rows[0:1, 0:W],
                                               in_=s_src[kt:kt + 1, cc_ * W:(cc_ + 1) * W]))
                        qld.append(ld.n)

                        @block.scalar
                        def _(s, ldn=qld[-1],
                              qffree=(stt_cm[i - 1] if i >= 1 else 0)):
                            s.wait_ge(ld.h, ldn)
                            if qffree:
                                s.wait_ge(cm.h, qffree)
                            av.inc(s.copy(out=qflt[:, 0:W], in_=qraw[:, 0:W]))
                        cast_av.append(av.n)

                        @block.tensor
                        def _(t, i=i, ldn=qld[-1],
                              pfree=(stt_cm[i - 2] if i >= 2 else 0)):
                            t.wait_ge(ld.h, ldn)
                            if pfree:
                                t.wait_ge(cm.h, pfree)
                            else:
                                t.wait_ge(cm.h, cm.n)
                            last = None
                            for j in range(W // 512):
                                last = t.matmul(
                                    ps[:, (i % 2) * 2048 + j * 512:(i % 2) * 2048 + j * 512 + 512],
                                    ones_r[:], rows[0:1, j * 512:(j + 1) * 512],
                                    start=True, stop=True)
                            tp.inc(last)
                        bld.append(tp.n)

                        @block.vector
                        def _(v, i=i, avn=cast_av[-1], tpn=bld[-1],
                              wfree=(wst_st[i - 1] if i >= 1 else 0)):
                            v.wait_ge(av.h, avn)
                            v.wait_ge(tp.h, tpn)
                            if wfree:
                                v.wait_ge(st.h, wfree)
                            last = None
                            for j in range(W // 512):
                                last = v.scalar_tensor_tensor(
                                    out=wbf[:, j * 512:(j + 1) * 512],
                                    in0=qflt[:, j * 512:(j + 1) * 512], scalar=-128.0,
                                    in1=ps[:, (i % 2) * 2048 + j * 512:(i % 2) * 2048 + j * 512 + 512],
                                    op0=ALU.add, op1=ALU.mult)
                            cm.inc(last)
                        stt_cm.append(cm.n)

                        @block.gpsimd
                        def _(g, kt=kt, cc_=cc_, cmn=stt_cm[-1]):
                            g.wait_ge(cm.h, cmn)
                            st.dma(g.dma_start(
                                out=w_dst.ap()[kt * 128:(kt + 1) * 128,
                                               cc_ * W:(cc_ + 1) * W],
                                in_=wbf[:, 0:W]))
                        wst_st.append(st.n)

            dq_hist = ([], [], [], [], [])
            emit_dequant(f1q_d, f1s_d, w1b_dr, 8, 4096, dq_hist)
            emit_dequant(f2q_d, f2s_d, w2b_dr, 32, 1024, dq_hist)

            y1_hist = []
            f1_ld, f1_tp, f1_cm, f1_av, f1_mul = [], [], [], [], []
            ysb32 = ysb[:].bitcast(F32)
            for ft in range(32):
                @block.sync
                def _(s, ft=ft,
                      wfree=(f1_tp[(ft - 2) * NCH + NCH - 1] if ft >= 2 else 0)):
                    if ft == 0:
                        s.wait_ge(st.h, st.n)
                        s.wait_ge(tp.h, tp.n)
                    elif wfree:
                        s.wait_ge(tp.h, wfree)
                    for kt in range(8):
                        ld.dma(s.dma_start(
                            out=wstr[:, (ft % 2) * 1024 + kt * 128:(ft % 2) * 1024 + (kt + 1) * 128],
                            in_=w1b_dr.ap()[kt * 128:(kt + 1) * 128, ft * 128:(ft + 1) * 128]))
                f1_wld = ld.n

                for tc in range(NCH):
                    it = ft * NCH + tc

                    @block.tensor
                    def _(t, ft=ft, tc=tc, it=it, ldn=f1_wld,
                          bfree=(f1_cm[it - 2] if it >= 2 else 0)):
                        t.wait_ge(ld.h, ldn)
                        if bfree:
                            t.wait_ge(cm.h, bfree)
                        else:
                            t.wait_ge(cm.h, cm.n)
                        last = None
                        for kt in range(8):
                            last = t.matmul(ps[:, (it % 2) * 512:(it % 2) * 512 + TCH],
                                            wstr[:, (ft % 2) * 1024 + kt * 128:
                                                 (ft % 2) * 1024 + (kt + 1) * 128],
                                            hb[:, kt * TPC + tc * TCH:kt * TPC + tc * TCH + TCH],
                                            start=(kt == 0), stop=(kt == 7))
                        tp.inc(last)
                    f1_tp.append(tp.n)

                    vb = 'vsc' if it % 2 == 0 else 'y0'
                    @block.vector
                    def _(v, ft=ft, it=it, tpn=f1_tp[-1]):
                        v.wait_ge(tp.h, tpn)
                        dst = vsc[:] if it % 2 == 0 else ysb32[:, 0:TCH]
                        cm.inc(v.tensor_scalar(out=dst,
                                               in0=ps[:, (it % 2) * 512:(it % 2) * 512 + TCH],
                                               scalar1=prm[:, PRM_F1B + ft:PRM_F1B + ft + 1],
                                               scalar2=None, op0=ALU.add))
                    f1_cm.append(cm.n)

                    @block.scalar
                    def _(s, it=it, cmn=f1_cm[-1]):
                        s.wait_ge(cm.h, cmn)
                        src = vsc[:] if it % 2 == 0 else ysb32[:, 0:TCH]
                        dst = tsc[:] if it % 2 == 0 else ysb32[:, TCH:2 * TCH]
                        av.inc(s.activation(out=dst, in_=src, func=AF.Tanh, scale=0.5))
                    f1_av.append(av.n)

                    @block.vector
                    def _(v, it=it, avn=f1_av[-1]):
                        v.wait_ge(av.h, avn)
                        if len(y1_hist) >= 2:
                            v.wait_ge(st.h, y1_hist[-2])
                        tt_ = tsc[:] if it % 2 == 0 else ysb32[:, TCH:2 * TCH]
                        vv_ = vsc[:] if it % 2 == 0 else ysb32[:, 0:TCH]
                        v.tensor_scalar(out=tt_, in0=tt_, scalar1=0.5, scalar2=0.5,
                                        op0=ALU.mult, op1=ALU.add)
                        cm.inc(v.tensor_mul(out=ybf[:, (it % 2) * TCH:(it % 2 + 1) * TCH],
                                            in0=vv_, in1=tt_))
                    f1_mul.append(cm.n)

                    @block.gpsimd
                    def _(g, ft=ft, tc=tc, it=it, cmn=f1_mul[-1]):
                        g.wait_ge(cm.h, cmn)
                        st.dma(g.dma_start(
                            out=y1_dr.ap()[ft * 128:(ft + 1) * 128, tc * TCH:(tc + 1) * TCH],
                            in_=ybf[:, (it % 2) * TCH:(it % 2 + 1) * TCH]))
                    y1_hist.append(st.n)

            f2_ld, f2_tp = [], []
            for tc in range(NCH):
                for half in range(4):
                    @block.sync
                    def _(s, tc=tc, half=half):
                        s.wait_ge(st.h, st.n)
                        s.wait_ge(tp.h, tp.n)
                        for kt in range(8):
                            ld.dma(s.dma_start(
                                out=ysb[:, kt * TCH:(kt + 1) * TCH],
                                in_=y1_dr.ap()[(half * 8 + kt) * 128:(half * 8 + kt + 1) * 128,
                                               tc * TCH:(tc + 1) * TCH]))

                    for ot in range(8):
                        oid = (tc * 4 + half) * 8 + ot

                        @block.sync
                        def _(s, tc=tc, ot=ot, half=half, oid=oid,
                              wfree=(f2_tp[oid - 2] if oid >= 2 else 0)):
                            s.wait_ge(ld.h, ld.n)
                            if wfree:
                                s.wait_ge(tp.h, wfree)
                            else:
                                s.wait_ge(tp.h, tp.n)
                            for kt in range(8):
                                ld.dma(s.dma_start(
                                    out=wstr[:, 2048 + (oid % 2) * 1024 + kt * 128:
                                             2048 + (oid % 2) * 1024 + (kt + 1) * 128],
                                    in_=w2b_dr.ap()[(half * 8 + kt) * 128:(half * 8 + kt + 1) * 128,
                                                    ot * 128:(ot + 1) * 128]))
                        f2_ld.append(ld.n)

                        @block.tensor
                        def _(t, tc=tc, ot=ot, half=half, oid=oid, ldn=f2_ld[-1]):
                            t.wait_ge(ld.h, ldn)
                            if tc > 0 and half == 0 and ot == 0:
                                t.wait_ge(cm.h, cm.n)
                            last = None
                            for kt in range(8):
                                last = t.matmul(ps[:, ot * 512:ot * 512 + TCH],
                                                wstr[:, 2048 + (oid % 2) * 1024 + kt * 128:
                                                     2048 + (oid % 2) * 1024 + (kt + 1) * 128],
                                                ysb[:, kt * TCH:(kt + 1) * TCH],
                                                start=(half == 0 and kt == 0),
                                                stop=(half == 3 and kt == 7))
                            tp.inc(last)
                        f2_tp.append(tp.n)

                for ot in range(8):
                    @block.vector
                    def _(v, tc=tc, ot=ot):
                        v.wait_ge(tp.h, tp.n)
                        v.tensor_scalar(out=vsc[:], in0=ps[:, ot * 512:ot * 512 + TCH],
                                        scalar1=prm[:, PRM_F2B + ot:PRM_F2B + ot + 1],
                                        scalar2=None, op0=ALU.add)
                        cm.inc(v.scalar_tensor_tensor(
                            out=vsc[:], in0=vsc[:], scalar=0.5,
                            in1=xfm[:, ot * TPC + tc * TCH:ot * TPC + tc * TCH + TCH],
                            op0=ALU.mult, op1=ALU.add))

                    @block.scalar
                    def _(s, tc=tc, ot=ot):
                        s.wait_ge(cm.h, cm.n)
                        av.inc(s.activation(out=tsc[:], in_=vsc[:], func=AF.Tanh, scale=0.5))

                    @block.vector
                    def _(v, tc=tc, ot=ot):
                        v.wait_ge(av.h, av.n)
                        v.tensor_scalar(out=tsc[:], in0=tsc[:], scalar1=0.5, scalar2=0.5,
                                        op0=ALU.mult, op1=ALU.add)
                        cm.inc(v.tensor_mul(out=gsc[:], in0=vsc[:], in1=tsc[:]))

                    @block.tensor
                    def _(t, tc=tc, ot=ot):
                        t.wait_ge(cm.h, cm.n)
                        last = None
                        for j in range(4):
                            last = t.transpose(ps[:, ot * 512 + j * 128:ot * 512 + (j + 1) * 128],
                                               gsc[:, j * 128:(j + 1) * 128], ident[:])
                        tp.inc(last)

                    @block.vector
                    def _(v, tc=tc, ot=ot):
                        v.wait_ge(tp.h, tp.n)
                        if ot == 0:
                            v.wait_ge(st.h, st.n)
                        last = None
                        for j in range(4):
                            last = v.tensor_copy(
                                out=gtk[:, j * DM + ot * 128:j * DM + (ot + 1) * 128],
                                in_=ps[:, ot * 512 + j * 128:ot * 512 + (j + 1) * 128])
                        cm.inc(last)

                @block.gpsimd
                def _(g, tc=tc):
                    g.wait_ge(cm.h, cm.n)
                    last = None
                    for j in range(4):
                        last = st.dma(g.dma_start(
                            out=g0_dr.ap()[tc * TCH + j * 128:tc * TCH + (j + 1) * 128, :],
                            in_=gtk[:, j * DM:(j + 1) * DM]))
                        if "g0" in dout:
                            last = st.dma(g.dma_start(
                                out=dout["g0"][tc * TCH + j * 128:tc * TCH + (j + 1) * 128, :],
                                in_=gtk[:, j * DM:(j + 1) * DM]))

            @block.tensor
            def _(t):
                t.wait_ge(cm.h, cm.n)
                tp.inc(t.nop())

            @block.vector
            def _(v):
                v.wait_ge(st.h, st.n)
                v.wait_ge(tp.h, tp.n)
                v.wait_ge(av.h, av.n)
                cm.inc(v.tensor_copy(out=sqs[:, 0:1], in_=ones_c[:]))
            esA2.close()
            if stop_after == "ffn":
                @block.gpsimd
                def _(g):
                    g.wait_ge(cc1.h, 1)
                    g.wait_ge(st.h, st.n)
                return _finish(nc, es, esA)
        barA = cm.n

        esB = contextlib.ExitStack()
        with esB:
            EB = esB.enter_context
            ckv2x = EB(nc.sbuf_tensor("ckv2x", [128, 8192], BF16))
            ckv65 = EB(nc.sbuf_tensor("ckv65", [128, 64 * 128], BF16))
            ckvs = EB(nc.sbuf_tensor("ckvs", [128, 128], F32))
            wpT = EB(nc.sbuf_tensor("wpT", [128, 64 * 128], BF16))
            wvu = EB(nc.sbuf_tensor("wvu", [128, 8 * 64], BF16))
            Mb = EB(nc.sbuf_tensor("Mb", [64, 16 * 1024], BF16))
            beffb = EB(nc.sbuf_tensor("beffb", [128, DM], F32))
            beffc = EB(nc.sbuf_tensor("beffc", [128, 8], F32))
            bvub = EB(nc.sbuf_tensor("bvub", [128, 8], BF16))
            onesP = EB(nc.sbuf_tensor("onesP", [65, 64], F32))
            beffr = EB(nc.sbuf_tensor("beffr", [1, DM], F32))
            ptb = EB(nc.sbuf_tensor("ptb", [128, 8 * TCH], BF16))
            osb = EB(nc.sbuf_tensor("osb", [128, 4 * TCH], F32))
            onb = EB(nc.sbuf_tensor("onb", [64, 4 * TCH], BF16))
            yac = EB(nc.sbuf_tensor("yac", [128, 4 * DM], F32))
            xg = EB(nc.sbuf_tensor("xg", [128, 4 * DM], F32))
            gld = EB(nc.sbuf_tensor("gld", [128, 4 * DM], F32))
            wld = EB(nc.sbuf_tensor("wld", [128, DM], F32))

            @block.gpsimd
            def _(g):
                g.wait_ge(cm.h, barA)
                g.wait_ge(cc1.h, 1)
                g.memset(onesP[:], 1.0)
                g.memset(ckv65[:], 0.0)
                g.memset(ckvs[:], 0.0)
                last = None
                for j in range(64):
                    last = g.memset(ckv65[:, j * 128 + 64:j * 128 + 65], 1.0)
                gm.inc(last)

            for j in range(64):
                @block.sync
                def _(s, j=j):
                    if j == 0:
                        s.wait_ge(gm.h, gm.n)
                    else:
                        s.wait_ge(cm.h, cm.n)
                    ld.dma(s.dma_start(out=ckvs[:, 0:64], in_=ckv_fu.ap()[j * 128:(j + 1) * 128, :]))

                @block.tensor
                def _(t, j=j):
                    t.wait_ge(ld.h, ld.n)
                    tp.inc(t.transpose(ps[:, 0:128], ckvs[:], ident[:]))

                @block.vector
                def _(v, j=j):
                    v.wait_ge(tp.h, tp.n)
                    v.tensor_copy(out=ckv65[:, j * 128:j * 128 + 64], in_=ckvs[:, 0:64])
                    cm.inc(v.tensor_copy(out=ckv2x[0:64, j * 128:(j + 1) * 128],
                                         in_=ps[0:64, 0:128]))

            @block.vector
            def _(v):
                v.wait_ge(cm.h, cm.n)
                last = None
                for j in range(64):
                    last = v.tensor_copy(out=ckv2x[64:128, j * 128:(j + 1) * 128],
                                         in_=ckv2x[0:64, j * 128:(j + 1) * 128])
                cm.inc(last)

            if stop_after == "b0":
                @block.gpsimd
                def _(g):
                    g.wait_ge(cm.h, cm.n)
                    g.wait_ge(st.h, st.n)
                return _finish(nc, es, esB)
            for i in range(8):
                @block.sync
                def _(s, i=i):
                    s.wait_ge(cm.h, cm.n)
                    ld.dma(s.dma_start(out=wld[:, 0:64], in_=wvu_d[i * 128:(i + 1) * 128, :]))

                @block.vector
                def _(v, i=i):
                    v.wait_ge(ld.h, ld.n)
                    cm.inc(v.tensor_copy(out=wvu[:, i * 64:(i + 1) * 64], in_=wld[:, 0:64]))

            emit_wpT_src = wpr_d
            for i in range(8):
                @block.sync
                def _(s, i=i):
                    s.wait_ge(cm.h, cm.n)
                    ld.dma(s.dma_start(out=wld[:], in_=emit_wpT_src[i * 128:(i + 1) * 128, :]))

                @block.tensor
                def _(t, i=i):
                    t.wait_ge(ld.h, ld.n)
                    for ct in range(NCT):
                        tp.inc(t.transpose(ps[:, (ct % 8) * 512:(ct % 8) * 512 + 128],
                                           wld[:, ct * 128:(ct + 1) * 128], ident[:]))

                @block.vector
                def _(v, i=i):
                    v.wait_ge(tp.h, tp.n)
                    last = None
                    for ct in range(NCT):
                        last = v.tensor_copy(out=wpT[:, (i * 8 + ct) * 128:(i * 8 + ct + 1) * 128],
                                             in_=ps[:, (ct % 8) * 512:(ct % 8) * 512 + 128])
                    cm.inc(last)

            if stop_after == "b1":
                @block.gpsimd
                def _(g):
                    g.wait_ge(cm.h, cm.n)
                    g.wait_ge(st.h, st.n)
                return _finish(nc, es, esB)
            for h in range(16):
                it, bp = h // 2, (h % 2) * 64
                for grp in range(2):
                    @block.tensor
                    def _(t, h=h, it=it, bp=bp, grp=grp):
                        t.wait_ge(cm.h, cm.n)
                        last = None
                        for j in range(4):
                            cb = grp * 4 + j
                            last = t.matmul(ps[0:64, j * 512:j * 512 + 128],
                                            wvu[bp:bp + 64, it * 64:(it + 1) * 64],
                                            wpT[bp:bp + 64, (cb * 8 + it) * 128:(cb * 8 + it + 1) * 128],
                                            start=True, stop=True)
                        tp.inc(last)

                    @block.vector
                    def _(v, h=h, it=it, bp=bp, grp=grp):
                        v.wait_ge(tp.h, tp.n)
                        last = None
                        for j in range(4):
                            cb = grp * 4 + j
                            last = v.tensor_copy(
                                out=Mb[0:64,
                                       h * 1024 + cb * 128:h * 1024 + (cb + 1) * 128],
                                in_=ps[0:64, j * 512:j * 512 + 128])
                        cm.inc(last)
            M_done = cm.n

            if stop_after == "b2":
                @block.gpsimd
                def _(g):
                    g.wait_ge(cm.h, cm.n)
                    g.wait_ge(st.h, st.n)
                return _finish(nc, es, esB)
            @block.vector
            def _(v):
                v.wait_ge(cm.h, M_done)
                cm.inc(v.tensor_copy(out=bvub[:], in_=prm[:, PRM_BVU:PRM_BVU + 8]))
            bvub_done = cm.n

            @block.sync
            def _(s):
                s.wait_ge(cm.h, bvub_done)
                for ct in range(8):
                    ld.dma(s.dma_start(out=beffr[0:1, ct * 128:(ct + 1) * 128],
                                       in_=bpr_d[ct:ct + 1, :]))
            bpr_ld = ld.n

            for ct in range(8):
                @block.tensor
                def _(t, ct=ct):
                    t.wait_ge(cm.h, cm.n)
                    last = None
                    for it in range(8):
                        last = t.matmul(ps[0:1, it * 512:it * 512 + 128],
                                        bvub[:, it:it + 1],
                                        wpT[:, (ct * 8 + it) * 128:(ct * 8 + it + 1) * 128],
                                        start=True, stop=True)
                    tp.inc(last)

                @block.vector
                def _(v, ct=ct):
                    v.wait_ge(tp.h, tp.n)
                    if ct == 0:
                        v.wait_ge(ld.h, bpr_ld)
                    last = None
                    for it in range(8):
                        last = v.tensor_add(out=beffr[0:1, ct * 128:(ct + 1) * 128],
                                            in0=beffr[0:1, ct * 128:(ct + 1) * 128],
                                            in1=ps[0:1, it * 512:it * 512 + 128])
                    cm.inc(last)
            beffr_done = cm.n

            @block.tensor
            def _(t):
                t.wait_ge(cm.h, beffr_done)
                t.matmul(ps[:, 2048:2560], ones_r[:], beffr[0:1, 0:512], start=True, stop=True)
                tp.inc(t.matmul(ps[:, 2560:3072], ones_r[:], beffr[0:1, 512:1024],
                                start=True, stop=True))

            @block.vector
            def _(v):
                v.wait_ge(tp.h, tp.n)
                v.tensor_copy(out=beffb[:, 0:512], in_=ps[:, 2048:2560])
                cm.inc(v.tensor_copy(out=beffb[:, 512:1024], in_=ps[:, 2560:3072]))
            prep_done = cm.n

            if stop_after == "aprep":
                if "beff" in dout:
                    @block.gpsimd
                    def _(g):
                        g.wait_ge(cm.h, prep_done)
                        st.dma(g.dma_start(out=dout["beff"], in_=beffb[:]))
                if "mb" in dout:
                    for i in range(16):
                        @block.vector
                        def _(v, i=i):
                            v.wait_ge(st.h, st.n)
                            cm.inc(v.tensor_copy(out=sqs[:], in_=Mb[:, i * 512:(i + 1) * 512]))

                        @block.gpsimd
                        def _(g, i=i):
                            g.wait_ge(cm.h, cm.n)
                            st.dma(g.dma_start(out=dout["mb"][:, i * 512:(i + 1) * 512], in_=sqs[:]))
                if "ckv2x" in dout:
                    for i in range(16):
                        @block.vector
                        def _(v, i=i):
                            v.wait_ge(st.h, st.n)
                            cm.inc(v.tensor_copy(out=sqs[:], in_=ckv2x[:, i * 512:(i + 1) * 512]))

                        @block.gpsimd
                        def _(g, i=i):
                            g.wait_ge(cm.h, cm.n)
                            st.dma(g.dma_start(out=dout["ckv2x"][:, i * 512:(i + 1) * 512], in_=sqs[:]))
                @block.gpsimd
                def _(g):
                    g.wait_ge(st.h, st.n)
                return _finish(nc, es, esB)

            NQC = 1 if stop_after == "attn1" else NCH
            NHG = 1 if stop_after == "attn1" else 4
            onb_cm, yac_cm = [], []
            for qc in range(NQC):
                q0 = qc * TCH
                for hg in range(NHG * 2):
                    heads = [2 * hg + i for i in range(2)]
                    sc0 = sc_s.n
                    ex0 = ex_s.n
                    pv0 = pv_s.n

                    @block.tensor
                    def _(t, q0=q0, heads=heads, sc0=sc0, ex0=ex0, pv0=pv0,
                          gate=(onb_cm[-1] if onb_cm else 0),
                          gate2=(yac_cm[-1] if yac_cm else 0)):
                        if qc == 0 and hg == 0:
                            t.wait_ge(cm.h, prep_done)
                        elif gate:
                            t.wait_ge(cm.h, gate)
                        else:
                            t.wait_ge(cm.h, cm.n)

                        def scores(kt):
                            if kt >= 2:
                                t.wait_ge(ex_s.h, ex0 + (kt - 1))
                            for i, h in enumerate(heads):
                                ft = h // 2
                                bp = (h % 2) * 64
                                ins = t.matmul(
                                    ps[:, (kt % 2) * 1024 + i * 512:(kt % 2) * 1024 + i * 512 + 512],
                                    ckv2x[bp:bp + 64, kt * 128:(kt + 1) * 128],
                                    qfm[bp:bp + 64, ft * TPC + q0:ft * TPC + q0 + TCH],
                                    start=True, stop=True,
                                    tile_position=(bp, 0))
                                if i == 1:
                                    sc_s.inc(ins)

                        def pv(kt):
                            if kt == 0 and gate2:
                                t.wait_ge(cm.h, gate2)
                            t.wait_ge(ex_s.h, ex0 + (kt + 1))
                            for i in range(2):
                                ins = t.matmul(ps[:, 2048 + i * 512:2048 + i * 512 + 512],
                                               ckv65[:, kt * 128:(kt + 1) * 128],
                                               ptb[:, ((kt % 2) * 2 + i) * TCH:
                                                   ((kt % 2) * 2 + i + 1) * TCH],
                                               start=(kt == 0), stop=(kt == 63))
                                if i == 1:
                                    pv_s.inc(ins)

                        scores(0)
                        for kt in range(1, 64):
                            scores(kt)
                            pv(kt - 1)
                        pv(63)

                    @block.scalar
                    def _(s, sc0=sc0, pv0=pv0):
                        for kt in range(64):
                            s.wait_ge(sc_s.h, sc0 + kt + 1)
                            if kt >= 2:
                                s.wait_ge(pv_s.h, pv0 + kt - 1)
                            ins = s.activation(
                                out=ptb[:, (kt % 2) * 2 * TCH:((kt % 2) * 2 + 2) * TCH],
                                in_=ps[:, (kt % 2) * 1024:(kt % 2) * 1024 + 1024],
                                func=AF.Exp, scale=0.125, bias=negK[:])
                            ex_s.inc(ins)

                    @block.vector
                    def _(v, hg=hg, pv0=pv0):
                        v.wait_ge(pv_s.h, pv0 + 64)
                        for i in range(2):
                            v.tensor_copy(out=osb[:, i * TCH:i * TCH + TCH],
                                          in_=ps[:, 2048 + i * 512:2048 + i * 512 + TCH])
                        last = None
                        for i in range(2):
                            last = v.reciprocal(out=osb[64:65, i * TCH:i * TCH + TCH],
                                                in_=osb[64:65, i * TCH:i * TCH + TCH])
                        cm.inc(last)

                    @block.tensor
                    def _(t, hg=hg):
                        t.wait_ge(cm.h, cm.n)
                        last = None
                        for i in range(2):
                            last = t.matmul(ps[0:64, i * 512:i * 512 + TCH],
                                            onesP[64:65, 0:64],
                                            osb[64:65, i * TCH:i * TCH + TCH],
                                            start=True, stop=True)
                        tp.inc(last)

                    @block.vector
                    def _(v, hg=hg):
                        v.wait_ge(tp.h, tp.n)
                        last = None
                        for i in range(2):
                            last = v.tensor_mul(
                                out=onb[0:64, i * TCH:i * TCH + TCH],
                                in0=osb[0:64, i * TCH:i * TCH + TCH],
                                in1=ps[0:64, i * 512:i * 512 + TCH])
                        cm.inc(last)
                    onb_cm.append(cm.n)

                    prj_sc0 = sc_s.n

                    @block.tensor
                    def _(t, hg=hg, heads=heads, prj_sc0=prj_sc0):
                        t.wait_ge(cm.h, cm.n)
                        for qt in range(4):
                            for ch in range(2):
                                p = qt * 2 + ch
                                if p >= 4:
                                    t.wait_ge(sc_s.h, prj_sc0 + p - 3)
                                last = None
                                for i, h in enumerate(heads):
                                    last = t.matmul(
                                        ps[:, 2048 + (qt % 2) * 1024 + ch * 512:
                                           2048 + (qt % 2) * 1024 + ch * 512 + 512],
                                        onb[0:64,
                                            i * TCH + qt * 128:i * TCH + qt * 128 + 128],
                                        Mb[0:64,
                                           h * 1024 + ch * 512:h * 1024 + (ch + 1) * 512],
                                        start=(i == 0), stop=(i == 1))
                                tp.inc(last)

                    @block.vector
                    def _(v, hg=hg):
                        base = tp.n - 8
                        k = 0
                        last = None
                        for qt in range(4):
                            for ch in range(2):
                                k += 1
                                v.wait_ge(tp.h, base + k)
                                dst = yac[:, qt * DM + ch * 512:qt * DM + (ch + 1) * 512]
                                src = ps[:, 2048 + (qt % 2) * 1024 + ch * 512:
                                         2048 + (qt % 2) * 1024 + ch * 512 + 512]
                                if hg == 0:
                                    last = v.tensor_copy(out=dst, in_=src)
                                else:
                                    last = v.tensor_add(out=dst, in0=dst, in1=src)
                                if k <= 4:
                                    sc_s.inc(last)
                                elif k == 8:
                                    cm.inc(last)
                    yac_cm.append(cm.n)

                if stop_after == "attn1":
                    if "yac" in dout:
                        @block.gpsimd
                        def _(g):
                            g.wait_ge(cm.h, cm.n)
                            st.dma(g.dma_start(out=dout["yac"], in_=yac[:]))
                    @block.gpsimd
                    def _(g):
                        g.wait_ge(st.h, st.n)
                    return _finish(nc, es, esB)
                @block.sync
                def _(s, qc=qc):
                    s.wait_ge(cm.h, cm.n)
                    s.wait_ge(st.h, st.n)
                    last = None
                    for qt in range(4):
                        ld.dma(s.dma_start(out=xg[:, qt * DM:(qt + 1) * DM],
                                           in_=x_d[qc * TCH + qt * 128:qc * TCH + (qt + 1) * 128, :]))
                        last = ld.dma(s.dma_start(
                            out=gld[:, qt * DM:(qt + 1) * DM],
                            in_=g0_dr.ap()[qc * TCH + qt * 128:qc * TCH + (qt + 1) * 128, :]))

                @block.vector
                def _(v, qc=qc):
                    v.wait_ge(ld.h, ld.n)
                    last = None
                    for qt in range(4):
                        v.tensor_add(out=yac[:, qt * DM:(qt + 1) * DM],
                                     in0=yac[:, qt * DM:(qt + 1) * DM], in1=beffb[:])
                        v.tensor_mul(out=gld[:, qt * DM:(qt + 1) * DM],
                                     in0=gld[:, qt * DM:(qt + 1) * DM],
                                     in1=yac[:, qt * DM:(qt + 1) * DM])
                        last = v.tensor_add(out=xg[:, qt * DM:(qt + 1) * DM],
                                            in0=xg[:, qt * DM:(qt + 1) * DM],
                                            in1=gld[:, qt * DM:(qt + 1) * DM])
                    cm.inc(last)

                @block.gpsimd
                def _(g, qc=qc):
                    g.wait_ge(cm.h, cm.n)
                    last = None
                    for qt in range(4):
                        last = st.dma(g.dma_start(
                            out=x1_d[qc * TCH + qt * 128:qc * TCH + (qt + 1) * 128, :],
                            in_=xg[:, qt * DM:(qt + 1) * DM]))
                        if "s0" in dout:
                            last = st.dma(g.dma_start(
                                out=dout["s0"][qc * TCH + qt * 128:qc * TCH + (qt + 1) * 128, :],
                                in_=yac[:, qt * DM:(qt + 1) * DM]))

            @block.gpsimd
            def _(g):
                g.wait_ge(st.h, st.n)
                if fuse is not None:
                    g.nop().then_inc(fuse["bar"].h, 1)

            if fuse is not None:
                @block.sync
                def _(s):
                    s.wait_ge(ld.h, ld.n)
                    s.wait_ge(st.h, st.n)
                    s.nop().then_inc(fuse["bar"].h, 1)

                @block.vector
                def _(v):
                    v.nop().then_inc(fuse["bar"].h, 1)

                @block.scalar
                def _(sc):
                    sc.nop().then_inc(fuse["bar"].h, 1)

                @block.tensor
                def _(t):
                    t.nop().then_inc(fuse["bar"].h, 1)

    return nc



def l0_in_map(inp, core):
    x = np.ascontiguousarray(inp["L0_attn_x"].reshape(-1, DM)[core * TPC:(core + 1) * TPC])
    return dict(
        x=x,
        wq_d=inp["L0_attn_wq_d"],
        bq_d=inp["L0_attn_bq_d"].reshape(4, 128),
        wqk_d=inp["L0_attn_w_qk"],
        bqk_d=inp["L0_attn_b_qk"].reshape(8, 128),
        wkv_d=inp["L0_attn_wkv_d"],
        bkv_d=inp["L0_attn_bkv_d"].reshape(1, 64),
        wvu_d=inp["L0_attn_wv_u"],
        bvu_d=inp["L0_attn_bv_u"].reshape(8, 128),
        wpr_d=inp["L0_attn_w_proj"],
        bpr_d=inp["L0_attn_b_proj"].reshape(8, 128),
        gam_d=inp["L0_ffn_ln_gamma"].reshape(8, 128),
        bet_d=inp["L0_ffn_ln_beta"].reshape(8, 128),
        f1q_d=inp["L0_ffn_fc1_q"],
        f1s_d=inp["L0_ffn_fc1_scales"],
        f1b_d=inp["L0_ffn_fc1_bias"].reshape(32, 128),
        f2q_d=inp["L0_ffn_fc2_q"],
        f2s_d=inp["L0_ffn_fc2_scales"],
        f2b_d=inp["L0_ffn_fc2_bias"].reshape(8, 128),
    )





DI = 2048
DSL = 256
NST = 16
DTR = 128


def build_l1(dbg=(), stop_after="all", races=True, fuse=None):
    if fuse is None:
        nc = bass.Bass("TRN2", target_bir_lowering=False, detect_race_conditions=races)
        x1_d = nc.dram_tensor("x1", [TPC, DM], F32, kind="ExternalInput").ap()
    else:
        nc = fuse["nc"]
        x1_d = fuse["x1_d"]
    lpfx = "" if fuse is None else "q_"
    win_d = nc.dram_tensor("win_d", [DM, 2 * DI], F32, kind="ExternalInput").ap()
    ck_d = nc.dram_tensor("ck_d", [4, DSL], F32, kind="ExternalInput").ap()
    cb_d = nc.dram_tensor("cb_d", [2, 128], F32, kind="ExternalInput").ap()
    wxp_d = nc.dram_tensor("wxp_d", [DSL, 160], F32, kind="ExternalInput").ap()
    wdt_d = nc.dram_tensor("wdt_d", [DTR, DSL], F32, kind="ExternalInput").ap()
    bdt_d = nc.dram_tensor("bdt_d", [2, 128], F32, kind="ExternalInput").ap()
    alog_d = nc.dram_tensor("alog_d", [DSL, NST], F32, kind="ExternalInput").ap()
    dv_d = nc.dram_tensor("dv_d", [2, 128], F32, kind="ExternalInput").ap()
    wout_d = nc.dram_tensor("wout_d", [DI, DM], F32, kind="ExternalInput").ap()
    wg_d = nc.dram_tensor("wg_d", [DM, NST], F32, kind="ExternalInput").ap()
    w1s_d = nc.dram_tensor("w1s_d", [2 * DM, 4096], F32, kind="ExternalInput").ap()
    w2s_d = nc.dram_tensor("w2s_d", [2 * 4096, DM], F32, kind="ExternalInput").ap()
    ccst_d = nc.dram_tensor("ccst_d", [1, 4], F32, kind="ExternalInput").ap()

    x2_d = nc.dram_tensor("x2", [TPC, DM], F32, kind="ExternalOutput").ap()
    dout = {}
    for name, shape, dt in [
        ("xn1", [128, NCT * TPC], F32), ("gate", [TPC, 1], F32), ("gsv", [TPC, 1], F32), ("ttb", [128, 32], F32),
        ("xin", [128, 4 * 8192], F32), ("u", [128, 4 * 8192], F32),
        ("xdb", [8 * TPC, 160], F32), ("dtf", [128, 4 * 8192], F32),
        ("yz", [128, 4 * 8192], F32), ("ptab", [8 * TPC, 1], F32),
        ("gst", [8 * TPC, 1], F32), ("eo", [TPC, DM], F32),
        ("s1", [TPC, DM], F32), ("g1", [TPC, DM], F32), ("x1late", [TPC, DM], F32),
    ]:
        if name in dbg:
            dout[name] = nc.dram_tensor("dbg_" + name, shape, dt, kind="ExternalOutput").ap()

    xn1_bo = nc.dram_tensor("xn1_bo", [TPC, DM], BF16)
    xn1_fu = nc.dram_tensor("xn1_fu", [8 * TPC, DM], BF16)
    idx_bo = nc.dram_tensor("idx_bo", [TPC, 1], F32)
    idx_fu = nc.dram_tensor("idx_fu", [8 * TPC, 1], F32)
    gsv_bo = nc.dram_tensor("gsv_bo", [TPC, 1], F32)
    gsv_fu = nc.dram_tensor("gsv_fu", [8 * TPC, 1], F32)
    xa_i = nc.dram_tensor("xa_i", [DI, TPC], F32)
    xa_o = nc.dram_tensor("xa_o", [DI, TPC], F32)
    za_i = nc.dram_tensor("za_i", [DI, TPC], F32)
    za_o = nc.dram_tensor("za_o", [DI, TPC], F32)
    u_dr = nc.dram_tensor("u_dr", [4 * 128, 8192], F32)
    dtl_dr = nc.dram_tensor("dtl_dr", [2 * DTR, 8192], F32)
    bc_dr = nc.dram_tensor("bc_dr", [2 * 32, 8192], F32)
    xdb_bo = nc.dram_tensor("xdb_bo", [8 * TPC, 160], F32)
    xdb_fu = nc.dram_tensor("xdb_fu", [8 * TPC, 160], F32)
    p_dr = nc.dram_tensor("p_dr", [8 * TPC, 1], F32)
    gs_dr = nc.dram_tensor("gs_dr", [8 * TPC, 1], F32)
    yz_i = nc.dram_tensor("yz_i", [DI, TPC], F32)
    yz_o = nc.dram_tensor("yz_o", [DI, TPC], F32)
    eo_bo = nc.dram_tensor("eo_bo", [TPC, DM], BF16)
    eo_fu = nc.dram_tensor("eo_fu", [8 * TPC, DM], BF16)
    pcl_bo = nc.dram_tensor("pcl_bo", [8 * TPC, 1], F32)
    pcl_sl = nc.dram_tensor("pcl_sl", [8 * TPC, 1], F32)
    gcl_bo = nc.dram_tensor("gcl_bo", [8 * TPC, 1], F32)
    gcl_sl = nc.dram_tensor("gcl_sl", [8 * TPC, 1], F32)

    es = contextlib.ExitStack()
    E = es.enter_context
    with es:
        ident = E(nc.sbuf_tensor(lpfx + "ident", [128, 128], F32))
        ones_c = E(nc.sbuf_tensor(lpfx + "ones_c", [128, 1], F32))
        ones_r = E(nc.sbuf_tensor(lpfx + "ones_r", [1, 128], F32))
        iotaF = E(nc.sbuf_tensor(lpfx + "iotaF", [128, 128], F32))
        iotaC = E(nc.sbuf_tensor(lpfx + "iotaC", [128, 1], F32))
        io16 = E(nc.sbuf_tensor(lpfx + "io16", [128, 16], F32))
        rows = E(nc.sbuf_tensor(lpfx + "rows", [1, 2048], F32))
        sqs = E(nc.sbuf_tensor(lpfx + "sqs", [128, TCH], F32))
        prm = E(nc.sbuf_tensor(lpfx + "prm", [128, 48], F32))
        tu32 = E(nc.sbuf_tensor(lpfx + "tu32", [128, 16], U32))
        ps = E(nc.psum_tensor(lpfx + "ps", [128, 4096], F32))

        PRM_A, PRM_CK, PRM_CB, PRM_D, PRM_BDT = 0, 32, 40, 42, 44

        SE = E if fuse is None else (lambda s: fuse["E"](s))
        ld = Sem(SE(nc.semaphore("b_ld")))
        st = Sem(SE(nc.semaphore("b_st")))
        cm = Sem(SE(nc.semaphore("b_cm")))
        tp = Sem(SE(nc.semaphore("b_tp")))
        av = Sem(SE(nc.semaphore("b_av")))
        gm = Sem(SE(nc.semaphore("b_gm")))
        sf = Sem(SE(nc.semaphore("b_sf")))
        cc = [Sem(SE(nc.semaphore(f"b_cc{i}"))) for i in range(8)]

        block = E(nc.Block())

        if fuse is not None:
            bar = fuse["bar"]

            @block.sync
            def _(s):
                s.wait_ge(bar.h, 5)

            @block.gpsimd
            def _(g):
                g.wait_ge(bar.h, 5)

            @block.vector
            def _(v):
                v.wait_ge(bar.h, 5)

            @block.scalar
            def _(sc):
                sc.wait_ge(bar.h, 5)

            @block.tensor
            def _(t):
                t.wait_ge(bar.h, 5)

        @block.gpsimd
        def _(g):
            g.memset(ident[:], 0.0)
            g.affine_select(out=ident[:], in_=ident[:], compare_op=ALU.not_equal,
                            fill=1.0, base=0, pattern=[[-1, 128]], channel_multiplier=1)
            g.memset(ones_c[:], 1.0)
            g.memset(ones_r[:], 1.0)
            g.iota(iotaF[:], [[1, 128]], channel_multiplier=0,
                   allow_small_or_imprecise_dtypes=True)
            g.iota(iotaC[:], [[1, 1]], channel_multiplier=1,
                   allow_small_or_imprecise_dtypes=True)
            gm.inc(g.iota(io16[:], [[1, 16]], channel_multiplier=0,
                          allow_small_or_imprecise_dtypes=True))

        @block.sync
        def _(s):
            ld.dma(s.dma_start(out=prm[:, PRM_A:PRM_A + 16], in_=alog_d[0:128, :]))
            ld.dma(s.dma_start(out=prm[:, PRM_A + 16:PRM_A + 32], in_=alog_d[128:256, :]))
            for k in range(4):
                ld.dma(s.dma_start(out=prm[:, PRM_CK + k:PRM_CK + k + 1],
                                   in_=ck_d[k:k + 1, 0:128]))
                ld.dma(s.dma_start(out=prm[:, PRM_CK + 4 + k:PRM_CK + 4 + k + 1],
                                   in_=ck_d[k:k + 1, 128:256]))
            for src, col in [(cb_d, PRM_CB), (dv_d, PRM_D), (bdt_d, PRM_BDT)]:
                for r in range(2):
                    ld.dma(s.dma_start(out=prm[:, col + r:col + r + 1], in_=src[r:r + 1, :]))
        p0_ld = ld.n

        @block.vector
        def _(v):
            v.wait_ge(ld.h, p0_ld)
            v.wait_ge(gm.h, gm.n)
            cm.inc(v.tensor_scalar_add(out=io16[:], in0=io16[:], scalar1=65536.0))

        @block.scalar
        def _(s):
            load_act_set(nc, s, "natural_log_exp_and_others")
            s.wait_ge(ld.h, p0_ld)
            av.inc(s.activation(out=prm[:, PRM_A:PRM_A + 32], in_=prm[:, PRM_A:PRM_A + 32],
                                func=AF.Exp))

        @block.vector
        def _(v):
            v.wait_ge(av.h, av.n)
            cm.inc(v.tensor_scalar_mul(out=prm[:, PRM_A:PRM_A + 32],
                                       in0=prm[:, PRM_A:PRM_A + 32], scalar1=-1.0))
        consts_done = cm.n



        esA = contextlib.ExitStack()
        EA = esA.enter_context
        with esA:
            esAx = contextlib.ExitStack()
            EAx = esAx.enter_context
            x1fm = EAx(nc.sbuf_tensor(lpfx + "x1fm", [128, NCT * TPC], F32))
            xt = EAx(nc.sbuf_tensor(lpfx + "xt", [128, DM], F32))

            for tt in range(NTT):
                @block.sync
                def _(s, tt=tt):
                    if tt == 0:
                        s.wait_ge(cm.h, consts_done)
                    else:
                        s.wait_ge(cm.h, cm.n)
                    ld.dma(s.dma_start(out=xt[:], in_=x1_d[tt * 128:(tt + 1) * 128, :]))

                @block.tensor
                def _(t, tt=tt):
                    t.wait_ge(ld.h, ld.n)
                    for ct in range(NCT):
                        tp.inc(t.transpose(ps[:, ct * 512:ct * 512 + 128],
                                           xt[:, ct * 128:(ct + 1) * 128], ident[:]))

                @block.vector
                def _(v, tt=tt):
                    v.wait_ge(tp.h, tp.n)
                    last = None
                    for ct in range(NCT):
                        last = v.tensor_copy(
                            out=x1fm[:, ct * TPC + tt * 128:ct * TPC + (tt + 1) * 128],
                            in_=ps[:, ct * 512:ct * 512 + 128])
                    cm.inc(last)

            r_mu = rows[0:1, 0:TCH]
            r_va = rows[0:1, TCH:2 * TCH]
            r_t1 = rows[0:1, 2 * TCH:3 * TCH]
            ps_sum = ps[0:1, 1024:1024 + TCH]
            ps_sq = ps[0:1, 1536:1536 + TCH]
            ps_mu = ps[:, 2048:2048 + TCH]
            ps_rs = ps[:, 2560:2560 + TCH]

            for tc in range(NCH):
                c0 = tc * TCH
                for ct in range(NCT):
                    @block.scalar
                    def _(s, ct=ct, c0=c0):
                        s.wait_ge(cm.h, cm.n)
                        if ct > 0:
                            s.wait_ge(tp.h, tp.n)
                        av.inc(s.activation(out=sqs[:],
                                            in_=x1fm[:, ct * TPC + c0:ct * TPC + c0 + TCH],
                                            func=AF.Square))

                    @block.tensor
                    def _(t, ct=ct, c0=c0):
                        t.wait_ge(av.h, av.n)
                        t.matmul(ps_sum, ones_c[:],
                                 x1fm[:, ct * TPC + c0:ct * TPC + c0 + TCH],
                                 start=(ct == 0), stop=(ct == NCT - 1))
                        tp.inc(t.matmul(ps_sq, ones_c[:], sqs[:],
                                        start=(ct == 0), stop=(ct == NCT - 1)))

                @block.vector
                def _(v, c0=c0):
                    v.wait_ge(tp.h, tp.n)
                    v.tensor_scalar(out=r_mu, in0=ps_sum, scalar1=1.0 / DM,
                                    scalar2=None, op0=ALU.mult)
                    v.tensor_scalar(out=r_va, in0=ps_sq, scalar1=1.0 / DM,
                                    scalar2=None, op0=ALU.mult)
                    v.tensor_mul(out=r_t1, in0=r_mu, in1=r_mu)
                    v.tensor_sub(out=r_va, in0=r_va, in1=r_t1)
                    cm.inc(v.tensor_scalar_add(out=r_va, in0=r_va, scalar1=1e-5))

                @block.scalar
                def _(s):
                    s.wait_ge(cm.h, cm.n)
                    s.activation(out=r_t1, in_=r_va, func=AF.Ln)
                    av.inc(s.activation(out=r_va, in_=r_t1, func=AF.Exp, scale=-0.5))

                @block.tensor
                def _(t):
                    t.wait_ge(av.h, av.n)
                    t.matmul(ps_mu, ones_r[:], r_mu, start=True, stop=True)
                    tp.inc(t.matmul(ps_rs, ones_r[:], r_va, start=True, stop=True))

                @block.vector
                def _(v, c0=c0):
                    v.wait_ge(tp.h, tp.n)
                    last = None
                    for ct in range(NCT):
                        sl = x1fm[:, ct * TPC + c0:ct * TPC + c0 + TCH]
                        v.tensor_sub(out=sl, in0=sl, in1=ps_mu)
                        last = v.tensor_mul(out=sl, in0=sl, in1=ps_rs)
                    cm.inc(last)
            ln_done = cm.n

            if "xn1" in dout:
                @block.gpsimd
                def _(g):
                    g.wait_ge(cm.h, ln_done)
                    st.dma(g.dma_start(out=dout["xn1"], in_=x1fm[:]))

            esP2 = contextlib.ExitStack()
            E2 = esP2.enter_context
            xtkb = E2(nc.sbuf_tensor(lpfx + "xtkb", [128, DM], BF16))
            wgl = E2(nc.sbuf_tensor(lpfx + "wgl", [128, 8 * 128], F32))
            lfm = E2(nc.sbuf_tensor(lpfx + "lfm", [32, TPC], F32))
            ltk = E2(nc.sbuf_tensor(lpfx + "ltk", [128, 48], F32))
            gcol = E2(nc.sbuf_tensor(lpfx + "gcol", [128, 8], F32))
            xnbw = E2(nc.sbuf_tensor(lpfx + "xnbw", [128, NCT * TPC], BF16))

            @block.vector
            def _(v):
                v.wait_ge(cm.h, ln_done)
                last = None
                for ct in range(NCT):
                    last = v.tensor_copy(out=xnbw[:, ct * TPC:(ct + 1) * TPC],
                                         in_=x1fm[:, ct * TPC:(ct + 1) * TPC])
                cm.inc(last)

            for tt in range(NTT):
                @block.tensor
                def _(t, tt=tt):
                    t.wait_ge(cm.h, cm.n)
                    for ct in range(NCT):
                        tp.inc(t.transpose(ps[:, ct * 512:ct * 512 + 128],
                                           x1fm[:, ct * TPC + tt * 128:ct * TPC + (tt + 1) * 128],
                                           ident[:]))

                @block.vector
                def _(v, tt=tt):
                    v.wait_ge(tp.h, tp.n)
                    v.wait_ge(st.h, st.n)
                    last = None
                    for ct in range(NCT):
                        last = v.tensor_copy(out=xtkb[:, ct * 128:(ct + 1) * 128],
                                             in_=ps[:, ct * 512:ct * 512 + 128])
                    cm.inc(last)

                @block.gpsimd
                def _(g, tt=tt):
                    g.wait_ge(cm.h, cm.n)
                    st.dma(g.dma_start(out=xn1_bo.ap()[tt * 128:(tt + 1) * 128, :],
                                       in_=xtkb[:]))

            @block.gpsimd
            def _(g):
                g.wait_ge(st.h, st.n)
                cc[0].inc(g.collective_compute(
                    "AllGather", ALU.bypass, replica_groups=GROUPS8,
                    ins=[xn1_bo.ap().opt()], outs=[xn1_fu.ap().opt()]))

            @block.gpsimd
            def _(g):
                gm.inc(g.memset(wgl[:], 0.0))

            for kt in range(8):
                @block.sync
                def _(s, kt=kt):
                    if kt == 0:
                        s.wait_ge(gm.h, gm.n)
                    else:
                        s.wait_ge(cm.h, cm.n)
                    ld.dma(s.dma_start(out=xt[:, 0:16], in_=wg_d[kt * 128:(kt + 1) * 128, :]))

                @block.vector
                def _(v, kt=kt):
                    v.wait_ge(ld.h, ld.n)
                    cm.inc(v.tensor_copy(out=wgl[:, kt * 128:kt * 128 + 16], in_=xt[:, 0:16]))
            wgl_done = cm.n

            for tc in range(NCH):
                @block.tensor
                def _(t, tc=tc):
                    t.wait_ge(cm.h, cm.n)
                    last = None
                    for kt in range(8):
                        last = t.matmul(ps[:, 0:TCH],
                                        wgl[:, kt * 128:(kt + 1) * 128],
                                        x1fm[:, kt * TPC + tc * TCH:kt * TPC + tc * TCH + TCH],
                                        start=(kt == 0), stop=(kt == 7))
                    tp.inc(last)

                @block.vector
                def _(v, tc=tc):
                    v.wait_ge(tp.h, tp.n)
                    cm.inc(v.tensor_copy(out=lfm[0:32, tc * TCH:(tc + 1) * TCH],
                                         in_=ps[0:32, 0:TCH]))
            lfm_done = cm.n

            for tt in range(NTT):
                @block.tensor
                def _(t, tt=tt):
                    t.wait_ge(cm.h, cm.n)
                    tp.inc(t.transpose(ps[:, 3072:3072 + 32],
                                       lfm[0:32, tt * 128:(tt + 1) * 128],
                                       ident[0:32, 0:32]))

                @block.vector
                def _(v, tt=tt):
                    v.wait_ge(tp.h, tp.n)
                    v.wait_ge(st.h, st.n)
                    v.tensor_copy(out=ltk[:, 0:32], in_=ps[:, 3072:3072 + 32])
                    v.tensor_reduce(out=gcol[:, 0:1], in_=ltk[:, 0:16], axis=AX.X, op=ALU.max)
                    sf.inc(v.tensor_scalar_mul(out=gcol[:, 1:2], in0=gcol[:, 0:1], scalar1=-1.0))

                @block.scalar
                def _(s, tt=tt):
                    s.wait_ge(sf.h, sf.n)
                    av.inc(s.activation(out=ltk[:, 16:32], in_=ltk[:, 0:16],
                                        func=AF.Exp, bias=gcol[:, 1:2]))

                @block.vector
                def _(v, tt=tt):
                    v.wait_ge(av.h, av.n)
                    sf.inc(v.tensor_reduce(out=gcol[:, 2:3], in_=ltk[:, 16:32],
                                           axis=AX.X, op=ALU.sum if hasattr(ALU, "sum") else ALU.add))
                    v.wait_ge(sf.h, sf.n)
                    sf.inc(v.reciprocal(out=gcol[:, 3:4], in_=gcol[:, 2:3]))
                    v.wait_ge(sf.h, sf.n)
                    sf.inc(v.tensor_scalar(out=ltk[:, 32:48], in0=ltk[:, 16:32],
                                           scalar1=gcol[:, 3:4], scalar2=None, op0=ALU.mult))
                    v.wait_ge(sf.h, sf.n)
                    sf.inc(v.tensor_reduce(out=gcol[:, 4:5], in_=ltk[:, 32:48],
                                           axis=AX.X, op=ALU.max))
                    v.wait_ge(sf.h, sf.n)
                    sf.inc(v.tensor_scalar(out=ltk[:, 16:32], in0=ltk[:, 32:48],
                                           scalar1=gcol[:, 4:5], scalar2=None, op0=ALU.is_ge))
                    v.wait_ge(sf.h, sf.n)
                    sf.inc(v.scalar_tensor_tensor(out=ltk[:, 0:16], in0=ltk[:, 16:32],
                                                  scalar=-65536.0, in1=io16[:],
                                                  op0=ALU.mult, op1=ALU.add))
                    v.wait_ge(sf.h, sf.n)
                    cm.inc(v.tensor_reduce(out=gcol[:, 5:6], in_=ltk[:, 0:16],
                                           axis=AX.X, op=ALU.min))

                @block.gpsimd
                def _(g, tt=tt):
                    g.wait_ge(cm.h, cm.n)
                    st.dma(g.dma_start(out=idx_bo.ap()[tt * 128:(tt + 1) * 128, :],
                                       in_=gcol[:, 5:6]))
                    st.dma(g.dma_start(out=gsv_bo.ap()[tt * 128:(tt + 1) * 128, :],
                                       in_=gcol[:, 4:5]))

            @block.gpsimd
            def _(g):
                g.wait_ge(st.h, st.n)
                cc[1].inc(g.collective_compute(
                    "AllGather", ALU.bypass, replica_groups=GROUPS8,
                    ins=[idx_bo.ap().opt()], outs=[idx_fu.ap().opt()]))
                cc[1].inc(g.collective_compute(
                    "AllGather", ALU.bypass, replica_groups=GROUPS8,
                    ins=[gsv_bo.ap().opt()], outs=[gsv_fu.ap().opt()]))

            if "gate" in dout:
                @block.gpsimd
                def _(g):
                    g.wait_ge(cc[1].h, 1)
                    st.dma(g.dma_start(out=dout["gate"], in_=idx_bo.ap()))
                    if "gsv" in dout:
                        st.dma(g.dma_start(out=dout["gsv"], in_=gsv_bo.ap()))
            esP2.close()
            if stop_after == "gate":
                @block.gpsimd
                def _(g):
                    g.wait_ge(cc[0].h, 1)
                    g.wait_ge(cc[1].h, 2)
                    g.wait_ge(st.h, st.n)
                return _finish(nc, es, esAx, esA)

            esP3 = contextlib.ExitStack()
            E3 = esP3.enter_context
            xnbw3 = E3(nc.sbuf_tensor(lpfx + "xnbw3", [128, NCT * TPC], BF16))
            wstr = E3(nc.sbuf_tensor(lpfx + "wstr", [128, 2 * 1024], BF16))
            wld3 = E3(nc.sbuf_tensor(lpfx + "wld3", [128, 2 * 1024], F32))
            xzc = E3(nc.sbuf_tensor(lpfx + "xzc", [128, 2 * TCH], F32))

            @block.vector
            def _(v):
                v.wait_ge(cm.h, cm.n)
                last = None
                for ct in range(NCT):
                    last = v.tensor_copy(out=xnbw3[:, ct * TPC:(ct + 1) * TPC],
                                         in_=x1fm[:, ct * TPC:(ct + 1) * TPC])
                cm.inc(last)
            xnb3_done = cm.n

            p3_ld, p3_cast, p3_mm, p3_ev, p3_st = [], [], [], [], []
            for ft in range(32):
                @block.sync
                def _(s, ft=ft,
                      wfree=(p3_cast[ft - 2] if ft >= 2 else 0)):
                    if ft == 0:
                        s.wait_ge(cm.h, cm.n)
                    elif wfree:
                        s.wait_ge(cm.h, wfree)
                    for kt in range(8):
                        ld.dma(s.dma_start(
                            out=wld3[:, (ft % 2) * 1024 + kt * 128:(ft % 2) * 1024 + (kt + 1) * 128],
                            in_=win_d[kt * 128:(kt + 1) * 128, ft * 128:(ft + 1) * 128]))
                p3_ld.append(ld.n)

                @block.vector
                def _(v, ft=ft, ldn=p3_ld[-1],
                      sfree=(p3_mm[(ft - 2) * NCH + NCH - 1] if ft >= 2 else 0)):
                    v.wait_ge(ld.h, ldn)
                    if sfree:
                        v.wait_ge(tp.h, sfree)
                    cm.inc(v.tensor_copy(out=wstr[:, (ft % 2) * 1024:(ft % 2 + 1) * 1024],
                                         in_=wld3[:, (ft % 2) * 1024:(ft % 2 + 1) * 1024]))
                p3_cast.append(cm.n)

                for tc in range(NCH):
                    it3 = ft * NCH + tc

                    @block.tensor
                    def _(t, ft=ft, tc=tc, cmn=p3_cast[-1],
                          bfree=(p3_ev[it3 - 2] if it3 >= 2 else 0)):
                        t.wait_ge(cm.h, cmn)
                        if bfree:
                            t.wait_ge(cm.h, bfree)
                        last = None
                        for kt in range(8):
                            last = t.matmul(ps[:, (it3 % 2) * 512:(it3 % 2) * 512 + TCH],
                                            wstr[:, (ft % 2) * 1024 + kt * 128:
                                                 (ft % 2) * 1024 + (kt + 1) * 128],
                                            xnbw3[:, kt * TPC + tc * TCH:kt * TPC + tc * TCH + TCH],
                                            start=(kt == 0), stop=(kt == 7))
                        tp.inc(last)
                    p3_mm.append(tp.n)

                    @block.vector
                    def _(v, ft=ft, tc=tc, it3=it3, tpn=p3_mm[-1],
                          xfree=(p3_st[it3 - 2] if it3 >= 2 else 0)):
                        v.wait_ge(tp.h, tpn)
                        if xfree:
                            v.wait_ge(st.h, xfree)
                        else:
                            v.wait_ge(st.h, st.n)
                        cm.inc(v.tensor_copy(out=xzc[:, (it3 % 2) * TCH:(it3 % 2 + 1) * TCH],
                                             in_=ps[:, (it3 % 2) * 512:(it3 % 2) * 512 + TCH]))
                    p3_ev.append(cm.n)

                    @block.gpsimd
                    def _(g, ft=ft, tc=tc, it3=it3, cmn=p3_ev[-1]):
                        g.wait_ge(cm.h, cmn)
                        dst = xa_i if ft < 16 else za_i
                        fr = ft if ft < 16 else ft - 16
                        st.dma(g.dma_start(
                            out=dst.ap()[fr * 128:(fr + 1) * 128, tc * TCH:(tc + 1) * TCH],
                            in_=xzc[:, (it3 % 2) * TCH:(it3 % 2 + 1) * TCH]))
                    p3_st.append(st.n)

                if ft == 15:
                    @block.gpsimd
                    def _(g):
                        g.wait_ge(st.h, st.n)
                        with allow_ici_all_to_all():
                            cc[2].inc(g.collective_compute(
                                "AllToAll", ALU.bypass, replica_groups=GROUPS8,
                                ins=[xa_i.ap().opt()], outs=[xa_o.ap().opt()]))

            @block.gpsimd
            def _(g):
                g.wait_ge(st.h, st.n)
                with allow_ici_all_to_all():
                    cc[3].inc(g.collective_compute(
                        "AllToAll", ALU.bypass, replica_groups=GROUPS8,
                        ins=[za_i.ap().opt()], outs=[za_o.ap().opt()]))
            esP3.close()

            if stop_after == "p3":
                @block.gpsimd
                def _(g):
                    g.wait_ge(cc[2].h, 1)
                    g.wait_ge(cc[3].h, 1)
                    g.wait_ge(st.h, st.n)
                return _finish(nc, es, esAx, esA)

            esAx.close()
            esP4 = contextlib.ExitStack()
            E4 = esP4.enter_context
            preb = E4(nc.sbuf_tensor(lpfx + "preb", [128, 8192], F32))
            wxpb = E4(nc.sbuf_tensor(lpfx + "wxpb", [128, 2 * 160], BF16))
            wdtb = E4(nc.sbuf_tensor(lpfx + "wdtb", [128, 2 * 128], BF16))
            t1b = E4(nc.sbuf_tensor(lpfx + "t1b", [128, TCH], F32))
            t2b = E4(nc.sbuf_tensor(lpfx + "t2b", [128, TCH], F32))
            esP4x = contextlib.ExitStack()
            xinb = esP4x.enter_context(nc.sbuf_tensor(lpfx + "xinb", [128, 8200], F32))
            esP4a = contextlib.ExitStack()
            E4a = esP4a.enter_context
            ubf = E4a(nc.sbuf_tensor(lpfx + "ubf", [128, 8192], BF16))
            xdba = E4a(nc.sbuf_tensor(lpfx + "xdba", [128, 64 * 160], F32))

            @block.scalar
            def _(s):
                s.wait_ge(cm.h, cm.n)
                load_act_set(nc, s, "exp_and_others")
                av.inc(s.nop())

            @block.sync
            def _(s):
                ld.dma(s.dma_start(out=xinb[:, 0:160], in_=wxp_d[0:128, :]))
                ld.dma(s.dma_start(out=xinb[:, 160:320], in_=wxp_d[128:256, :]))
                ld.dma(s.dma_start(out=xinb[:, 320:448], in_=wdt_d[:, 0:128]))
                ld.dma(s.dma_start(out=xinb[:, 448:576], in_=wdt_d[:, 128:256]))

            @block.vector
            def _(v):
                v.wait_ge(ld.h, ld.n)
                v.tensor_copy(out=wxpb[:, 0:160], in_=xinb[:, 0:160])
                v.tensor_copy(out=wxpb[:, 160:320], in_=xinb[:, 160:320])
                v.tensor_copy(out=wdtb[:, 0:128], in_=xinb[:, 320:448])
                cm.inc(v.tensor_copy(out=wdtb[:, 128:256], in_=xinb[:, 448:576]))
            p4w_done = cm.n

            for b in range(2):
                for dt_ in range(2):
                    pix = 2 * b + dt_

                    @block.gpsimd
                    def _(g, b=b, dt_=dt_):
                        g.wait_ge(cm.h, cm.n)
                        gm.inc(g.memset(xinb[:, 0:3], 0.0))

                    @block.sync
                    def _(s, b=b, dt_=dt_):
                        s.wait_ge(gm.h, gm.n)
                        if b == 0 and dt_ == 0:
                            s.wait_ge(cc[2].h, 1)
                        for jj in range(4):
                            ld.dma(s.dma_start(
                                out=xinb[:, 3 + jj * 2048:3 + (jj + 1) * 2048],
                                in_=xa_o.ap()[(4 * b + jj) * 256 + dt_ * 128:
                                              (4 * b + jj) * 256 + dt_ * 128 + 128, :]))

                    @block.vector
                    def _(v, b=b, dt_=dt_):
                        v.wait_ge(ld.h, ld.n)
                        v.wait_ge(st.h, st.n)
                        ckb = PRM_CK + dt_ * 4
                        sf.inc(v.tensor_scalar(out=preb[:], in0=xinb[:, 0:8192],
                                               scalar1=prm[:, ckb:ckb + 1], scalar2=None,
                                               op0=ALU.mult))
                        for k in range(1, 4):
                            v.wait_ge(sf.h, sf.n)
                            sf.inc(v.scalar_tensor_tensor(out=preb[:], in0=xinb[:, k:k + 8192],
                                                          scalar=prm[:, ckb + k:ckb + k + 1],
                                                          in1=preb[:], op0=ALU.mult, op1=ALU.add))
                        v.wait_ge(sf.h, sf.n)
                        cm.inc(v.tensor_scalar(out=preb[:], in0=preb[:],
                                               scalar1=prm[:, PRM_CB + dt_:PRM_CB + dt_ + 1],
                                               scalar2=None, op0=ALU.add))

                    for chh in range(4):
                        @block.scalar
                        def _(s, chh=chh):
                            s.wait_ge(cm.h, cm.n)
                            av.inc(s.activation(out=xinb[:, chh * 2048:(chh + 1) * 2048],
                                                in_=preb[:, chh * 2048:(chh + 1) * 2048],
                                                func=AF.Tanh, scale=0.5))

                    @block.vector
                    def _(v, b=b, dt_=dt_):
                        v.wait_ge(av.h, av.n)
                        sf.inc(v.tensor_scalar(out=xinb[:, 0:8192], in0=xinb[:, 0:8192],
                                                scalar1=0.5, scalar2=0.5,
                                                op0=ALU.mult, op1=ALU.add))
                        v.wait_ge(sf.h, sf.n)
                        sf.inc(v.tensor_mul(out=preb[:], in0=preb[:], in1=xinb[:, 0:8192]))
                        v.wait_ge(sf.h, sf.n)
                        v.tensor_copy(out=ubf[:, 0:4096], in_=preb[:, 0:4096])
                        cm.inc(v.tensor_copy(out=ubf[:, 4096:8192], in_=preb[:, 4096:8192]))

                    @block.gpsimd
                    def _(g, pix=pix):
                        g.wait_ge(cm.h, cm.n)
                        st.dma(g.dma_start(out=u_dr.ap()[pix * 128:(pix + 1) * 128, :],
                                           in_=preb[:]))
                        if "u" in dout:
                            st.dma(g.dma_start(out=dout["u"][:, pix * 8192:(pix + 1) * 8192],
                                               in_=preb[:]))

                    for tt in range(64):
                        @block.tensor
                        def _(t, tt=tt, dt_=dt_):
                            t.wait_ge(cm.h, cm.n)
                            if tt > 0:
                                t.wait_ge(cm.h, cm.n)
                            tp.inc(t.matmul(ps[:, (tt % 2) * 512:(tt % 2) * 512 + 160],
                                            ubf[:, tt * 128:(tt + 1) * 128],
                                            wxpb[:, dt_ * 160:(dt_ + 1) * 160],
                                            start=True, stop=True))

                        @block.vector
                        def _(v, tt=tt, dt_=dt_):
                            v.wait_ge(tp.h, tp.n)
                            if dt_ == 0 and tt == 0:
                                v.wait_ge(st.h, st.n)
                            if dt_ == 0:
                                ins = v.tensor_copy(out=xdba[:, tt * 160:(tt + 1) * 160],
                                                    in_=ps[:, (tt % 2) * 512:(tt % 2) * 512 + 160])
                            else:
                                ins = v.tensor_add(out=xdba[:, tt * 160:(tt + 1) * 160],
                                                   in0=xdba[:, tt * 160:(tt + 1) * 160],
                                                   in1=ps[:, (tt % 2) * 512:(tt % 2) * 512 + 160])
                            cm.inc(ins)

                if True:
                    @block.gpsimd
                    def _(g, b=b):
                        g.wait_ge(cm.h, cm.n)
                        last = None
                        for tt in range(64):
                            last = st.dma(g.dma_start(
                                out=xdb_bo.ap()[b * 8192 + tt * 128:b * 8192 + (tt + 1) * 128, :],
                                in_=xdba[:, tt * 160:(tt + 1) * 160]))

                @block.gpsimd
                def _(g, b=b):
                    g.wait_ge(st.h, st.n)
                    cc[4].inc(g.collective_compute(
                        "AllReduce", ALU.add, replica_groups=GROUPS8,
                        ins=[xdb_bo.ap()[b * 8192:(b + 1) * 8192, :].opt()],
                        outs=[xdb_fu.ap()[b * 8192:(b + 1) * 8192, :].opt()]))
            esP4a.close()

            if "xdb" in dout:
                @block.gpsimd
                def _(g):
                    g.wait_ge(cc[4].h, 2)
                    st.dma(g.dma_start(out=dout["xdb"], in_=xdb_fu.ap()))

            pb_ld, pb_tp, pb_cm, pb_st = [], [], [], []
            for b in range(2):
                for tt in range(64):
                    i4 = b * 64 + tt
                    s4 = i4 % 2

                    @block.sync
                    def _(s, b=b, tt=tt, s4=s4,
                          xfree=(pb_tp[i4 - 2] if i4 >= 2 else 0)):
                        if tt == 0:
                            s.wait_ge(cc[4].h, b + 1)
                        if tt == 0 and b == 0:
                            s.wait_ge(cm.h, cm.n)
                        if xfree:
                            s.wait_ge(tp.h, xfree)
                        ld.dma(s.dma_start(out=xinb[:, s4 * 2048:s4 * 2048 + 160],
                                           in_=xdb_fu.ap()[b * 8192 + tt * 128:
                                                           b * 8192 + (tt + 1) * 128, :]))
                    pb_ld.append(ld.n)

                    @block.tensor
                    def _(t, s4=s4, ldn=pb_ld[-1],
                          pfree=(pb_cm[i4 - 2] if i4 >= 2 else 0)):
                        t.wait_ge(ld.h, ldn)
                        if pfree:
                            t.wait_ge(cm.h, pfree)
                        t.transpose(ps[:, s4 * 1024:s4 * 1024 + 128],
                                    xinb[:, s4 * 2048:s4 * 2048 + 128], ident[:])
                        tp.inc(t.transpose(ps[:, s4 * 1024 + 512:s4 * 1024 + 640],
                                           xinb[:, s4 * 2048 + 32:s4 * 2048 + 160], ident[:]))
                    pb_tp.append(tp.n)

                    @block.vector
                    def _(v, s4=s4, tpn=pb_tp[-1],
                          tfree=(pb_st[i4 - 2] if i4 >= 2 else 0)):
                        v.wait_ge(tp.h, tpn)
                        if tfree:
                            v.wait_ge(st.h, tfree)
                        else:
                            v.wait_ge(st.h, st.n)
                        v.tensor_copy(out=t1b[:, s4 * 128:s4 * 128 + 128],
                                      in_=ps[:, s4 * 1024:s4 * 1024 + 128])
                        cm.inc(v.tensor_copy(out=t2b[:, s4 * 128:s4 * 128 + 128],
                                             in_=ps[:, s4 * 1024 + 512:s4 * 1024 + 640]))
                    pb_cm.append(cm.n)

                    @block.gpsimd
                    def _(g, b=b, tt=tt, s4=s4, cmn=pb_cm[-1]):
                        g.wait_ge(cm.h, cmn)
                        st.dma(g.dma_start(out=dtl_dr.ap()[b * 128:(b + 1) * 128,
                                                           tt * 128:(tt + 1) * 128],
                                           in_=t1b[:, s4 * 128:s4 * 128 + 128]))
                        st.dma(g.dma_start(out=bc_dr.ap()[b * 32:(b + 1) * 32,
                                                          tt * 128:(tt + 1) * 128],
                                           in_=t2b[96:128, s4 * 128:s4 * 128 + 128]))
                    pb_st.append(st.n)

            yz_hist = []
            row_ldB, row_ld, dta_sf = [], [], []
            dbu_sf, main1_sf, gp_gm, avA = [], [], [], []
            esP4x.close()
            esP4c = contextlib.ExitStack()
            E4c = esP4c.enter_context
            dtf = E4c(nc.sbuf_tensor(lpfx + "dtf", [128, 8192], F32))
            dlt = E4c(nc.sbuf_tensor(lpfx + "dlt", [128, 8192], F32))
            wch = E4c(nc.sbuf_tensor(lpfx + "wch", [128, 2048], F32))
            tA = E4c(nc.sbuf_tensor(lpfx + "tA", [128, 2048], F32))
            tB = E4c(nc.sbuf_tensor(lpfx + "tB", [128, 2048], F32))
            tC = E4c(nc.sbuf_tensor(lpfx + "tC", [128, 2 * 2048], F32))
            ych = E4c(nc.sbuf_tensor(lpfx + "ych", [128, 2048], F32))
            zch = E4c(nc.sbuf_tensor(lpfx + "zch", [128, 2048], F32))
            sbB = E4c(nc.sbuf_tensor(lpfx + "sbB", [128, 2048], F32))
            sbC = E4c(nc.sbuf_tensor(lpfx + "sbC", [128, 2048], F32))
            dltb = E4c(nc.sbuf_tensor(lpfx + "dltb", [128, 512], BF16))
            carry = E4c(nc.sbuf_tensor(lpfx + "carry", [128, NST], F32))

            for b in range(2):
                for dt_ in range(2):
                    pix = 2 * b + dt_

                    @block.sync
                    def _(s, b=b, dt_=dt_, pix=pix):
                        s.wait_ge(cm.h, cm.n)
                        s.wait_ge(st.h, st.n)
                        for hh in range(2):
                            ld.dma(s.dma_start(out=preb[:, hh * 4096:(hh + 1) * 4096],
                                               in_=u_dr.ap()[pix * 128:(pix + 1) * 128,
                                                             hh * 4096:(hh + 1) * 4096]))
                            ld.dma(s.dma_start(out=dlt[:, hh * 4096:(hh + 1) * 4096],
                                               in_=dtl_dr.ap()[b * 128:(b + 1) * 128,
                                                               hh * 4096:(hh + 1) * 4096]))

                    @block.gpsimd
                    def _(g):
                        g.wait_ge(cm.h, cm.n)
                        gm.inc(g.memset(carry[:], 0.0))
                    pass_gm = gm.n

                    @block.scalar
                    def _(s):
                        load_act_set(nc, s, "natural_log_exp_and_others")
                        av.inc(s.nop())

                    for tch in range(16):
                        @block.vector
                        def _(v, tch=tch):
                            v.wait_ge(ld.h, ld.n)
                            v.wait_ge(tp.h, tp.n)
                            cm.inc(v.tensor_copy(out=dltb[:],
                                                 in_=dlt[:, tch * 512:(tch + 1) * 512]))

                        @block.tensor
                        def _(t, dt_=dt_, tch=tch):
                            t.wait_ge(cm.h, cm.n)
                            tp.inc(t.matmul(ps[:, (tch % 2) * 512:(tch % 2) * 512 + TCH],
                                            wdtb[:, dt_ * 128:(dt_ + 1) * 128],
                                            dltb[:],
                                            start=True, stop=True))

                        @block.vector
                        def _(v, dt_=dt_, tch=tch):
                            v.wait_ge(tp.h, tp.n)
                            cm.inc(v.tensor_scalar(out=t1b[:], in0=ps[:, (tch % 2) * 512:(tch % 2) * 512 + TCH],
                                                   scalar1=prm[:, PRM_BDT + dt_:PRM_BDT + dt_ + 1],
                                                   scalar2=None, op0=ALU.add))

                        @block.scalar
                        def _(s, tch=tch):
                            s.wait_ge(cm.h, cm.n)
                            s.activation(out=t2b[:], in_=t1b[:], func=AF.Abs)
                            s.activation(out=sqs[:], in_=t2b[:], func=AF.Exp, scale=-1.0)
                            s.activation(out=t2b[:], in_=sqs[:], func=AF.Ln, bias=1.0)
                            av.inc(s.activation(out=sqs[:], in_=t1b[:], func=AF.Relu))

                        @block.vector
                        def _(v, tch=tch):
                            v.wait_ge(av.h, av.n)
                            cm.inc(v.tensor_add(out=dtf[:, tch * 512:(tch + 1) * 512],
                                                in0=t2b[:], in1=sqs[:]))

                    if "dtf" in dout and b == 0:
                        @block.gpsimd
                        def _(g, dt_=dt_):
                            g.wait_ge(cm.h, cm.n)
                            st.dma(g.dma_start(out=dout["dtf"][:, dt_ * 8192:(dt_ + 1) * 8192]
                                               if dt_ < 2 else None, in_=dtf[:]))

                    @block.scalar
                    def _(s):
                        s.wait_ge(cm.h, cm.n)
                        load_act_set(nc, s, "exp_and_others")
                        av.inc(s.nop())

                    for ch in range(4):
                        @block.vector
                        def _(v, b=b, dt_=dt_, ch=ch):
                            v.wait_ge(cm.h, cm.n)
                            v.wait_ge(av.h, av.n)
                            v.memset(ych[:], 0.0)
                            cm.inc(v.tensor_mul(out=wch[:],
                                                in0=dtf[:, ch * 2048:(ch + 1) * 2048],
                                                in1=preb[:, ch * 2048:(ch + 1) * 2048]))

                        chhead_cm = cm.n

                        def emit_pre(k, ch=ch, dt_=dt_):
                            @block.vector
                            def _(v, k=k, avn=(avA[k - 2] if k >= 2 else 0)):
                                if avn:
                                    v.wait_ge(av.h, avn)
                                acol = PRM_A + dt_ * 16 + k
                                dst = tA if k % 2 == 0 else tB
                                sf.inc(v.tensor_scalar(out=dst[:],
                                                       in0=dtf[:, ch * 2048:(ch + 1) * 2048],
                                                       scalar1=prm[:, acol:acol + 1],
                                                       scalar2=None, op0=ALU.mult))
                            while len(dta_sf) <= k:
                                dta_sf.append(None)
                            dta_sf[k] = sf.n

                            @block.scalar
                            def _(s, k=k, sfn=dta_sf[k]):
                                s.wait_ge(sf.h, sfn)
                                src = tA if k % 2 == 0 else tB
                                av.inc(s.activation(
                                    out=dlt[:, 4096 + (k % 2) * 2048:4096 + (k % 2 + 1) * 2048],
                                    in_=src[:], func=AF.Exp))
                            while len(avA) <= k:
                                avA.append(None)
                            avA[k] = av.n

                        for n in range(NST):
                            sl = n % 2

                            @block.sync
                            def _(s, b=b, ch=ch, n=n,
                                  bfree=(dbu_sf[-1] if dbu_sf else 0),
                                  cfree=(gp_gm[-1] if gp_gm else 0)):
                                if bfree:
                                    s.wait_ge(sf.h, bfree)
                                ld.dma(s.dma_start(
                                    out=sbB[:],
                                    in_=bc_dr.ap()[b * 32 + n:b * 32 + n + 1,
                                                   ch * 2048:(ch + 1) * 2048].partition_broadcast(128)))
                                if cfree:
                                    s.wait_ge(gm.h, cfree)
                                ld.dma(s.dma_start(
                                    out=sbC[:],
                                    in_=bc_dr.ap()[b * 32 + 16 + n:b * 32 + 17 + n,
                                                   ch * 2048:(ch + 1) * 2048].partition_broadcast(128)))
                            row_ldB.append(ld.n - 16)
                            row_ld.append(ld.n)

                            if n == 0:
                                emit_pre(0)
                            if n + 1 < NST:
                                emit_pre(n + 1)

                            @block.vector
                            def _(v, ch=ch, n=n, sl=sl, avn=avA[n], ldn=row_ldB[-1],
                                  tfree=(gp_gm[-2] if len(gp_gm) >= 2 else 0),
                                  pgm=(pass_gm if (ch == 0 and n == 0) else 0)):
                                v.wait_ge(av.h, avn)
                                v.wait_ge(ld.h, ldn)
                                if tfree:
                                    v.wait_ge(gm.h, tfree)
                                if pgm:
                                    v.wait_ge(gm.h, pgm)
                                dst = tA if sl == 0 else tB
                                sf.inc(v.tensor_mul(out=dst[:], in0=wch[:], in1=sbB[:]))
                                v.wait_ge(sf.h, sf.n)
                                v.tensor_tensor_scan(out=tC[:, sl * 2048:(sl + 1) * 2048],
                                                     data0=dlt[:, 4096 + sl * 2048:4096 + (sl + 1) * 2048],
                                                     data1=dst[:], initial=carry[:, n:n + 1],
                                                     op0=ALU.mult, op1=ALU.add)
                                sf.inc(v.tensor_copy(out=sqs[:, 0:1], in_=ones_c[:]))
                                v.wait_ge(sf.h, sf.n)
                                sf.inc(v.tensor_copy(out=carry[:, n:n + 1],
                                                     in_=tC[:, sl * 2048 + 2047:sl * 2048 + 2048]))
                            dbu_sf.append(sf.n - 2)
                            main1_sf.append(sf.n)

                            @block.vector
                            def _(v, n=n, sl=sl, ldn=row_ld[-1]):
                                v.wait_ge(ld.h, ldn)
                                sf.inc(v.tensor_mul(out=zch[:], in0=tC[:, sl * 2048:(sl + 1) * 2048],
                                                    in1=sbC[:]))
                                v.wait_ge(sf.h, sf.n)
                                gm.inc(v.tensor_add(out=ych[:], in0=ych[:], in1=zch[:]))
                            gp_gm.append(gm.n)

                        @block.sync
                        def _(s, b=b, dt_=dt_, ch=ch, gmn=gp_gm[-1]):
                            s.wait_ge(cm.h, cm.n)
                            s.wait_ge(gm.h, gmn)
                            if b == 0 and dt_ == 0 and ch == 0:
                                s.wait_ge(cc[3].h, 1)
                            ld.dma(s.dma_start(out=zch[:],
                                               in_=za_o.ap()[(4 * b + ch) * 256 + dt_ * 128:
                                                             (4 * b + ch) * 256 + dt_ * 128 + 128, :]))

                        @block.scalar
                        def _(s, ch=ch):
                            s.wait_ge(ld.h, ld.n)
                            av.inc(s.activation(out=tB[:], in_=zch[:], func=AF.Tanh, scale=0.5))

                        @block.vector
                        def _(v, b=b, dt_=dt_, ch=ch, gmn=gp_gm[-1]):
                            v.wait_ge(av.h, av.n)
                            v.wait_ge(gm.h, gmn)
                            if len(yz_hist) >= 2:
                                v.wait_ge(st.h, yz_hist[-2])
                            sf.inc(v.scalar_tensor_tensor(out=ych[:],
                                                          in0=preb[:, ch * 2048:(ch + 1) * 2048],
                                                          scalar=prm[:, PRM_D + dt_:PRM_D + dt_ + 1],
                                                          in1=ych[:], op0=ALU.mult, op1=ALU.add))
                            sf.inc(v.tensor_scalar(out=tB[:], in0=tB[:], scalar1=0.5, scalar2=0.5,
                                                   op0=ALU.mult, op1=ALU.add))
                            v.wait_ge(sf.h, sf.n)
                            sf.inc(v.tensor_mul(out=tA[:], in0=zch[:], in1=tB[:]))
                            v.wait_ge(sf.h, sf.n)
                            cm.inc(v.tensor_mul(out=dlt[:, (ch % 2) * 2048:(ch % 2 + 1) * 2048],
                                                in0=ych[:], in1=tA[:]))

                        @block.gpsimd
                        def _(g, b=b, dt_=dt_, ch=ch):
                            g.wait_ge(cm.h, cm.n)
                            st.dma(g.dma_start(
                                out=yz_i.ap()[(4 * b + ch) * 256 + dt_ * 128:
                                              (4 * b + ch) * 256 + dt_ * 128 + 128, :],
                                in_=dlt[:, (ch % 2) * 2048:(ch % 2 + 1) * 2048]))
                            if "yz" in dout:
                                st.dma(g.dma_start(
                                    out=dout["yz"][:, (2 * b + dt_) * 8192 + ch * 2048:
                                                   (2 * b + dt_) * 8192 + (ch + 1) * 2048],
                                    in_=dlt[:, (ch % 2) * 2048:(ch % 2 + 1) * 2048]))
                        yz_hist.append(st.n)

            @block.gpsimd
            def _(g):
                g.wait_ge(st.h, st.n)
                with allow_ici_all_to_all():
                    cc[5].inc(g.collective_compute(
                        "AllToAll", ALU.bypass, replica_groups=GROUPS8,
                        ins=[yz_i.ap().opt()], outs=[yz_o.ap().opt()]))
            esP4c.close()
            esP4.close()
            if stop_after == "scan":
                @block.gpsimd
                def _(g):
                    g.wait_ge(cc[5].h, 1)
                    g.wait_ge(st.h, st.n)
                return _finish(nc, es, esA)

            esP5 = contextlib.ExitStack()
            E5 = esP5.enter_context
            oh5 = E5(nc.sbuf_tensor(lpfx + "oh5", [16, 4096], F32))
            cs5 = E5(nc.sbuf_tensor(lpfx + "cs5", [16, 4096], F32))
            prow = E5(nc.sbuf_tensor(lpfx + "prow", [1, 4096], F32))
            prow2 = E5(nc.sbuf_tensor(lpfx + "prow2", [1, 4096], F32))
            vrow = E5(nc.sbuf_tensor(lpfx + "vrow", [1, 4096], F32))
            grow = E5(nc.sbuf_tensor(lpfx + "grow", [1, 4096], F32))
            car16 = E5(nc.sbuf_tensor(lpfx + "car16", [16, 1], F32))
            ccol = E5(nc.sbuf_tensor(lpfx + "ccol", [128, 2], F32))
            pcol = E5(nc.sbuf_tensor(lpfx + "pcol", [128, 8], F32))
            mh5 = E5(nc.sbuf_tensor(lpfx + "mh5", [128, 16], F32))
            mlo5 = E5(nc.sbuf_tensor(lpfx + "mlo5", [128, 128], F32))
            tb32 = E5(nc.sbuf_tensor(lpfx + "tb32", [32, 128], F32))
            ttab = E5(nc.sbuf_tensor(lpfx + "ttab", [128, 32], F32))

            @block.sync
            def _(s):
                s.wait_ge(cm.h, cm.n)
                ld.dma(s.dma_start(out=prow[0:1, 0:4], in_=ccst_d))

            @block.tensor
            def _(t):
                t.wait_ge(ld.h, ld.n)
                tp.inc(t.matmul(ps[:, 0:4], ones_r[:], prow[0:1, 0:4], start=True, stop=True))

            @block.vector
            def _(v):
                v.wait_ge(tp.h, tp.n)
                cm.inc(v.tensor_copy(out=ccol[:, 0:2], in_=ps[:, 0:2]))

            @block.gpsimd
            def _(g):
                g.wait_ge(cm.h, cm.n)
                gm.inc(g.memset(car16[:], 0.0))

            for ch5 in range(4):
                @block.sync
                def _(s, ch5=ch5):
                    s.wait_ge(gm.h, gm.n)
                    s.wait_ge(cc[1].h, 2)
                    s.wait_ge(cm.h, cm.n)
                    s.wait_ge(st.h, st.n)
                    ld.dma(s.dma_start(out=prow[:],
                                       in_=idx_fu.ap()[ch5 * 4096:(ch5 + 1) * 4096, :]))
                    ld.dma(s.dma_start(out=grow[:],
                                       in_=gsv_fu.ap()[ch5 * 4096:(ch5 + 1) * 4096, :]))

                @block.tensor
                def _(t, ch5=ch5):
                    t.wait_ge(ld.h, ld.n)
                    last = None
                    for q in range(8):
                        last = t.matmul(ps[0:16, q * 512:(q + 1) * 512],
                                        ones_r[0:1, 0:16], prow[0:1, q * 512:(q + 1) * 512],
                                        start=True, stop=True)
                    tp.inc(last)

                @block.vector
                def _(v, ch5=ch5):
                    v.wait_ge(tp.h, tp.n)
                    sf.inc(v.tensor_scalar(out=oh5[:], in0=ps[0:16, 0:4096],
                                           scalar1=iotaC[0:16, 0:1], scalar2=None,
                                           op0=ALU.is_equal))
                    v.wait_ge(sf.h, sf.n)
                    v.tensor_tensor_scan(out=cs5[:], data0=oh5[:], data1=oh5[:],
                                         initial=car16[:, 0:1], op0=ALU.add, op1=ALU.bypass)
                    sf.inc(v.tensor_copy(out=sqs[:, 0:1], in_=ones_c[:]))
                    v.wait_ge(sf.h, sf.n)
                    sf.inc(v.tensor_copy(out=car16[:, 0:1], in_=cs5[:, 4095:4096]))
                    v.wait_ge(sf.h, sf.n)
                    cm.inc(v.tensor_mul(out=oh5[:], in0=oh5[:], in1=cs5[:]))

                @block.tensor
                def _(t, ch5=ch5):
                    t.wait_ge(cm.h, cm.n)
                    last = None
                    for q in range(8):
                        last = t.matmul(ps[0:1, 2048 + q * 256:2048 + q * 256 + 512]
                                        if False else ps[0:1, q * 512:q * 512 + 512],
                                        ones_c[0:16, 0:1], oh5[:, q * 512:(q + 1) * 512],
                                        start=True, stop=True)
                    tp.inc(last)

                @block.vector
                def _(v, ch5=ch5):
                    v.wait_ge(tp.h, tp.n)
                    sf.inc(v.tensor_scalar_add(out=prow2[:], in0=ps[0:1, 0:4096], scalar1=-1.0))
                    v.wait_ge(sf.h, sf.n)
                    sf.inc(v.tensor_scalar(out=vrow[:], in0=prow2[:], scalar1=1023.0,
                                           scalar2=None, op0=ALU.is_le))
                    v.wait_ge(sf.h, sf.n)
                    sf.inc(v.tensor_mul(out=grow[:], in0=grow[:], in1=vrow[:]))
                    sf.inc(v.tensor_scalar(out=prow2[:], in0=prow2[:], scalar1=1023.0,
                                           scalar2=None, op0=ALU.min))
                    v.wait_ge(sf.h, sf.n)
                    sf.inc(v.scalar_tensor_tensor(out=prow2[:], in0=prow[:], scalar=1024.0,
                                                  in1=prow2[:], op0=ALU.mult, op1=ALU.add))
                    sf.inc(v.tensor_scalar(out=vrow[:], in0=vrow[:], scalar1=-200000.0,
                                           scalar2=200000.0, op0=ALU.mult, op1=ALU.add))
                    v.wait_ge(sf.h, sf.n)
                    cm.inc(v.tensor_add(out=vrow[:], in0=vrow[:], in1=prow2[:]))

                @block.gpsimd
                def _(g, ch5=ch5):
                    g.wait_ge(cm.h, cm.n)
                    st.dma(g.dma_start(out=pcl_bo.ap()[ch5 * 4096:(ch5 + 1) * 4096, :],
                                       in_=prow2[:]))
                    st.dma(g.dma_start(out=gcl_bo.ap()[ch5 * 4096:(ch5 + 1) * 4096, :],
                                       in_=grow[:]))
                    st.dma(g.dma_start(out=p_dr.ap()[ch5 * 4096:(ch5 + 1) * 4096, :],
                                       in_=vrow[:]))
                    if "ptab" in dout:
                        st.dma(g.dma_start(out=dout["ptab"][ch5 * 4096:(ch5 + 1) * 4096, :],
                                           in_=vrow[:]))
                    if "gst" in dout:
                        st.dma(g.dma_start(out=dout["gst"][ch5 * 4096:(ch5 + 1) * 4096, :],
                                           in_=grow[:]))

            @block.gpsimd
            def _(g):
                g.wait_ge(st.h, st.n)
                with allow_ici_all_to_all():
                    cc[6].inc(g.collective_compute(
                        "AllToAll", ALU.bypass, replica_groups=GROUPS8,
                        ins=[pcl_bo.ap().opt()], outs=[pcl_sl.ap().opt()]))
                    cc[6].inc(g.collective_compute(
                        "AllToAll", ALU.bypass, replica_groups=GROUPS8,
                        ins=[gcl_bo.ap().opt()], outs=[gcl_sl.ap().opt()]))

            for tt5 in range(128):
                @block.sync
                def _(s, tt5=tt5):
                    s.wait_ge(cm.h, cm.n)
                    ld.dma(s.dma_start(out=pcol[:, 0:1],
                                       in_=p_dr.ap()[tt5 * 128:(tt5 + 1) * 128, :]))

                @block.vector
                def _(v, tt5=tt5):
                    v.wait_ge(ld.h, ld.n)
                    sf.inc(v.tensor_scalar(out=pcol[:, 1:2], in0=pcol[:, 0:1],
                                           scalar1=ccol[:, 0:1], scalar2=None, op0=ALU.subtract))
                    v.wait_ge(sf.h, sf.n)
                    sf.inc(v.tensor_scalar(out=pcol[:, 2:3], in0=pcol[:, 1:2], scalar1=0.0,
                                           scalar2=None, op0=ALU.is_lt))
                    v.wait_ge(sf.h, sf.n)
                    sf.inc(v.scalar_tensor_tensor(out=pcol[:, 1:2], in0=pcol[:, 2:3],
                                                  scalar=300000.0, in1=pcol[:, 1:2],
                                                  op0=ALU.mult, op1=ALU.add))
                    v.wait_ge(sf.h, sf.n)
                    sf.inc(v.tensor_scalar(out=pcol[:, 4:5], in0=pcol[:, 1:2],
                                           scalar1=1.0 / 128.0, scalar2=-0.49609375,
                                           op0=ALU.mult, op1=ALU.add))
                    v.wait_ge(sf.h, sf.n)
                    sf.inc(v.tensor_scalar_add(out=pcol[:, 4:5], in0=pcol[:, 4:5],
                                               scalar1=12582912.0))
                    v.wait_ge(sf.h, sf.n)
                    sf.inc(v.tensor_scalar_add(out=pcol[:, 4:5], in0=pcol[:, 4:5],
                                               scalar1=-12582912.0))
                    v.wait_ge(sf.h, sf.n)
                    sf.inc(v.scalar_tensor_tensor(out=pcol[:, 3:4], in0=pcol[:, 4:5],
                                                  scalar=-128.0, in1=pcol[:, 1:2],
                                                  op0=ALU.mult, op1=ALU.add))
                    v.wait_ge(sf.h, sf.n)
                    v.tensor_scalar(out=mh5[:], in0=iotaF[:, 0:16],
                                    scalar1=pcol[:, 4:5], scalar2=None, op0=ALU.is_equal)
                    sf.inc(v.tensor_scalar(out=mlo5[:], in0=iotaF[:],
                                           scalar1=pcol[:, 3:4], scalar2=None, op0=ALU.is_equal))
                    sf.inc(v.tensor_scalar_add(out=pcol[:, 5:6], in0=iotaC[:],
                                               scalar1=float(tt5 * 128 + 1)))
                    v.wait_ge(sf.h, sf.n)
                    cm.inc(v.tensor_scalar(out=mh5[:], in0=mh5[:],
                                           scalar1=pcol[:, 5:6], scalar2=None, op0=ALU.mult))

                @block.tensor
                def _(t, tt5=tt5):
                    t.wait_ge(cm.h, cm.n)
                    tp.inc(t.matmul(ps[:, 3584:3584 + 16], mlo5[:], mh5[:],
                                    start=(tt5 == 0), stop=(tt5 == 127)))

            @block.vector
            def _(v):
                v.wait_ge(tp.h, tp.n)
                sf.inc(v.tensor_copy(out=ttab[:, 0:16], in_=ps[:, 3584:3584 + 16]))
                v.wait_ge(sf.h, sf.n)
                sf.inc(v.tensor_scalar(out=ttab[:, 16:32], in0=ttab[:, 0:16], scalar1=0.0,
                                       scalar2=None, op0=ALU.is_equal))
                sf.inc(v.tensor_scalar_add(out=ttab[:, 0:16], in0=ttab[:, 0:16], scalar1=-1.0))
                v.wait_ge(sf.h, sf.n)
                cm.inc(v.scalar_tensor_tensor(out=ttab[:, 0:16], in0=ttab[:, 16:32],
                                              scalar=100000.0, in1=ttab[:, 0:16],
                                              op0=ALU.mult, op1=ALU.add))

            if "ttb" in dout:
                @block.gpsimd
                def _(g):
                    g.wait_ge(cm.h, cm.n)
                    st.dma(g.dma_start(out=dout["ttb"], in_=ttab[:]))

            @block.scalar
            def _(s):
                s.wait_ge(cm.h, cm.n)
                av.inc(s.copy(out=tu32[:], in_=ttab[:, 0:16]))
            ttab_done = av.n

            esP5.close()
            esP5b = contextlib.ExitStack()
            E5b = esP5b.enter_context
            dtok = E5b(nc.sbuf_tensor(lpfx + "dtok", [128, DM], BF16))
            dtokf = E5b(nc.sbuf_tensor(lpfx + "dtokf", [128, DM], F32))
            dfm = E5b(nc.sbuf_tensor(lpfx + "dfm", [128, 16 * 1024], BF16))
            wld5 = E5b(nc.sbuf_tensor(lpfx + "wld5", [128, 2 * 1024], F32))
            wst5 = E5b(nc.sbuf_tensor(lpfx + "wst5", [128, 2 * 1024], BF16))
            hbm = E5b(nc.sbuf_tensor(lpfx + "hbm", [128, 32 * 1024], BF16))
            eof = E5b(nc.sbuf_tensor(lpfx + "eof", [128, 1024], F32))
            eotk = E5b(nc.sbuf_tensor(lpfx + "eotk", [128, 8 * DM], BF16))

            for g5 in range(16):
                @block.gpsimd
                def _(g, g5=g5):
                    if g5 == 0:
                        g.wait_ge(av.h, ttab_done)
                        g.wait_ge(cc[0].h, 1)
                        r = g.alloc_register()
                        g.reg_mov(r, 16383)
                        nc._moe_breg = r
                    g.wait_ge(cm.h, cm.n)
                    gm.inc(g.memset(dtok[:], 0.0))

                @block.gpsimd
                def _(g, g5=g5):
                    g.wait_ge(gm.h, gm.n)
                    st.dma(g.indirect_dma_start(
                        out=dtok[:], out_offset=None,
                        in_=xn1_fu.ap(), in_offset=IndirectOffsetOnAxis(tu32[:, g5:g5 + 1], 0),
                        bounds_check=nc._moe_breg, oob_is_err=False))

                @block.vector
                def _(v, g5=g5):
                    v.wait_ge(st.h, st.n)
                    cm.inc(v.tensor_copy(out=dtokf[:], in_=dtok[:]))

                @block.tensor
                def _(t, g5=g5):
                    t.wait_ge(cm.h, cm.n)
                    for ct in range(NCT):
                        tp.inc(t.transpose(ps[:, ct * 512:ct * 512 + 128],
                                           dtokf[:, ct * 128:(ct + 1) * 128], ident[:]))

                @block.vector
                def _(v, g5=g5):
                    v.wait_ge(tp.h, tp.n)
                    last = None
                    for ct in range(NCT):
                        last = v.tensor_copy(out=dfm[:, ct * 2048 + g5 * 128:ct * 2048 + (g5 + 1) * 128],
                                             in_=ps[:, ct * 512:ct * 512 + 128])
                    cm.inc(last)
            disp_done = cm.n

            cm_cast = {}
            tp_mm = {}
            av_re = {}
            ld_ld = {}
            ev_cm = {}
            li = 0
            for e5 in range(2):
                e_first = li
                for phase in range(2):
                    for sub5 in range(32):
                        if phase == 0:
                            fi = sub5
                        else:
                            ot, kq = sub5 // 4, sub5 % 4
                        bb = li % 2

                        @block.sync
                        def _(s, e5=e5, phase=phase, sub5=sub5, li=li, bb=bb):
                            if li >= 2:
                                s.wait_ge(cm.h, cm_cast[li - 2])
                            else:
                                s.wait_ge(cm.h, cm.n)
                                s.wait_ge(tp.h, tp.n)
                            for kt in range(8):
                                if phase == 0:
                                    src = w1s_d[e5 * 1024 + kt * 128:e5 * 1024 + (kt + 1) * 128,
                                                sub5 * 128:(sub5 + 1) * 128]
                                else:
                                    ot, kq = sub5 // 4, sub5 % 4
                                    src = w2s_d[e5 * 4096 + kq * 1024 + kt * 128:
                                                e5 * 4096 + kq * 1024 + (kt + 1) * 128,
                                                ot * 128:(ot + 1) * 128]
                                ld.dma(s.dma_start(
                                    out=wld5[:, bb * 1024 + kt * 128:bb * 1024 + (kt + 1) * 128],
                                    in_=src))
                        ld_ld[li] = ld.n

                        @block.vector
                        def _(v, li=li, bb=bb):
                            v.wait_ge(ld.h, ld_ld[li])
                            if li >= 2:
                                v.wait_ge(tp.h, tp_mm[li - 2])
                            cm.inc(v.tensor_copy(out=wst5[:, bb * 1024:(bb + 1) * 1024],
                                                 in_=wld5[:, bb * 1024:(bb + 1) * 1024]))
                        cm_cast[li] = cm.n

                        @block.tensor
                        def _(t, e5=e5, phase=phase, sub5=sub5, li=li, bb=bb, e_first=e_first):
                            t.wait_ge(cm.h, cm_cast[li])
                            if phase == 0:
                                fi = sub5
                                if fi >= 2:
                                    t.wait_ge(av.h, av_re[e_first + fi - 2])
                                for kt in range(8):
                                    for hh in range(2):
                                        ins = t.matmul(
                                            ps[:, (fi % 2) * 1024 + hh * 512:
                                               (fi % 2) * 1024 + hh * 512 + 512],
                                            wst5[:, bb * 1024 + kt * 128:bb * 1024 + (kt + 1) * 128],
                                            dfm[:, kt * 2048 + e5 * 1024 + hh * 512:
                                                kt * 2048 + e5 * 1024 + hh * 512 + 512],
                                            start=(kt == 0), stop=(kt == 7))
                                tp.inc(ins)
                            else:
                                ot, kq = sub5 // 4, sub5 % 4
                                if sub5 == 0:
                                    t.wait_ge(av.h, av_re[e_first + 31])
                                if kq == 0 and ot >= 1:
                                    t.wait_ge(cm.h, ev_cm[(e5, ot - 1)])
                                for kt in range(8):
                                    for hh in range(2):
                                        ins = t.matmul(
                                            ps[:, 2048 + hh * 512:2048 + hh * 512 + 512],
                                            wst5[:, bb * 1024 + kt * 128:bb * 1024 + (kt + 1) * 128],
                                            hbm[:, (kq * 8 + kt) * 1024 + hh * 512:
                                                (kq * 8 + kt) * 1024 + hh * 512 + 512],
                                            start=(kq == 0 and kt == 0),
                                            stop=(kq == 3 and kt == 7))
                                tp.inc(ins)
                        tp_mm[li] = tp.n

                        if phase == 0:
                            fi = sub5

                            @block.scalar
                            def _(s, e5=e5, fi=fi, li=li):
                                s.wait_ge(tp.h, tp_mm[li])
                                if e5 == 1 and fi == 0:
                                    s.wait_ge(tp.h, tp_mm[li - 1])
                                last = None
                                for hh in range(2):
                                    last = s.activation(
                                        out=hbm[:, fi * 1024 + hh * 512:fi * 1024 + hh * 512 + 512],
                                        in_=ps[:, (fi % 2) * 1024 + hh * 512:
                                               (fi % 2) * 1024 + hh * 512 + 512],
                                        func=AF.Relu)
                                av.inc(last)
                            av_re[li] = av.n
                        elif sub5 % 4 == 3:
                            ot = sub5 // 4

                            @block.vector
                            def _(v, e5=e5, ot=ot, li=li):
                                v.wait_ge(tp.h, tp_mm[li])
                                if e5 == 1 and ot == 0:
                                    v.wait_ge(st.h, st.n)
                                cm.inc(v.tensor_copy(out=eof[:], in_=ps[:, 2048:3072]))

                            @block.tensor
                            def _(t, ot=ot):
                                t.wait_ge(cm.h, cm.n)
                                for sub in range(8):
                                    tp.inc(t.transpose(
                                        ps[:, 3072 + sub * 128:3072 + (sub + 1) * 128],
                                        eof[:, sub * 128:(sub + 1) * 128], ident[:]))

                            @block.vector
                            def _(v, ot=ot):
                                v.wait_ge(tp.h, tp.n)
                                last = None
                                for sub in range(8):
                                    last = v.tensor_copy(
                                        out=eotk[:, sub * DM + ot * 128:sub * DM + (ot + 1) * 128],
                                        in_=ps[:, 3072 + sub * 128:3072 + (sub + 1) * 128])
                                cm.inc(last)
                            ev_cm[(e5, ot)] = cm.n
                        li += 1

                @block.gpsimd
                def _(g, e5=e5):
                    g.wait_ge(cm.h, ev_cm[(e5, 7)])
                    for sub in range(8):
                        st.dma(g.dma_start(
                            out=eo_bo.ap()[e5 * 1024 + sub * 128:e5 * 1024 + (sub + 1) * 128, :],
                            in_=eotk[:, sub * DM:(sub + 1) * DM]))

            @block.gpsimd
            def _(g):
                g.wait_ge(st.h, st.n)
                cc[7].inc(g.collective_compute(
                    "AllGather", ALU.bypass, replica_groups=GROUPS8,
                    ins=[eo_bo.ap().opt()], outs=[eo_fu.ap().opt()]))

            if "eo" in dout:
                @block.gpsimd
                def _(g):
                    g.wait_ge(cc[7].h, 1)
                    st.dma(g.dma_start(out=dout["eo"], in_=eo_bo.ap()))
            esP5b.close()
            if stop_after == "moe":
                @block.gpsimd
                def _(g):
                    g.wait_ge(cc[6].h, 2)
                    g.wait_ge(cc[7].h, 1)
                    g.wait_ge(st.h, st.n)
                return _finish(nc, es, esA)

            esP6 = contextlib.ExitStack()
            E6 = esP6.enter_context
            wob = E6(nc.sbuf_tensor(lpfx + "wob", [128, 16 * 1024], BF16))
            wld6 = E6(nc.sbuf_tensor(lpfx + "wld6", [128, 1024], F32))
            yzl = E6(nc.sbuf_tensor(lpfx + "yzl", [128, 2 * 16 * 128], F32))
            yzb = E6(nc.sbuf_tensor(lpfx + "yzb", [128, 2 * 16 * 128], BF16))
            s1t = E6(nc.sbuf_tensor(lpfx + "s1t", [128, 2 * DM], F32))
            eor = E6(nc.sbuf_tensor(lpfx + "eor", [128, 2 * DM], BF16))
            g1t = E6(nc.sbuf_tensor(lpfx + "g1t", [128, 2 * DM], F32))
            x1t = E6(nc.sbuf_tensor(lpfx + "x1t", [128, 2 * DM], F32))
            pcol6 = E6(nc.sbuf_tensor(lpfx + "pcol6", [128, 2 * 2], F32))
            pu32 = E6(nc.sbuf_tensor(lpfx + "pu32", [128, 2 * 1], U32))
            if "x1late" in dout:
                x1t2 = E6(nc.sbuf_tensor(lpfx + "x1t2", [128, DM], F32))

            for dt6 in range(16):
                @block.sync
                def _(s, dt6=dt6):
                    s.wait_ge(cm.h, cm.n)
                    ld.dma(s.dma_start(out=wld6[:], in_=wout_d[dt6 * 128:(dt6 + 1) * 128, :]))

                @block.vector
                def _(v, dt6=dt6):
                    v.wait_ge(ld.h, ld.n)
                    cm.inc(v.tensor_copy(out=wob[:, dt6 * 1024:(dt6 + 1) * 1024], in_=wld6[:]))

            p6_ld, p6_cast, p6_mm, p6_g, p6_comb, p6_st, p6_gst, p6_sil = [], [], [], [], [], [], [], []
            for tt6 in range(NTT):
                s6 = tt6 % 2
                O = s6 * DM
                OY = s6 * 2048

                @block.sync
                def _(s, tt6=tt6, s6=s6, O=O, OY=OY,
                      xfree=(p6_st[tt6 - 2] if tt6 >= 2 else 0),
                      yfree=(p6_cast[tt6 - 2] if tt6 >= 2 else 0),
                      pfree=(p6_sil[tt6 - 2] if tt6 >= 2 else 0)):
                    if tt6 == 0:
                        s.wait_ge(cm.h, cm.n)
                        s.wait_ge(st.h, st.n)
                        s.wait_ge(cc[5].h, 1)
                        s.wait_ge(cc[6].h, 2)
                    if xfree:
                        s.wait_ge(st.h, xfree)
                    if yfree:
                        s.wait_ge(cm.h, yfree)
                    if pfree:
                        s.wait_ge(cm.h, pfree)
                    for dt6 in range(16):
                        ld.dma(s.dma_start(
                            out=yzl[:, OY + dt6 * 128:OY + (dt6 + 1) * 128],
                            in_=yz_o.ap()[dt6 * 128:(dt6 + 1) * 128,
                                          tt6 * 128:(tt6 + 1) * 128]))
                    ld.dma(s.dma_start(out=x1t[:, O:O + DM], in_=x1_d[tt6 * 128:(tt6 + 1) * 128, :]))
                    ld.dma(s.dma_start(out=pcol6[:, s6 * 2:s6 * 2 + 1],
                                       in_=pcl_sl.ap()[tt6 * 128:(tt6 + 1) * 128, :]))
                    ld.dma(s.dma_start(out=pcol6[:, s6 * 2 + 1:s6 * 2 + 2],
                                       in_=gcl_sl.ap()[tt6 * 128:(tt6 + 1) * 128, :]))
                p6_ld.append(ld.n)

                @block.vector
                def _(v, tt6=tt6, OY=OY, ldn=p6_ld[-1],
                      bfree=(p6_mm[tt6 - 2] if tt6 >= 2 else 0)):
                    v.wait_ge(ld.h, ldn)
                    if bfree:
                        v.wait_ge(tp.h, bfree)
                    last = None
                    for dt6 in range(16):
                        last = v.tensor_copy(out=yzb[:, OY + dt6 * 128:OY + (dt6 + 1) * 128],
                                             in_=yzl[:, OY + dt6 * 128:OY + (dt6 + 1) * 128])
                    if "x1late" in dout:
                        last = v.tensor_copy(out=x1t2[:], in_=x1t[:, (OY // 2):(OY // 2) + DM])
                    cm.inc(last)
                p6_cast.append(cm.n)

                @block.scalar
                def _(s, tt6=tt6, s6=s6, ldn=p6_ld[-1],
                      gdone=(p6_gst[tt6 - 1] if tt6 >= 1 else 0)):
                    s.wait_ge(ld.h, ldn)
                    if gdone:
                        s.wait_ge(st.h, gdone)
                    av.inc(s.copy(out=pu32[:, s6:s6 + 1], in_=pcol6[:, s6 * 2:s6 * 2 + 1]))
                p6_pu = av.n

                @block.gpsimd
                def _(g, tt6=tt6, s6=s6, O=O, avn=p6_pu,
                      efree=(p6_g[tt6 - 2] if tt6 >= 2 else 0)):
                    g.wait_ge(av.h, avn)
                    if efree:
                        g.wait_ge(sf.h, efree)
                    if tt6 == 0:
                        g.wait_ge(cc[7].h, 1)
                        g.memset(eor[:], 0.0)
                    st.dma(g.indirect_dma_start(
                        out=eor[:, O:O + DM], out_offset=None,
                        in_=eo_fu.ap(), in_offset=IndirectOffsetOnAxis(pu32[:, s6:s6 + 1], 0),
                        bounds_check=nc._moe_breg, oob_is_err=False))
                p6_gst.append(st.n)

                @block.tensor
                def _(t, tt6=tt6, OY=OY, s6=s6, cmn=p6_cast[-1],
                      sfree=(p6_comb[tt6 - 2] if tt6 >= 2 else 0)):
                    t.wait_ge(cm.h, cmn)
                    if sfree:
                        t.wait_ge(cm.h, sfree)
                    for chn in range(2):
                        last = None
                        for dt6 in range(16):
                            last = t.matmul(ps[:, s6 * 1024 + chn * 512:s6 * 1024 + chn * 512 + 512],
                                            yzb[:, OY + dt6 * 128:OY + (dt6 + 1) * 128],
                                            wob[:, dt6 * 1024 + chn * 512:dt6 * 1024 + (chn + 1) * 512],
                                            start=(dt6 == 0), stop=(dt6 == 15))
                        tp.inc(last)
                p6_mm.append(tp.n)

                @block.vector
                def _(v, tt6=tt6, s6=s6, O=O, tpn=p6_mm[-1]):
                    v.wait_ge(tp.h, tpn)
                    v.tensor_copy(out=s1t[:, O:O + 512], in_=ps[:, s6 * 1024:s6 * 1024 + 512])
                    cm.inc(v.tensor_copy(out=s1t[:, O + 512:O + 1024],
                                         in_=ps[:, s6 * 1024 + 512:s6 * 1024 + 1024]))
                p6_comb.append(cm.n)

                @block.vector
                def _(v, tt6=tt6, s6=s6, O=O, gstn=p6_gst[-1]):
                    v.wait_ge(st.h, gstn)
                    sf.inc(v.tensor_scalar(out=g1t[:, O:O + DM], in0=eor[:, O:O + DM],
                                           scalar1=pcol6[:, s6 * 2 + 1:s6 * 2 + 2],
                                           scalar2=None, op0=ALU.mult))
                p6_g.append(sf.n)

                @block.scalar
                def _(s, tt6=tt6, O=O, sfn=p6_g[-1]):
                    s.wait_ge(sf.h, sfn)
                    av.inc(s.activation(out=sqs[:, 0:512],
                                        in_=g1t[:, O:O + 512], func=AF.Tanh, scale=0.5))
                @block.scalar
                def _(s, tt6=tt6, O=O):
                    av.inc(s.activation(out=t1b[:, 0:512],
                                        in_=g1t[:, O + 512:O + 1024], func=AF.Tanh, scale=0.5))

                @block.vector
                def _(v, tt6=tt6, s6=s6, O=O):
                    v.wait_ge(av.h, av.n)
                    v.wait_ge(cm.h, p6_comb[-1])
                    v.tensor_scalar(out=sqs[:, 0:512], in0=sqs[:, 0:512], scalar1=0.5,
                                    scalar2=0.5, op0=ALU.mult, op1=ALU.add)
                    v.tensor_scalar(out=t1b[:, 0:512], in0=t1b[:, 0:512], scalar1=0.5,
                                    scalar2=0.5, op0=ALU.mult, op1=ALU.add)
                    v.tensor_mul(out=g1t[:, O:O + 512], in0=g1t[:, O:O + 512], in1=sqs[:, 0:512])
                    v.tensor_mul(out=g1t[:, O + 512:O + 1024], in0=g1t[:, O + 512:O + 1024],
                                 in1=t1b[:, 0:512])
                    v.tensor_mul(out=g1t[:, O:O + DM], in0=s1t[:, O:O + DM], in1=g1t[:, O:O + DM])
                    cm.inc(v.tensor_add(out=x1t[:, O:O + DM], in0=x1t[:, O:O + DM],
                                        in1=g1t[:, O:O + DM]))
                p6_sil.append(cm.n)

                @block.gpsimd
                def _(g, tt6=tt6, O=O, cmn=p6_sil[-1]):
                    g.wait_ge(cm.h, cmn)
                    st.dma(g.dma_start(out=x2_d[tt6 * 128:(tt6 + 1) * 128, :], in_=x1t[:, O:O + DM]))
                    if "s1" in dout:
                        st.dma(g.dma_start(out=dout["s1"][tt6 * 128:(tt6 + 1) * 128, :],
                                           in_=s1t[:, O:O + DM]))
                    if "g1" in dout:
                        st.dma(g.dma_start(out=dout["g1"][tt6 * 128:(tt6 + 1) * 128, :],
                                           in_=g1t[:, O:O + DM]))
                    if "x1late" in dout:
                        st.dma(g.dma_start(out=dout["x1late"][tt6 * 128:(tt6 + 1) * 128, :],
                                           in_=x1t2[:]))
                p6_st.append(st.n)

            @block.gpsimd
            def _(g):
                g.wait_ge(st.h, st.n)
            esP6.close()
    return nc



def l1_in_map(inp, core, x1):
    m = dict(
        win_d=inp["L1_attn_W_in"],
        ck_d=np.ascontiguousarray(inp["L1_attn_conv_k"].reshape(4, DI)[:, core * DSL:(core + 1) * DSL]),
        cb_d=np.ascontiguousarray(inp["L1_attn_conv_b"].reshape(16, 128)[2 * core:2 * core + 2]),
        wxp_d=np.ascontiguousarray(inp["L1_attn_W_xproj"][core * DSL:(core + 1) * DSL]),
        wdt_d=np.ascontiguousarray(inp["L1_attn_W_dt"][:, core * DSL:(core + 1) * DSL]),
        bdt_d=np.ascontiguousarray(inp["L1_attn_b_dt"].reshape(16, 128)[2 * core:2 * core + 2]),
        alog_d=np.ascontiguousarray(inp["L1_attn_A_log"][core * DSL:(core + 1) * DSL]),
        dv_d=np.ascontiguousarray(inp["L1_attn_D"].reshape(16, 128)[2 * core:2 * core + 2]),
        wout_d=inp["L1_attn_W_out"],
        wg_d=inp["L1_ffn_Wg"],
        w1s_d=np.ascontiguousarray(inp["L1_ffn_w1"][2 * core:2 * core + 2].reshape(2 * DM, 4096)),
        w2s_d=np.ascontiguousarray(inp["L1_ffn_w2"][2 * core:2 * core + 2].reshape(2 * 4096, DM)),
        ccst_d=np.array([[2048.0 * core, 2048.0 * core, 0.0, 0.0]], np.float32),
    )
    if x1 is not None:
        m["x1"] = np.ascontiguousarray(x1)
    return m


def build_fused(races=True, dbg0=(), dbg1=()):
    nc = bass.Bass("TRN2", target_bir_lowering=False, detect_race_conditions=races)
    es0 = contextlib.ExitStack()
    with es0:
        fz = dict(nc=nc, E=es0.enter_context,
                  bar=Sem(es0.enter_context(nc.semaphore("bar"))))
        build_l0(fuse=fz, dbg=dbg0)
        build_l1(fuse=fz, dbg=dbg1)
    return nc


def kernel(**inputs):
    inp = {k: np.asarray(v) for k, v in inputs.items()}
    nc = build_fused()
    maps = []
    for c in range(NCORES):
        m = l0_in_map(inp, c)
        m.update(l1_in_map(inp, c, None))
        maps.append(m)
    res = run_bass_kernel_spmd(nc, maps, core_ids=list(range(NCORES)))
    out = np.concatenate([np.asarray(res.results[c]["x2"], np.float32)
                          for c in range(NCORES)], 0)
    return out.reshape(2, 8192, DM)


# version 25
# speedup vs baseline: 1.0619x; 1.0101x over previous
import contextlib
import numpy as np
import ml_dtypes
import concourse.bass as bass
import concourse.mybir as mybir
from concourse.bass import IndirectOffsetOnAxis
from concourse.bass_utils import run_bass_kernel_spmd
from concourse.aot_env import allow_ici_all_to_all
from concourse.hw_specs import get_activation_tables

F32 = mybir.dt.float32
BF16 = mybir.dt.bfloat16
I32 = mybir.dt.int32
U32 = mybir.dt.uint32
AF = mybir.ActivationFunctionType
ALU = mybir.AluOpType
AX = mybir.AxisListType

NCORES = 8
TPC = 2048
DM = 1024
NCT = DM // 128
TCH = 512
NCH = TPC // TCH
NTT = TPC // 128
GROUPS4 = [[0, 1, 2, 3], [4, 5, 6, 7]]
GROUPS8 = [[0, 1, 2, 3, 4, 5, 6, 7]]
BFNP = np.dtype(ml_dtypes.bfloat16)


class Sem:
    __slots__ = ("h", "n")

    def __init__(self, h):
        self.h = h
        self.n = 0

    def inc(self, instr, k=1):
        instr.then_inc(self.h, k)
        self.n += k
        return instr

    def dma(self, instr):
        return self.inc(instr, 16)

    def inc16(self, instr):
        return self.inc(instr, 16)


def load_act_set(nc, eng, set_name):
    tabs = list(get_activation_tables(nc.m.arch).keys())
    idx = tabs.index(set_name)
    return eng.add_instruction(mybir.InstLoadActFuncSet(
        name=nc.get_next_instruction_name(), act_func_set_id=idx, ins=[], outs=[]))



def _finish(nc, es, *stacks):
    for st_ in stacks:
        if st_ is not None:
            st_.close()
    es.close()
    return nc

def build_l0(dbg=(), stop_after="all", races=True, fuse=None):
    if fuse is None:
        nc = bass.Bass("TRN2", target_bir_lowering=False, detect_race_conditions=races)
    else:
        nc = fuse["nc"]

    x_d = nc.dram_tensor("x", [TPC, DM], F32, kind="ExternalInput").ap()
    wq_d = nc.dram_tensor("wq_d", [512, DM], F32, kind="ExternalInput").ap()
    bq_d = nc.dram_tensor("bq_d", [4, 128], F32, kind="ExternalInput").ap()
    wqk_d = nc.dram_tensor("wqk_d", [DM, 512], F32, kind="ExternalInput").ap()
    bqk_d = nc.dram_tensor("bqk_d", [8, 128], F32, kind="ExternalInput").ap()
    wkv_d = nc.dram_tensor("wkv_d", [64, DM], F32, kind="ExternalInput").ap()
    bkv_d = nc.dram_tensor("bkv_d", [1, 64], F32, kind="ExternalInput").ap()
    wvu_d = nc.dram_tensor("wvu_d", [DM, 64], F32, kind="ExternalInput").ap()
    bvu_d = nc.dram_tensor("bvu_d", [8, 128], F32, kind="ExternalInput").ap()
    wpr_d = nc.dram_tensor("wpr_d", [DM, DM], F32, kind="ExternalInput").ap()
    bpr_d = nc.dram_tensor("bpr_d", [8, 128], F32, kind="ExternalInput").ap()
    gam_d = nc.dram_tensor("gam_d", [8, 128], F32, kind="ExternalInput").ap()
    bet_d = nc.dram_tensor("bet_d", [8, 128], F32, kind="ExternalInput").ap()
    f1q_d = nc.dram_tensor("f1q_d", [DM, 4096], I32, kind="ExternalInput").ap()
    f1s_d = nc.dram_tensor("f1s_d", [8, 4096], F32, kind="ExternalInput").ap()
    f1b_d = nc.dram_tensor("f1b_d", [32, 128], F32, kind="ExternalInput").ap()
    f2q_d = nc.dram_tensor("f2q_d", [4096, DM], I32, kind="ExternalInput").ap()
    f2s_d = nc.dram_tensor("f2s_d", [32, DM], F32, kind="ExternalInput").ap()
    f2b_d = nc.dram_tensor("f2b_d", [8, 128], F32, kind="ExternalInput").ap()

    if fuse is None:
        x1_d = nc.dram_tensor("x1", [TPC, DM], F32, kind="ExternalOutput").ap()
    else:
        x1_d = nc.dram_tensor("x1", [TPC, DM], F32).ap()
        fuse["x1_d"] = x1_d
    dout = {}
    for name, shape, dt in [
        ("xn", [128, NCT * TPC], F32), ("h", [128, NCT * TPC], F32),
        ("g0", [TPC, DM], F32), ("q", [128, NCT * TPC], F32),
        ("ckv", [8192, 64], F32), ("s0", [TPC, DM], F32),
        ("beff", [128, DM], F32), ("mb", [128, 8 * 1024], F32),
        ("yac", [128, 4 * DM], F32), ("ckv2x", [128, 8192], F32),
    ]:
        if name in dbg:
            dout[name] = nc.dram_tensor("dbg_" + name, shape, dt, kind="ExternalOutput").ap()

    ckv_bo = nc.dram_tensor("ckv_bo", [TPC, 64], F32)
    ckv_fu = nc.dram_tensor("ckv_fu", [8192, 64], F32)
    g0_dr = nc.dram_tensor("g0_dr", [TPC, DM], F32)
    y1_dr = nc.dram_tensor("y1_dr", [4096, TPC], BF16)
    w1b_dr = nc.dram_tensor("w1b_dr", [DM, 4096], BF16)
    w2b_dr = nc.dram_tensor("w2b_dr", [4096, DM], BF16)

    es = contextlib.ExitStack()
    E = es.enter_context
    with es:
        ident = E(nc.sbuf_tensor("ident", [128, 128], F32))
        ones_c = E(nc.sbuf_tensor("ones_c", [128, 1], F32))
        ones_r = E(nc.sbuf_tensor("ones_r", [1, 128], F32))
        negK = E(nc.sbuf_tensor("negK", [128, 1], F32))
        qfm = E(nc.sbuf_tensor("qfm", [128, NCT * TPC], BF16))
        rows = E(nc.sbuf_tensor("rows", [1, 4 * TCH], F32))
        sqs = E(nc.sbuf_tensor("sqs", [128, TCH], F32))
        prm = E(nc.sbuf_tensor("prm", [128, 96], F32))
        ps = E(nc.psum_tensor("ps", [128, 4096], F32))

        PRM_GAM, PRM_BET, PRM_F1B, PRM_F2B, PRM_BQ, PRM_BQK, PRM_BKV, PRM_BPR, PRM_BVU = \
            0, 8, 16, 48, 56, 60, 68, 69, 77

        SE = E if fuse is None else (lambda s: fuse["E"](s))
        ld = Sem(SE(nc.semaphore("a_ld")))
        st = Sem(SE(nc.semaphore("a_st")))
        cm = Sem(SE(nc.semaphore("a_cm")))
        tp = Sem(SE(nc.semaphore("a_tp")))
        av = Sem(SE(nc.semaphore("a_av")))
        gm = Sem(SE(nc.semaphore("a_gm")))
        sc_s = Sem(SE(nc.semaphore("a_sc_s")))
        ex_s = Sem(SE(nc.semaphore("a_ex_s")))
        pv_s = Sem(SE(nc.semaphore("a_pv_s")))
        cc1 = Sem(SE(nc.semaphore("a_cc1")))

        block = E(nc.Block())

        @block.gpsimd
        def _(g):
            g.memset(ident[:], 0.0)
            g.affine_select(out=ident[:], in_=ident[:], compare_op=ALU.not_equal,
                            fill=1.0, base=0, pattern=[[-1, 128]], channel_multiplier=1)
            g.memset(ones_c[:], 1.0)
            g.memset(ones_r[:], 1.0)
            gm.inc(g.memset(negK[:], -20.0))

        @block.sync
        def _(s):
            for src, col, n in [(gam_d, PRM_GAM, 8), (bet_d, PRM_BET, 8),
                                (f1b_d, PRM_F1B, 32), (f2b_d, PRM_F2B, 8),
                                (bq_d, PRM_BQ, 4), (bqk_d, PRM_BQK, 8),
                                (bpr_d, PRM_BPR, 8), (bvu_d, PRM_BVU, 8)]:
                for r in range(n):
                    ld.dma(s.dma_start(out=prm[:, col + r:col + r + 1], in_=src[r:r + 1, :]))
            ld.dma(s.dma_start(out=prm[0:64, PRM_BKV:PRM_BKV + 1], in_=bkv_d[0:1, :]))
        p0_ld = ld.n

        esA = contextlib.ExitStack()
        with esA:
            EA = esA.enter_context
            xfm = EA(nc.sbuf_tensor("xfm", [128, NCT * TPC], F32))
            hb = EA(nc.sbuf_tensor("hb", [128, NCT * TPC], BF16))
            xt = EA(nc.sbuf_tensor("xt", [128, DM], F32))
            esA1 = contextlib.ExitStack()
            EA1 = esA1.enter_context
            xnb = EA1(nc.sbuf_tensor("xnb", [128, NCT * TCH], BF16))
            cqb = EA1(nc.sbuf_tensor("cqb", [128, 4 * TCH], BF16))
            wql = EA1(nc.sbuf_tensor("wql", [128, 32 * 128], BF16))
            wkl = EA1(nc.sbuf_tensor("wkl", [128, 32 * 128], BF16))
            wvl = EA1(nc.sbuf_tensor("wvl", [128, 8 * 64], BF16))
            ckvf = EA1(nc.sbuf_tensor("ckvf", [64, TPC], F32))
            ckvt = EA1(nc.sbuf_tensor("ckvt", [128, 4 * 64], F32))

            for tt in range(NTT):
                @block.sync
                def _(s, tt=tt):
                    if tt == 0:
                        s.wait_ge(gm.h, 1)
                        s.wait_ge(ld.h, p0_ld)
                    else:
                        s.wait_ge(cm.h, cm.n)
                    ld.dma(s.dma_start(out=xt[:], in_=x_d[tt * 128:(tt + 1) * 128, :]))

                @block.tensor
                def _(t, tt=tt):
                    t.wait_ge(ld.h, ld.n)
                    for ct in range(NCT):
                        tp.inc(t.transpose(ps[:, ct * 512:ct * 512 + 128],
                                           xt[:, ct * 128:(ct + 1) * 128], ident[:]))

                @block.vector
                def _(v, tt=tt):
                    v.wait_ge(tp.h, tp.n)
                    last = None
                    for ct in range(NCT):
                        last = v.tensor_copy(
                            out=xfm[:, ct * TPC + tt * 128:ct * TPC + (tt + 1) * 128],
                            in_=ps[:, ct * 512:ct * 512 + 128])
                    cm.inc(last)

            @block.scalar
            def _(s):
                load_act_set(nc, s, "natural_log_exp_and_others")
                av.inc(s.nop())

            r_mu = rows[0:1, 0:TCH]
            r_ms = rows[0:1, TCH:2 * TCH]
            r_va = rows[0:1, 2 * TCH:3 * TCH]
            r_rs = rows[0:1, 3 * TCH:4 * TCH]
            ps_sum = ps[0:1, 1024:1024 + TCH]
            ps_sq = ps[0:1, 1536:1536 + TCH]
            ps_mu = ps[:, 2048:2048 + TCH]
            ps_rs = ps[:, 2560:2560 + TCH]

            def emit_ln(affine):
                for tc in range(NCH):
                    c0 = tc * TCH

                    for ct in range(NCT):
                        @block.scalar
                        def _(s, ct=ct, c0=c0):
                            s.wait_ge(cm.h, cm.n)
                            if ct > 0:
                                s.wait_ge(tp.h, tp.n)
                            av.inc(s.activation(out=sqs[:],
                                                in_=xfm[:, ct * TPC + c0:ct * TPC + c0 + TCH],
                                                func=AF.Square))

                        @block.tensor
                        def _(t, ct=ct, c0=c0):
                            t.wait_ge(av.h, av.n)
                            t.matmul(ps_sum, ones_c[:],
                                     xfm[:, ct * TPC + c0:ct * TPC + c0 + TCH],
                                     start=(ct == 0), stop=(ct == NCT - 1))
                            tp.inc(t.matmul(ps_sq, ones_c[:], sqs[:],
                                            start=(ct == 0), stop=(ct == NCT - 1)))

                    @block.vector
                    def _(v, c0=c0):
                        v.wait_ge(tp.h, tp.n)
                        v.tensor_scalar(out=r_mu, in0=ps_sum, scalar1=1.0 / DM,
                                        scalar2=None, op0=ALU.mult)
                        v.tensor_scalar(out=r_ms, in0=ps_sq, scalar1=1.0 / DM,
                                        scalar2=None, op0=ALU.mult)
                        v.tensor_mul(out=r_va, in0=r_mu, in1=r_mu)
                        v.tensor_sub(out=r_va, in0=r_ms, in1=r_va)
                        cm.inc(v.tensor_scalar_add(out=r_va, in0=r_va, scalar1=1e-5))

                    @block.scalar
                    def _(s):
                        s.wait_ge(cm.h, cm.n)
                        s.activation(out=r_rs, in_=r_va, func=AF.Ln)
                        av.inc(s.activation(out=r_rs, in_=r_rs, func=AF.Exp, scale=-0.5))

                    @block.tensor
                    def _(t):
                        t.wait_ge(av.h, av.n)
                        t.matmul(ps_mu, ones_r[:], r_mu, start=True, stop=True)
                        tp.inc(t.matmul(ps_rs, ones_r[:], r_rs, start=True, stop=True))

                    @block.vector
                    def _(v, c0=c0):
                        v.wait_ge(tp.h, tp.n)
                        last = None
                        for ct in range(NCT):
                            sl = xfm[:, ct * TPC + c0:ct * TPC + c0 + TCH]
                            if not affine:
                                v.tensor_sub(out=sl, in0=sl, in1=ps_mu)
                                last = v.tensor_mul(out=sl, in0=sl, in1=ps_rs)
                            else:
                                v.tensor_sub(out=sqs[:], in0=sl, in1=ps_mu)
                                v.tensor_mul(out=sqs[:], in0=sqs[:], in1=ps_rs)
                                v.tensor_scalar(out=sqs[:], in0=sqs[:],
                                                scalar1=prm[:, PRM_GAM + ct:PRM_GAM + ct + 1],
                                                scalar2=None, op0=ALU.mult)
                                last = v.tensor_scalar(
                                    out=hb[:, ct * TPC + c0:ct * TPC + c0 + TCH],
                                    in0=sqs[:],
                                    scalar1=prm[:, PRM_BET + ct:PRM_BET + ct + 1],
                                    scalar2=None, op0=ALU.add)
                        cm.inc(last)

            emit_ln(False)
            _stop_ln0 = (stop_after == "ln0")

            if "xn" in dout:
                @block.gpsimd
                def _(g):
                    g.wait_ge(cm.h, cm.n)
                    st.dma(g.dma_start(out=dout["xn"], in_=xfm[:]))

            if stop_after == "ln0":
                @block.gpsimd
                def _(g):
                    g.wait_ge(st.h, st.n)
                return _finish(nc, es, esA1, esA)
            emit_ln(True)

            if "h" in dout:
                for ct in range(NCT):
                    for tc in range(NCH):
                        @block.vector
                        def _(v, ct=ct, tc=tc):
                            v.wait_ge(st.h, st.n)
                            cm.inc(v.tensor_copy(out=sqs[:],
                                                 in_=hb[:, ct * TPC + tc * TCH:ct * TPC + tc * TCH + TCH]))

                        @block.gpsimd
                        def _(g, ct=ct, tc=tc):
                            g.wait_ge(cm.h, cm.n)
                            st.dma(g.dma_start(
                                out=dout["h"][:, ct * TPC + tc * TCH:ct * TPC + tc * TCH + TCH],
                                in_=sqs[:]))

            if stop_after == "ln":
                @block.gpsimd
                def _(g):
                    g.wait_ge(st.h, st.n)
                return _finish(nc, es, esA1, esA)
            def emit_wT(src, nrows, ncols, dst, dst_block):
                nrt = (nrows + 127) // 128
                for i in range(nrt):
                    rpart = min(128, nrows - i * 128)
                    @block.sync
                    def _(s, i=i, rpart=rpart):
                        s.wait_ge(cm.h, cm.n)
                        ld.dma(s.dma_start(out=xt[0:rpart, 0:ncols],
                                           in_=src[i * 128:i * 128 + rpart, :]))

                    @block.tensor
                    def _(t, i=i, rpart=rpart):
                        t.wait_ge(ld.h, ld.n)
                        for ct in range(ncols // 128):
                            tp.inc(t.transpose(ps[:, (ct % 8) * 512:(ct % 8) * 512 + rpart],
                                               xt[0:rpart, ct * 128:(ct + 1) * 128],
                                               ident[0:rpart, 0:rpart]))

                    @block.vector
                    def _(v, i=i, rpart=rpart):
                        v.wait_ge(tp.h, tp.n)
                        last = None
                        for ct in range(ncols // 128):
                            off = dst_block(i, ct)
                            last = v.tensor_copy(out=dst[:, off:off + rpart],
                                                 in_=ps[:, (ct % 8) * 512:(ct % 8) * 512 + rpart])
                        cm.inc(last)

            emit_wT(wq_d, 512, DM, wql, lambda i, ct: (ct * 4 + i) * 128)
            emit_wT(wqk_d, DM, 512, wkl, lambda i, ct: (ct * 8 + i) * 128)
            emit_wT(wkv_d, 64, DM, wvl, lambda i, ct: ct * 64)

            for tc in range(NCH):
                c0 = tc * TCH

                @block.vector
                def _(v, c0=c0):
                    v.wait_ge(cm.h, cm.n)
                    last = None
                    for ct in range(NCT):
                        last = v.tensor_copy(out=xnb[:, ct * TCH:(ct + 1) * TCH],
                                             in_=xfm[:, ct * TPC + c0:ct * TPC + c0 + TCH])
                    cm.inc(last)

                @block.tensor
                def _(t, c0=c0):
                    t.wait_ge(cm.h, cm.n)
                    last = None
                    for qlt in range(4):
                        for ct in range(NCT):
                            last = t.matmul(ps[:, qlt * 512:qlt * 512 + TCH],
                                            wql[:, (ct * 4 + qlt) * 128:(ct * 4 + qlt + 1) * 128],
                                            xnb[:, ct * TCH:(ct + 1) * TCH],
                                            start=(ct == 0), stop=(ct == NCT - 1))
                    tp.inc(last)

                @block.vector
                def _(v):
                    v.wait_ge(tp.h, tp.n)
                    last = None
                    for qlt in range(4):
                        last = v.tensor_scalar(out=cqb[:, qlt * TCH:(qlt + 1) * TCH],
                                               in0=ps[:, qlt * 512:qlt * 512 + TCH],
                                               scalar1=prm[:, PRM_BQ + qlt:PRM_BQ + qlt + 1],
                                               scalar2=None, op0=ALU.add)
                    cm.inc(last)

                @block.tensor
                def _(t, c0=c0):
                    t.wait_ge(cm.h, cm.n)
                    last = None
                    for ft in range(NCT):
                        for qlt in range(4):
                            last = t.matmul(ps[:, ft * 512:ft * 512 + TCH],
                                            wkl[:, (qlt * 8 + ft) * 128:(qlt * 8 + ft + 1) * 128],
                                            cqb[:, qlt * TCH:(qlt + 1) * TCH],
                                            start=(qlt == 0), stop=(qlt == 3))
                    tp.inc(last)

                @block.vector
                def _(v, c0=c0):
                    v.wait_ge(tp.h, tp.n)
                    last = None
                    for ft in range(NCT):
                        last = v.tensor_scalar(out=qfm[:, ft * TPC + c0:ft * TPC + c0 + TCH],
                                               in0=ps[:, ft * 512:ft * 512 + TCH],
                                               scalar1=prm[:, PRM_BQK + ft:PRM_BQK + ft + 1],
                                               scalar2=None, op0=ALU.add)
                    cm.inc(last)

                @block.tensor
                def _(t, c0=c0):
                    t.wait_ge(cm.h, cm.n)
                    last = None
                    for ct in range(NCT):
                        last = t.matmul(ps[0:64, 0:TCH],
                                        wvl[:, ct * 64:(ct + 1) * 64],
                                        xnb[:, ct * TCH:(ct + 1) * TCH],
                                        start=(ct == 0), stop=(ct == NCT - 1))
                    tp.inc(last)

                @block.vector
                def _(v, c0=c0):
                    v.wait_ge(tp.h, tp.n)
                    cm.inc(v.tensor_scalar(out=ckvf[:, c0:c0 + TCH],
                                           in0=ps[0:64, 0:TCH],
                                           scalar1=prm[0:64, PRM_BKV:PRM_BKV + 1],
                                           scalar2=None, op0=ALU.add))

                @block.tensor
                def _(t, tc=tc, c0=c0):
                    t.wait_ge(cm.h, cm.n)
                    last = None
                    for j in range(4):
                        last = t.transpose(ps[:, 3072 + j * 128:3072 + j * 128 + 64],
                                           ckvf[0:64, c0 + j * 128:c0 + (j + 1) * 128],
                                           ident[0:64, 0:64])
                    tp.inc(last)

                @block.vector
                def _(v, tc=tc):
                    v.wait_ge(tp.h, tp.n)
                    v.wait_ge(st.h, st.n)
                    last = None
                    for j in range(4):
                        last = v.tensor_copy(out=ckvt[:, j * 64:(j + 1) * 64],
                                             in_=ps[:, 3072 + j * 128:3072 + j * 128 + 64])
                    cm.inc(last)

                @block.gpsimd
                def _(g, tc=tc):
                    g.wait_ge(cm.h, cm.n)
                    for j in range(4):
                        st.dma(g.dma_start(
                            out=ckv_bo.ap()[(tc * 4 + j) * 128:(tc * 4 + j + 1) * 128, :],
                            in_=ckvt[:, j * 64:(j + 1) * 64]))

            @block.gpsimd
            def _(g):
                g.wait_ge(st.h, st.n)
                cc1.inc(g.collective_compute(
                    "AllGather", ALU.bypass, replica_groups=GROUPS4,
                    ins=[ckv_bo.ap().opt()], outs=[ckv_fu.ap().opt()]))

            if "q" in dout:
                for ct in range(NCT):
                    for tc in range(NCH):
                        @block.vector
                        def _(v, ct=ct, tc=tc):
                            v.wait_ge(st.h, st.n)
                            cm.inc(v.tensor_copy(out=sqs[:],
                                                 in_=qfm[:, ct * TPC + tc * TCH:ct * TPC + tc * TCH + TCH]))

                        @block.gpsimd
                        def _(g, ct=ct, tc=tc):
                            g.wait_ge(cm.h, cm.n)
                            st.dma(g.dma_start(
                                out=dout["q"][:, ct * TPC + tc * TCH:ct * TPC + tc * TCH + TCH],
                                in_=sqs[:]))
            if "ckv" in dout:
                @block.gpsimd
                def _(g):
                    g.wait_ge(cc1.h, 1)
                    st.dma(g.dma_start(out=dout["ckv"], in_=ckv_fu.ap()))

            if stop_after == "qckv":
                @block.gpsimd
                def _(g):
                    g.wait_ge(cc1.h, 1)
                    g.wait_ge(st.h, st.n)
                return _finish(nc, es, esA1, esA)
            esA1.close()
            esA2 = contextlib.ExitStack()
            EA2 = esA2.enter_context
            qraw = EA2(nc.sbuf_tensor("qraw", [128, 2048], I32))
            qflt = EA2(nc.sbuf_tensor("qflt", [128, 2048], F32))
            wbf = EA2(nc.sbuf_tensor("wbf", [128, 2048], BF16))
            wstr = EA2(nc.sbuf_tensor("wstr", [128, 32 * 128], BF16))
            ysb = EA2(nc.sbuf_tensor("ysb", [128, 8 * TCH], BF16))
            vsc = EA2(nc.sbuf_tensor("vsc", [128, TCH], F32))
            tsc = EA2(nc.sbuf_tensor("tsc", [128, TCH], F32))
            gsc = EA2(nc.sbuf_tensor("gsc", [128, TCH], F32))
            ybf = EA2(nc.sbuf_tensor("ybf", [128, 2 * TCH], BF16))
            gtk = EA2(nc.sbuf_tensor("gtk", [128, 4 * DM], F32))

            @block.scalar
            def _(s):
                s.wait_ge(cm.h, cm.n)
                load_act_set(nc, s, "exp_and_others")
                av.inc(s.nop())

            def emit_dequant(q_src, s_src, w_dst, nkt, ncol, hist):
                W = min(2048, ncol)
                qld, bld, cast_av, stt_cm, wst_st = hist
                for kt in range(nkt):
                    for cc_ in range(ncol // W):
                        i = len(qld)

                        @block.sync
                        def _(s, kt=kt, cc_=cc_,
                              qfree=(cast_av[i - 1] if i >= 1 else 0),
                              rfree=(bld[i - 1] if i >= 1 else 0)):
                            if qfree:
                                s.wait_ge(av.h, qfree)
                                s.wait_ge(tp.h, rfree)
                            else:
                                s.wait_ge(av.h, av.n)
                                s.wait_ge(st.h, st.n)
                            ld.dma(s.dma_start(out=qraw[:, 0:W],
                                               in_=q_src[kt * 128:(kt + 1) * 128,
                                                         cc_ * W:(cc_ + 1) * W]))
                            ld.dma(s.dma_start(out=rows[0:1, 0:W],
                                               in_=s_src[kt:kt + 1, cc_ * W:(cc_ + 1) * W]))
                        qld.append(ld.n)

                        @block.scalar
                        def _(s, ldn=qld[-1],
                              qffree=(stt_cm[i - 1] if i >= 1 else 0)):
                            s.wait_ge(ld.h, ldn)
                            if qffree:
                                s.wait_ge(cm.h, qffree)
                            av.inc(s.copy(out=qflt[:, 0:W], in_=qraw[:, 0:W]))
                        cast_av.append(av.n)

                        @block.tensor
                        def _(t, i=i, ldn=qld[-1],
                              pfree=(stt_cm[i - 2] if i >= 2 else 0)):
                            t.wait_ge(ld.h, ldn)
                            if pfree:
                                t.wait_ge(cm.h, pfree)
                            else:
                                t.wait_ge(cm.h, cm.n)
                            last = None
                            for j in range(W // 512):
                                last = t.matmul(
                                    ps[:, (i % 2) * 2048 + j * 512:(i % 2) * 2048 + j * 512 + 512],
                                    ones_r[:], rows[0:1, j * 512:(j + 1) * 512],
                                    start=True, stop=True)
                            tp.inc(last)
                        bld.append(tp.n)

                        @block.vector
                        def _(v, i=i, avn=cast_av[-1], tpn=bld[-1],
                              wfree=(wst_st[i - 1] if i >= 1 else 0)):
                            v.wait_ge(av.h, avn)
                            v.wait_ge(tp.h, tpn)
                            if wfree:
                                v.wait_ge(st.h, wfree)
                            last = None
                            for j in range(W // 512):
                                last = v.scalar_tensor_tensor(
                                    out=wbf[:, j * 512:(j + 1) * 512],
                                    in0=qflt[:, j * 512:(j + 1) * 512], scalar=-128.0,
                                    in1=ps[:, (i % 2) * 2048 + j * 512:(i % 2) * 2048 + j * 512 + 512],
                                    op0=ALU.add, op1=ALU.mult)
                            cm.inc(last)
                        stt_cm.append(cm.n)

                        @block.gpsimd
                        def _(g, kt=kt, cc_=cc_, cmn=stt_cm[-1]):
                            g.wait_ge(cm.h, cmn)
                            st.dma(g.dma_start(
                                out=w_dst.ap()[kt * 128:(kt + 1) * 128,
                                               cc_ * W:(cc_ + 1) * W],
                                in_=wbf[:, 0:W]))
                        wst_st.append(st.n)

            dq_hist = ([], [], [], [], [])
            emit_dequant(f1q_d, f1s_d, w1b_dr, 8, 4096, dq_hist)
            emit_dequant(f2q_d, f2s_d, w2b_dr, 32, 1024, dq_hist)

            y1_hist = []
            f1_ld, f1_tp, f1_cm, f1_av, f1_mul = [], [], [], [], []
            ysb32 = ysb[:].bitcast(F32)
            for ft in range(32):
                @block.sync
                def _(s, ft=ft,
                      wfree=(f1_tp[(ft - 2) * NCH + NCH - 1] if ft >= 2 else 0)):
                    if ft == 0:
                        s.wait_ge(st.h, st.n)
                        s.wait_ge(tp.h, tp.n)
                    elif wfree:
                        s.wait_ge(tp.h, wfree)
                    for kt in range(8):
                        ld.dma(s.dma_start(
                            out=wstr[:, (ft % 2) * 1024 + kt * 128:(ft % 2) * 1024 + (kt + 1) * 128],
                            in_=w1b_dr.ap()[kt * 128:(kt + 1) * 128, ft * 128:(ft + 1) * 128]))
                f1_wld = ld.n

                for tc in range(NCH):
                    it = ft * NCH + tc

                    @block.tensor
                    def _(t, ft=ft, tc=tc, it=it, ldn=f1_wld,
                          bfree=(f1_cm[it - 2] if it >= 2 else 0)):
                        t.wait_ge(ld.h, ldn)
                        if bfree:
                            t.wait_ge(cm.h, bfree)
                        else:
                            t.wait_ge(cm.h, cm.n)
                        last = None
                        for kt in range(8):
                            last = t.matmul(ps[:, (it % 2) * 512:(it % 2) * 512 + TCH],
                                            wstr[:, (ft % 2) * 1024 + kt * 128:
                                                 (ft % 2) * 1024 + (kt + 1) * 128],
                                            hb[:, kt * TPC + tc * TCH:kt * TPC + tc * TCH + TCH],
                                            start=(kt == 0), stop=(kt == 7))
                        tp.inc(last)
                    f1_tp.append(tp.n)

                    vb = 'vsc' if it % 2 == 0 else 'y0'
                    @block.vector
                    def _(v, ft=ft, it=it, tpn=f1_tp[-1]):
                        v.wait_ge(tp.h, tpn)
                        dst = vsc[:] if it % 2 == 0 else ysb32[:, 0:TCH]
                        cm.inc(v.tensor_scalar(out=dst,
                                               in0=ps[:, (it % 2) * 512:(it % 2) * 512 + TCH],
                                               scalar1=prm[:, PRM_F1B + ft:PRM_F1B + ft + 1],
                                               scalar2=None, op0=ALU.add))
                    f1_cm.append(cm.n)

                    @block.scalar
                    def _(s, it=it, cmn=f1_cm[-1]):
                        s.wait_ge(cm.h, cmn)
                        src = vsc[:] if it % 2 == 0 else ysb32[:, 0:TCH]
                        dst = tsc[:] if it % 2 == 0 else ysb32[:, TCH:2 * TCH]
                        av.inc(s.activation(out=dst, in_=src, func=AF.Tanh, scale=0.5))
                    f1_av.append(av.n)

                    @block.vector
                    def _(v, it=it, avn=f1_av[-1]):
                        v.wait_ge(av.h, avn)
                        if len(y1_hist) >= 2:
                            v.wait_ge(st.h, y1_hist[-2])
                        tt_ = tsc[:] if it % 2 == 0 else ysb32[:, TCH:2 * TCH]
                        vv_ = vsc[:] if it % 2 == 0 else ysb32[:, 0:TCH]
                        v.tensor_scalar(out=tt_, in0=tt_, scalar1=0.5, scalar2=0.5,
                                        op0=ALU.mult, op1=ALU.add)
                        cm.inc(v.tensor_mul(out=ybf[:, (it % 2) * TCH:(it % 2 + 1) * TCH],
                                            in0=vv_, in1=tt_))
                    f1_mul.append(cm.n)

                    @block.gpsimd
                    def _(g, ft=ft, tc=tc, it=it, cmn=f1_mul[-1]):
                        g.wait_ge(cm.h, cmn)
                        st.dma(g.dma_start(
                            out=y1_dr.ap()[ft * 128:(ft + 1) * 128, tc * TCH:(tc + 1) * TCH],
                            in_=ybf[:, (it % 2) * TCH:(it % 2 + 1) * TCH]))
                    y1_hist.append(st.n)

            f2_ld, f2_tp = [], []
            for tc in range(NCH):
                for half in range(4):
                    @block.sync
                    def _(s, tc=tc, half=half):
                        s.wait_ge(st.h, st.n)
                        s.wait_ge(tp.h, tp.n)
                        for kt in range(8):
                            ld.dma(s.dma_start(
                                out=ysb[:, kt * TCH:(kt + 1) * TCH],
                                in_=y1_dr.ap()[(half * 8 + kt) * 128:(half * 8 + kt + 1) * 128,
                                               tc * TCH:(tc + 1) * TCH]))

                    for ot in range(8):
                        oid = (tc * 4 + half) * 8 + ot

                        @block.sync
                        def _(s, tc=tc, ot=ot, half=half, oid=oid,
                              wfree=(f2_tp[oid - 2] if oid >= 2 else 0)):
                            s.wait_ge(ld.h, ld.n)
                            if wfree:
                                s.wait_ge(tp.h, wfree)
                            else:
                                s.wait_ge(tp.h, tp.n)
                            for kt in range(8):
                                ld.dma(s.dma_start(
                                    out=wstr[:, 2048 + (oid % 2) * 1024 + kt * 128:
                                             2048 + (oid % 2) * 1024 + (kt + 1) * 128],
                                    in_=w2b_dr.ap()[(half * 8 + kt) * 128:(half * 8 + kt + 1) * 128,
                                                    ot * 128:(ot + 1) * 128]))
                        f2_ld.append(ld.n)

                        @block.tensor
                        def _(t, tc=tc, ot=ot, half=half, oid=oid, ldn=f2_ld[-1]):
                            t.wait_ge(ld.h, ldn)
                            if tc > 0 and half == 0 and ot == 0:
                                t.wait_ge(cm.h, cm.n)
                            last = None
                            for kt in range(8):
                                last = t.matmul(ps[:, ot * 512:ot * 512 + TCH],
                                                wstr[:, 2048 + (oid % 2) * 1024 + kt * 128:
                                                     2048 + (oid % 2) * 1024 + (kt + 1) * 128],
                                                ysb[:, kt * TCH:(kt + 1) * TCH],
                                                start=(half == 0 and kt == 0),
                                                stop=(half == 3 and kt == 7))
                            tp.inc(last)
                        f2_tp.append(tp.n)

                for ot in range(8):
                    @block.vector
                    def _(v, tc=tc, ot=ot):
                        v.wait_ge(tp.h, tp.n)
                        v.tensor_scalar(out=vsc[:], in0=ps[:, ot * 512:ot * 512 + TCH],
                                        scalar1=prm[:, PRM_F2B + ot:PRM_F2B + ot + 1],
                                        scalar2=None, op0=ALU.add)
                        cm.inc(v.scalar_tensor_tensor(
                            out=vsc[:], in0=vsc[:], scalar=0.5,
                            in1=xfm[:, ot * TPC + tc * TCH:ot * TPC + tc * TCH + TCH],
                            op0=ALU.mult, op1=ALU.add))

                    @block.scalar
                    def _(s, tc=tc, ot=ot):
                        s.wait_ge(cm.h, cm.n)
                        av.inc(s.activation(out=tsc[:], in_=vsc[:], func=AF.Tanh, scale=0.5))

                    @block.vector
                    def _(v, tc=tc, ot=ot):
                        v.wait_ge(av.h, av.n)
                        v.tensor_scalar(out=tsc[:], in0=tsc[:], scalar1=0.5, scalar2=0.5,
                                        op0=ALU.mult, op1=ALU.add)
                        cm.inc(v.tensor_mul(out=gsc[:], in0=vsc[:], in1=tsc[:]))

                    @block.tensor
                    def _(t, tc=tc, ot=ot):
                        t.wait_ge(cm.h, cm.n)
                        last = None
                        for j in range(4):
                            last = t.transpose(ps[:, ot * 512 + j * 128:ot * 512 + (j + 1) * 128],
                                               gsc[:, j * 128:(j + 1) * 128], ident[:])
                        tp.inc(last)

                    @block.vector
                    def _(v, tc=tc, ot=ot):
                        v.wait_ge(tp.h, tp.n)
                        if ot == 0:
                            v.wait_ge(st.h, st.n)
                        last = None
                        for j in range(4):
                            last = v.tensor_copy(
                                out=gtk[:, j * DM + ot * 128:j * DM + (ot + 1) * 128],
                                in_=ps[:, ot * 512 + j * 128:ot * 512 + (j + 1) * 128])
                        cm.inc(last)

                @block.gpsimd
                def _(g, tc=tc):
                    g.wait_ge(cm.h, cm.n)
                    last = None
                    for j in range(4):
                        last = st.dma(g.dma_start(
                            out=g0_dr.ap()[tc * TCH + j * 128:tc * TCH + (j + 1) * 128, :],
                            in_=gtk[:, j * DM:(j + 1) * DM]))
                        if "g0" in dout:
                            last = st.dma(g.dma_start(
                                out=dout["g0"][tc * TCH + j * 128:tc * TCH + (j + 1) * 128, :],
                                in_=gtk[:, j * DM:(j + 1) * DM]))

            @block.tensor
            def _(t):
                t.wait_ge(cm.h, cm.n)
                tp.inc(t.nop())

            @block.vector
            def _(v):
                v.wait_ge(st.h, st.n)
                v.wait_ge(tp.h, tp.n)
                v.wait_ge(av.h, av.n)
                cm.inc(v.tensor_copy(out=sqs[:, 0:1], in_=ones_c[:]))
            esA2.close()
            if stop_after == "ffn":
                @block.gpsimd
                def _(g):
                    g.wait_ge(cc1.h, 1)
                    g.wait_ge(st.h, st.n)
                return _finish(nc, es, esA)
        barA = cm.n

        esB = contextlib.ExitStack()
        with esB:
            EB = esB.enter_context
            ckv2x = EB(nc.sbuf_tensor("ckv2x", [128, 8192], BF16))
            ckv65 = EB(nc.sbuf_tensor("ckv65", [128, 64 * 128], BF16))
            ckvs = EB(nc.sbuf_tensor("ckvs", [128, 128], F32))
            wpT = EB(nc.sbuf_tensor("wpT", [128, 64 * 128], BF16))
            wvu = EB(nc.sbuf_tensor("wvu", [128, 8 * 64], BF16))
            Mb = EB(nc.sbuf_tensor("Mb", [64, 16 * 1024], BF16))
            beffb = EB(nc.sbuf_tensor("beffb", [128, DM], F32))
            beffc = EB(nc.sbuf_tensor("beffc", [128, 8], F32))
            bvub = EB(nc.sbuf_tensor("bvub", [128, 8], BF16))
            onesP = EB(nc.sbuf_tensor("onesP", [65, 64], F32))
            beffr = EB(nc.sbuf_tensor("beffr", [1, DM], F32))
            ptb = EB(nc.sbuf_tensor("ptb", [128, 8 * TCH], BF16))
            osb = EB(nc.sbuf_tensor("osb", [128, 4 * TCH], F32))
            onb = EB(nc.sbuf_tensor("onb", [64, 4 * TCH], BF16))
            yac = EB(nc.sbuf_tensor("yac", [128, 4 * DM], F32))
            xg = EB(nc.sbuf_tensor("xg", [128, 4 * DM], F32))
            gld = EB(nc.sbuf_tensor("gld", [128, 4 * DM], F32))
            wld = EB(nc.sbuf_tensor("wld", [128, DM], F32))

            @block.gpsimd
            def _(g):
                g.wait_ge(cm.h, barA)
                g.wait_ge(cc1.h, 1)
                g.memset(onesP[:], 1.0)
                g.memset(ckv65[:], 0.0)
                g.memset(ckvs[:], 0.0)
                last = None
                for j in range(64):
                    last = g.memset(ckv65[:, j * 128 + 64:j * 128 + 65], 1.0)
                gm.inc(last)

            for j in range(64):
                @block.sync
                def _(s, j=j):
                    if j == 0:
                        s.wait_ge(gm.h, gm.n)
                    else:
                        s.wait_ge(cm.h, cm.n)
                    ld.dma(s.dma_start(out=ckvs[:, 0:64], in_=ckv_fu.ap()[j * 128:(j + 1) * 128, :]))

                @block.tensor
                def _(t, j=j):
                    t.wait_ge(ld.h, ld.n)
                    tp.inc(t.transpose(ps[:, 0:128], ckvs[:], ident[:]))

                @block.vector
                def _(v, j=j):
                    v.wait_ge(tp.h, tp.n)
                    v.tensor_copy(out=ckv65[:, j * 128:j * 128 + 64], in_=ckvs[:, 0:64])
                    cm.inc(v.tensor_copy(out=ckv2x[0:64, j * 128:(j + 1) * 128],
                                         in_=ps[0:64, 0:128]))

            @block.vector
            def _(v):
                v.wait_ge(cm.h, cm.n)
                last = None
                for j in range(64):
                    last = v.tensor_copy(out=ckv2x[64:128, j * 128:(j + 1) * 128],
                                         in_=ckv2x[0:64, j * 128:(j + 1) * 128])
                cm.inc(last)

            if stop_after == "b0":
                @block.gpsimd
                def _(g):
                    g.wait_ge(cm.h, cm.n)
                    g.wait_ge(st.h, st.n)
                return _finish(nc, es, esB)
            for i in range(8):
                @block.sync
                def _(s, i=i):
                    s.wait_ge(cm.h, cm.n)
                    ld.dma(s.dma_start(out=wld[:, 0:64], in_=wvu_d[i * 128:(i + 1) * 128, :]))

                @block.vector
                def _(v, i=i):
                    v.wait_ge(ld.h, ld.n)
                    cm.inc(v.tensor_copy(out=wvu[:, i * 64:(i + 1) * 64], in_=wld[:, 0:64]))

            emit_wpT_src = wpr_d
            for i in range(8):
                @block.sync
                def _(s, i=i):
                    s.wait_ge(cm.h, cm.n)
                    ld.dma(s.dma_start(out=wld[:], in_=emit_wpT_src[i * 128:(i + 1) * 128, :]))

                @block.tensor
                def _(t, i=i):
                    t.wait_ge(ld.h, ld.n)
                    for ct in range(NCT):
                        tp.inc(t.transpose(ps[:, (ct % 8) * 512:(ct % 8) * 512 + 128],
                                           wld[:, ct * 128:(ct + 1) * 128], ident[:]))

                @block.vector
                def _(v, i=i):
                    v.wait_ge(tp.h, tp.n)
                    last = None
                    for ct in range(NCT):
                        last = v.tensor_copy(out=wpT[:, (i * 8 + ct) * 128:(i * 8 + ct + 1) * 128],
                                             in_=ps[:, (ct % 8) * 512:(ct % 8) * 512 + 128])
                    cm.inc(last)

            if stop_after == "b1":
                @block.gpsimd
                def _(g):
                    g.wait_ge(cm.h, cm.n)
                    g.wait_ge(st.h, st.n)
                return _finish(nc, es, esB)
            for h in range(16):
                it, bp = h // 2, (h % 2) * 64
                for grp in range(2):
                    @block.tensor
                    def _(t, h=h, it=it, bp=bp, grp=grp):
                        t.wait_ge(cm.h, cm.n)
                        last = None
                        for j in range(4):
                            cb = grp * 4 + j
                            last = t.matmul(ps[0:64, j * 512:j * 512 + 128],
                                            wvu[bp:bp + 64, it * 64:(it + 1) * 64],
                                            wpT[bp:bp + 64, (cb * 8 + it) * 128:(cb * 8 + it + 1) * 128],
                                            start=True, stop=True)
                        tp.inc(last)

                    @block.vector
                    def _(v, h=h, it=it, bp=bp, grp=grp):
                        v.wait_ge(tp.h, tp.n)
                        last = None
                        for j in range(4):
                            cb = grp * 4 + j
                            last = v.tensor_copy(
                                out=Mb[0:64,
                                       h * 1024 + cb * 128:h * 1024 + (cb + 1) * 128],
                                in_=ps[0:64, j * 512:j * 512 + 128])
                        cm.inc(last)
            M_done = cm.n

            if stop_after == "b2":
                @block.gpsimd
                def _(g):
                    g.wait_ge(cm.h, cm.n)
                    g.wait_ge(st.h, st.n)
                return _finish(nc, es, esB)
            @block.vector
            def _(v):
                v.wait_ge(cm.h, M_done)
                cm.inc(v.tensor_copy(out=bvub[:], in_=prm[:, PRM_BVU:PRM_BVU + 8]))
            bvub_done = cm.n

            @block.sync
            def _(s):
                s.wait_ge(cm.h, bvub_done)
                for ct in range(8):
                    ld.dma(s.dma_start(out=beffr[0:1, ct * 128:(ct + 1) * 128],
                                       in_=bpr_d[ct:ct + 1, :]))
            bpr_ld = ld.n

            for ct in range(8):
                @block.tensor
                def _(t, ct=ct):
                    t.wait_ge(cm.h, cm.n)
                    last = None
                    for it in range(8):
                        last = t.matmul(ps[0:1, it * 512:it * 512 + 128],
                                        bvub[:, it:it + 1],
                                        wpT[:, (ct * 8 + it) * 128:(ct * 8 + it + 1) * 128],
                                        start=True, stop=True)
                    tp.inc(last)

                @block.vector
                def _(v, ct=ct):
                    v.wait_ge(tp.h, tp.n)
                    if ct == 0:
                        v.wait_ge(ld.h, bpr_ld)
                    last = None
                    for it in range(8):
                        last = v.tensor_add(out=beffr[0:1, ct * 128:(ct + 1) * 128],
                                            in0=beffr[0:1, ct * 128:(ct + 1) * 128],
                                            in1=ps[0:1, it * 512:it * 512 + 128])
                    cm.inc(last)
            beffr_done = cm.n

            @block.tensor
            def _(t):
                t.wait_ge(cm.h, beffr_done)
                t.matmul(ps[:, 2048:2560], ones_r[:], beffr[0:1, 0:512], start=True, stop=True)
                tp.inc(t.matmul(ps[:, 2560:3072], ones_r[:], beffr[0:1, 512:1024],
                                start=True, stop=True))

            @block.vector
            def _(v):
                v.wait_ge(tp.h, tp.n)
                v.tensor_copy(out=beffb[:, 0:512], in_=ps[:, 2048:2560])
                cm.inc(v.tensor_copy(out=beffb[:, 512:1024], in_=ps[:, 2560:3072]))
            prep_done = cm.n

            if stop_after == "aprep":
                if "beff" in dout:
                    @block.gpsimd
                    def _(g):
                        g.wait_ge(cm.h, prep_done)
                        st.dma(g.dma_start(out=dout["beff"], in_=beffb[:]))
                if "mb" in dout:
                    for i in range(16):
                        @block.vector
                        def _(v, i=i):
                            v.wait_ge(st.h, st.n)
                            cm.inc(v.tensor_copy(out=sqs[:], in_=Mb[:, i * 512:(i + 1) * 512]))

                        @block.gpsimd
                        def _(g, i=i):
                            g.wait_ge(cm.h, cm.n)
                            st.dma(g.dma_start(out=dout["mb"][:, i * 512:(i + 1) * 512], in_=sqs[:]))
                if "ckv2x" in dout:
                    for i in range(16):
                        @block.vector
                        def _(v, i=i):
                            v.wait_ge(st.h, st.n)
                            cm.inc(v.tensor_copy(out=sqs[:], in_=ckv2x[:, i * 512:(i + 1) * 512]))

                        @block.gpsimd
                        def _(g, i=i):
                            g.wait_ge(cm.h, cm.n)
                            st.dma(g.dma_start(out=dout["ckv2x"][:, i * 512:(i + 1) * 512], in_=sqs[:]))
                @block.gpsimd
                def _(g):
                    g.wait_ge(st.h, st.n)
                return _finish(nc, es, esB)

            NQC = 1 if stop_after == "attn1" else NCH
            NHG = 1 if stop_after == "attn1" else 4
            onb_cm, yac_cm = [], []
            for qc in range(NQC):
                q0 = qc * TCH
                for hg in range(NHG * 2):
                    heads = [2 * hg + i for i in range(2)]
                    sc0 = sc_s.n
                    ex0 = ex_s.n
                    pv0 = pv_s.n

                    @block.tensor
                    def _(t, q0=q0, heads=heads, sc0=sc0, ex0=ex0, pv0=pv0,
                          gate=(onb_cm[-1] if onb_cm else 0),
                          gate2=(yac_cm[-1] if yac_cm else 0)):
                        if qc == 0 and hg == 0:
                            t.wait_ge(cm.h, prep_done)
                        elif gate:
                            t.wait_ge(cm.h, gate)
                        else:
                            t.wait_ge(cm.h, cm.n)

                        def scores(kt):
                            if kt >= 2:
                                t.wait_ge(ex_s.h, ex0 + (kt - 1))
                            for i, h in enumerate(heads):
                                ft = h // 2
                                bp = (h % 2) * 64
                                ins = t.matmul(
                                    ps[:, (kt % 2) * 1024 + i * 512:(kt % 2) * 1024 + i * 512 + 512],
                                    ckv2x[bp:bp + 64, kt * 128:(kt + 1) * 128],
                                    qfm[bp:bp + 64, ft * TPC + q0:ft * TPC + q0 + TCH],
                                    start=True, stop=True,
                                    tile_position=(bp, 0))
                                if i == 1:
                                    sc_s.inc(ins)

                        def pv(kt):
                            if kt == 0 and gate2:
                                t.wait_ge(cm.h, gate2)
                            t.wait_ge(ex_s.h, ex0 + (kt + 1))
                            for i in range(2):
                                ins = t.matmul(ps[:, 2048 + i * 512:2048 + i * 512 + 512],
                                               ckv65[:, kt * 128:(kt + 1) * 128],
                                               ptb[:, ((kt % 2) * 2 + i) * TCH:
                                                   ((kt % 2) * 2 + i + 1) * TCH],
                                               start=(kt == 0), stop=(kt == 63))
                                if i == 1:
                                    pv_s.inc(ins)

                        scores(0)
                        for kt in range(1, 64):
                            scores(kt)
                            pv(kt - 1)
                        pv(63)

                    @block.scalar
                    def _(s, sc0=sc0, pv0=pv0):
                        for kt in range(64):
                            s.wait_ge(sc_s.h, sc0 + kt + 1)
                            if kt >= 2:
                                s.wait_ge(pv_s.h, pv0 + kt - 1)
                            ins = s.activation(
                                out=ptb[:, (kt % 2) * 2 * TCH:((kt % 2) * 2 + 2) * TCH],
                                in_=ps[:, (kt % 2) * 1024:(kt % 2) * 1024 + 1024],
                                func=AF.Exp, scale=0.125, bias=negK[:])
                            ex_s.inc(ins)

                    @block.vector
                    def _(v, hg=hg, pv0=pv0):
                        v.wait_ge(pv_s.h, pv0 + 64)
                        for i in range(2):
                            v.tensor_copy(out=osb[:, i * TCH:i * TCH + TCH],
                                          in_=ps[:, 2048 + i * 512:2048 + i * 512 + TCH])
                        last = None
                        for i in range(2):
                            last = v.reciprocal(out=osb[64:65, i * TCH:i * TCH + TCH],
                                                in_=osb[64:65, i * TCH:i * TCH + TCH])
                        cm.inc(last)

                    @block.tensor
                    def _(t, hg=hg):
                        t.wait_ge(cm.h, cm.n)
                        last = None
                        for i in range(2):
                            last = t.matmul(ps[0:64, i * 512:i * 512 + TCH],
                                            onesP[64:65, 0:64],
                                            osb[64:65, i * TCH:i * TCH + TCH],
                                            start=True, stop=True)
                        tp.inc(last)

                    @block.vector
                    def _(v, hg=hg):
                        v.wait_ge(tp.h, tp.n)
                        last = None
                        for i in range(2):
                            last = v.tensor_mul(
                                out=onb[0:64, i * TCH:i * TCH + TCH],
                                in0=osb[0:64, i * TCH:i * TCH + TCH],
                                in1=ps[0:64, i * 512:i * 512 + TCH])
                        cm.inc(last)
                    onb_cm.append(cm.n)

                    prj_sc0 = sc_s.n

                    @block.tensor
                    def _(t, hg=hg, heads=heads, prj_sc0=prj_sc0):
                        t.wait_ge(cm.h, cm.n)
                        for qt in range(4):
                            for ch in range(2):
                                p = qt * 2 + ch
                                if p >= 4:
                                    t.wait_ge(sc_s.h, prj_sc0 + p - 3)
                                last = None
                                for i, h in enumerate(heads):
                                    last = t.matmul(
                                        ps[:, 2048 + (qt % 2) * 1024 + ch * 512:
                                           2048 + (qt % 2) * 1024 + ch * 512 + 512],
                                        onb[0:64,
                                            i * TCH + qt * 128:i * TCH + qt * 128 + 128],
                                        Mb[0:64,
                                           h * 1024 + ch * 512:h * 1024 + (ch + 1) * 512],
                                        start=(i == 0), stop=(i == 1))
                                tp.inc(last)

                    @block.vector
                    def _(v, hg=hg):
                        base = tp.n - 8
                        k = 0
                        last = None
                        for qt in range(4):
                            for ch in range(2):
                                k += 1
                                v.wait_ge(tp.h, base + k)
                                dst = yac[:, qt * DM + ch * 512:qt * DM + (ch + 1) * 512]
                                src = ps[:, 2048 + (qt % 2) * 1024 + ch * 512:
                                         2048 + (qt % 2) * 1024 + ch * 512 + 512]
                                if hg == 0:
                                    last = v.tensor_copy(out=dst, in_=src)
                                else:
                                    last = v.tensor_add(out=dst, in0=dst, in1=src)
                                if k <= 4:
                                    sc_s.inc(last)
                                elif k == 8:
                                    cm.inc(last)
                    yac_cm.append(cm.n)

                if stop_after == "attn1":
                    if "yac" in dout:
                        @block.gpsimd
                        def _(g):
                            g.wait_ge(cm.h, cm.n)
                            st.dma(g.dma_start(out=dout["yac"], in_=yac[:]))
                    @block.gpsimd
                    def _(g):
                        g.wait_ge(st.h, st.n)
                    return _finish(nc, es, esB)
                @block.sync
                def _(s, qc=qc):
                    s.wait_ge(cm.h, cm.n)
                    s.wait_ge(st.h, st.n)
                    last = None
                    for qt in range(4):
                        ld.dma(s.dma_start(out=xg[:, qt * DM:(qt + 1) * DM],
                                           in_=x_d[qc * TCH + qt * 128:qc * TCH + (qt + 1) * 128, :]))
                        last = ld.dma(s.dma_start(
                            out=gld[:, qt * DM:(qt + 1) * DM],
                            in_=g0_dr.ap()[qc * TCH + qt * 128:qc * TCH + (qt + 1) * 128, :]))

                @block.vector
                def _(v, qc=qc):
                    v.wait_ge(ld.h, ld.n)
                    last = None
                    for qt in range(4):
                        v.tensor_add(out=yac[:, qt * DM:(qt + 1) * DM],
                                     in0=yac[:, qt * DM:(qt + 1) * DM], in1=beffb[:])
                        v.tensor_mul(out=gld[:, qt * DM:(qt + 1) * DM],
                                     in0=gld[:, qt * DM:(qt + 1) * DM],
                                     in1=yac[:, qt * DM:(qt + 1) * DM])
                        last = v.tensor_add(out=xg[:, qt * DM:(qt + 1) * DM],
                                            in0=xg[:, qt * DM:(qt + 1) * DM],
                                            in1=gld[:, qt * DM:(qt + 1) * DM])
                    cm.inc(last)

                @block.gpsimd
                def _(g, qc=qc):
                    g.wait_ge(cm.h, cm.n)
                    last = None
                    for qt in range(4):
                        last = st.dma(g.dma_start(
                            out=x1_d[qc * TCH + qt * 128:qc * TCH + (qt + 1) * 128, :],
                            in_=xg[:, qt * DM:(qt + 1) * DM]))
                        if "s0" in dout:
                            last = st.dma(g.dma_start(
                                out=dout["s0"][qc * TCH + qt * 128:qc * TCH + (qt + 1) * 128, :],
                                in_=yac[:, qt * DM:(qt + 1) * DM]))

            @block.gpsimd
            def _(g):
                g.wait_ge(st.h, st.n)
                if fuse is not None:
                    g.nop().then_inc(fuse["bar"].h, 1)

            if fuse is not None:
                @block.sync
                def _(s):
                    s.wait_ge(ld.h, ld.n)
                    s.wait_ge(st.h, st.n)
                    s.nop().then_inc(fuse["bar"].h, 1)

                @block.vector
                def _(v):
                    v.nop().then_inc(fuse["bar"].h, 1)

                @block.scalar
                def _(sc):
                    sc.nop().then_inc(fuse["bar"].h, 1)

                @block.tensor
                def _(t):
                    t.nop().then_inc(fuse["bar"].h, 1)

    return nc



def l0_in_map(inp, core):
    x = np.ascontiguousarray(inp["L0_attn_x"].reshape(-1, DM)[core * TPC:(core + 1) * TPC])
    return dict(
        x=x,
        wq_d=inp["L0_attn_wq_d"],
        bq_d=inp["L0_attn_bq_d"].reshape(4, 128),
        wqk_d=inp["L0_attn_w_qk"],
        bqk_d=inp["L0_attn_b_qk"].reshape(8, 128),
        wkv_d=inp["L0_attn_wkv_d"],
        bkv_d=inp["L0_attn_bkv_d"].reshape(1, 64),
        wvu_d=inp["L0_attn_wv_u"],
        bvu_d=inp["L0_attn_bv_u"].reshape(8, 128),
        wpr_d=inp["L0_attn_w_proj"],
        bpr_d=inp["L0_attn_b_proj"].reshape(8, 128),
        gam_d=inp["L0_ffn_ln_gamma"].reshape(8, 128),
        bet_d=inp["L0_ffn_ln_beta"].reshape(8, 128),
        f1q_d=inp["L0_ffn_fc1_q"],
        f1s_d=inp["L0_ffn_fc1_scales"],
        f1b_d=inp["L0_ffn_fc1_bias"].reshape(32, 128),
        f2q_d=inp["L0_ffn_fc2_q"],
        f2s_d=inp["L0_ffn_fc2_scales"],
        f2b_d=inp["L0_ffn_fc2_bias"].reshape(8, 128),
    )





DI = 2048
DSL = 256
NST = 16
DTR = 128


def build_l1(dbg=(), stop_after="all", races=True, fuse=None):
    if fuse is None:
        nc = bass.Bass("TRN2", target_bir_lowering=False, detect_race_conditions=races)
        x1_d = nc.dram_tensor("x1", [TPC, DM], F32, kind="ExternalInput").ap()
    else:
        nc = fuse["nc"]
        x1_d = fuse["x1_d"]
    lpfx = "" if fuse is None else "q_"
    win_d = nc.dram_tensor("win_d", [DM, 2 * DI], F32, kind="ExternalInput").ap()
    ck_d = nc.dram_tensor("ck_d", [4, DSL], F32, kind="ExternalInput").ap()
    cb_d = nc.dram_tensor("cb_d", [2, 128], F32, kind="ExternalInput").ap()
    wxp_d = nc.dram_tensor("wxp_d", [DSL, 160], F32, kind="ExternalInput").ap()
    wdt_d = nc.dram_tensor("wdt_d", [DTR, DSL], F32, kind="ExternalInput").ap()
    bdt_d = nc.dram_tensor("bdt_d", [2, 128], F32, kind="ExternalInput").ap()
    alog_d = nc.dram_tensor("alog_d", [DSL, NST], F32, kind="ExternalInput").ap()
    dv_d = nc.dram_tensor("dv_d", [2, 128], F32, kind="ExternalInput").ap()
    wout_d = nc.dram_tensor("wout_d", [DI, DM], F32, kind="ExternalInput").ap()
    wg_d = nc.dram_tensor("wg_d", [DM, NST], F32, kind="ExternalInput").ap()
    w1s_d = nc.dram_tensor("w1s_d", [2 * DM, 4096], F32, kind="ExternalInput").ap()
    w2s_d = nc.dram_tensor("w2s_d", [2 * 4096, DM], F32, kind="ExternalInput").ap()
    ccst_d = nc.dram_tensor("ccst_d", [1, 4], F32, kind="ExternalInput").ap()

    x2_d = nc.dram_tensor("x2", [TPC, DM], F32, kind="ExternalOutput").ap()
    dout = {}
    for name, shape, dt in [
        ("xn1", [128, NCT * TPC], F32), ("gate", [TPC, 1], F32), ("gsv", [TPC, 1], F32), ("ttb", [128, 32], F32),
        ("xin", [128, 4 * 8192], F32), ("u", [128, 4 * 8192], F32),
        ("xdb", [8 * TPC, 160], F32), ("dtf", [128, 4 * 8192], F32),
        ("yz", [128, 4 * 8192], F32), ("ptab", [8 * TPC, 1], F32),
        ("gst", [8 * TPC, 1], F32), ("eo", [TPC, DM], F32),
        ("s1", [TPC, DM], F32), ("g1", [TPC, DM], F32), ("x1late", [TPC, DM], F32),
    ]:
        if name in dbg:
            dout[name] = nc.dram_tensor("dbg_" + name, shape, dt, kind="ExternalOutput").ap()

    xn1_bo = nc.dram_tensor("xn1_bo", [TPC, DM], BF16)
    xn1_fu = nc.dram_tensor("xn1_fu", [8 * TPC, DM], BF16)
    idx_bo = nc.dram_tensor("idx_bo", [TPC, 1], F32)
    idx_fu = nc.dram_tensor("idx_fu", [8 * TPC, 1], F32)
    gsv_bo = nc.dram_tensor("gsv_bo", [TPC, 1], F32)
    gsv_fu = nc.dram_tensor("gsv_fu", [8 * TPC, 1], F32)
    xa_i = nc.dram_tensor("xa_i", [DI, TPC], F32)
    xa_o = nc.dram_tensor("xa_o", [DI, TPC], F32)
    za_i = nc.dram_tensor("za_i", [DI, TPC], F32)
    za_o = nc.dram_tensor("za_o", [DI, TPC], F32)
    u_dr = nc.dram_tensor("u_dr", [4 * 128, 8192], F32)
    dtl_dr = nc.dram_tensor("dtl_dr", [2 * DTR, 8192], F32)
    bc_dr = nc.dram_tensor("bc_dr", [2 * 32, 8192], F32)
    xdb_bo = nc.dram_tensor("xdb_bo", [8 * TPC, 160], F32)
    xdb_fu = nc.dram_tensor("xdb_fu", [8 * TPC, 160], F32)
    p_dr = nc.dram_tensor("p_dr", [8 * TPC, 1], F32)
    gs_dr = nc.dram_tensor("gs_dr", [8 * TPC, 1], F32)
    yz_i = nc.dram_tensor("yz_i", [DI, TPC], F32)
    yz_o = nc.dram_tensor("yz_o", [DI, TPC], F32)
    eo_bo = nc.dram_tensor("eo_bo", [TPC, DM], BF16)
    eo_fu = nc.dram_tensor("eo_fu", [8 * TPC, DM], BF16)
    pcl_bo = nc.dram_tensor("pcl_bo", [8 * TPC, 1], F32)
    pcl_sl = nc.dram_tensor("pcl_sl", [8 * TPC, 1], F32)
    gcl_bo = nc.dram_tensor("gcl_bo", [8 * TPC, 1], F32)
    gcl_sl = nc.dram_tensor("gcl_sl", [8 * TPC, 1], F32)

    es = contextlib.ExitStack()
    E = es.enter_context
    with es:
        ident = E(nc.sbuf_tensor(lpfx + "ident", [128, 128], F32))
        ones_c = E(nc.sbuf_tensor(lpfx + "ones_c", [128, 1], F32))
        ones_r = E(nc.sbuf_tensor(lpfx + "ones_r", [1, 128], F32))
        iotaF = E(nc.sbuf_tensor(lpfx + "iotaF", [128, 128], F32))
        iotaC = E(nc.sbuf_tensor(lpfx + "iotaC", [128, 1], F32))
        io16 = E(nc.sbuf_tensor(lpfx + "io16", [128, 16], F32))
        rows = E(nc.sbuf_tensor(lpfx + "rows", [1, 2048], F32))
        sqs = E(nc.sbuf_tensor(lpfx + "sqs", [128, TCH], F32))
        prm = E(nc.sbuf_tensor(lpfx + "prm", [128, 48], F32))
        tu32 = E(nc.sbuf_tensor(lpfx + "tu32", [128, 16], U32))
        ps = E(nc.psum_tensor(lpfx + "ps", [128, 4096], F32))

        PRM_A, PRM_CK, PRM_CB, PRM_D, PRM_BDT = 0, 32, 40, 42, 44

        SE = E if fuse is None else (lambda s: fuse["E"](s))
        ld = Sem(SE(nc.semaphore("b_ld")))
        st = Sem(SE(nc.semaphore("b_st")))
        cm = Sem(SE(nc.semaphore("b_cm")))
        tp = Sem(SE(nc.semaphore("b_tp")))
        av = Sem(SE(nc.semaphore("b_av")))
        gm = Sem(SE(nc.semaphore("b_gm")))
        sf = Sem(SE(nc.semaphore("b_sf")))
        cc = [Sem(SE(nc.semaphore(f"b_cc{i}"))) for i in range(8)]

        block = E(nc.Block())

        if fuse is not None:
            bar = fuse["bar"]

            @block.sync
            def _(s):
                s.wait_ge(bar.h, 5)

            @block.gpsimd
            def _(g):
                g.wait_ge(bar.h, 5)

            @block.vector
            def _(v):
                v.wait_ge(bar.h, 5)

            @block.scalar
            def _(sc):
                sc.wait_ge(bar.h, 5)

            @block.tensor
            def _(t):
                t.wait_ge(bar.h, 5)

        @block.gpsimd
        def _(g):
            g.memset(ident[:], 0.0)
            g.affine_select(out=ident[:], in_=ident[:], compare_op=ALU.not_equal,
                            fill=1.0, base=0, pattern=[[-1, 128]], channel_multiplier=1)
            g.memset(ones_c[:], 1.0)
            g.memset(ones_r[:], 1.0)
            g.iota(iotaF[:], [[1, 128]], channel_multiplier=0,
                   allow_small_or_imprecise_dtypes=True)
            g.iota(iotaC[:], [[1, 1]], channel_multiplier=1,
                   allow_small_or_imprecise_dtypes=True)
            gm.inc(g.iota(io16[:], [[1, 16]], channel_multiplier=0,
                          allow_small_or_imprecise_dtypes=True))

        @block.sync
        def _(s):
            ld.dma(s.dma_start(out=prm[:, PRM_A:PRM_A + 16], in_=alog_d[0:128, :]))
            ld.dma(s.dma_start(out=prm[:, PRM_A + 16:PRM_A + 32], in_=alog_d[128:256, :]))
            for k in range(4):
                ld.dma(s.dma_start(out=prm[:, PRM_CK + k:PRM_CK + k + 1],
                                   in_=ck_d[k:k + 1, 0:128]))
                ld.dma(s.dma_start(out=prm[:, PRM_CK + 4 + k:PRM_CK + 4 + k + 1],
                                   in_=ck_d[k:k + 1, 128:256]))
            for src, col in [(cb_d, PRM_CB), (dv_d, PRM_D), (bdt_d, PRM_BDT)]:
                for r in range(2):
                    ld.dma(s.dma_start(out=prm[:, col + r:col + r + 1], in_=src[r:r + 1, :]))
        p0_ld = ld.n

        @block.vector
        def _(v):
            v.wait_ge(ld.h, p0_ld)
            v.wait_ge(gm.h, gm.n)
            cm.inc(v.tensor_scalar_add(out=io16[:], in0=io16[:], scalar1=65536.0))

        @block.scalar
        def _(s):
            load_act_set(nc, s, "natural_log_exp_and_others")
            s.wait_ge(ld.h, p0_ld)
            av.inc(s.activation(out=prm[:, PRM_A:PRM_A + 32], in_=prm[:, PRM_A:PRM_A + 32],
                                func=AF.Exp))

        @block.vector
        def _(v):
            v.wait_ge(av.h, av.n)
            cm.inc(v.tensor_scalar_mul(out=prm[:, PRM_A:PRM_A + 32],
                                       in0=prm[:, PRM_A:PRM_A + 32], scalar1=-1.0))
        consts_done = cm.n



        esA = contextlib.ExitStack()
        EA = esA.enter_context
        with esA:
            esAx = contextlib.ExitStack()
            EAx = esAx.enter_context
            x1fm = EAx(nc.sbuf_tensor(lpfx + "x1fm", [128, NCT * TPC], F32))
            xt = EAx(nc.sbuf_tensor(lpfx + "xt", [128, DM], F32))

            for tt in range(NTT):
                @block.sync
                def _(s, tt=tt):
                    if tt == 0:
                        s.wait_ge(cm.h, consts_done)
                    else:
                        s.wait_ge(cm.h, cm.n)
                    ld.dma(s.dma_start(out=xt[:], in_=x1_d[tt * 128:(tt + 1) * 128, :]))

                @block.tensor
                def _(t, tt=tt):
                    t.wait_ge(ld.h, ld.n)
                    for ct in range(NCT):
                        tp.inc(t.transpose(ps[:, ct * 512:ct * 512 + 128],
                                           xt[:, ct * 128:(ct + 1) * 128], ident[:]))

                @block.vector
                def _(v, tt=tt):
                    v.wait_ge(tp.h, tp.n)
                    last = None
                    for ct in range(NCT):
                        last = v.tensor_copy(
                            out=x1fm[:, ct * TPC + tt * 128:ct * TPC + (tt + 1) * 128],
                            in_=ps[:, ct * 512:ct * 512 + 128])
                    cm.inc(last)

            r_mu = rows[0:1, 0:TCH]
            r_va = rows[0:1, TCH:2 * TCH]
            r_t1 = rows[0:1, 2 * TCH:3 * TCH]
            ps_sum = ps[0:1, 1024:1024 + TCH]
            ps_sq = ps[0:1, 1536:1536 + TCH]
            ps_mu = ps[:, 2048:2048 + TCH]
            ps_rs = ps[:, 2560:2560 + TCH]

            for tc in range(NCH):
                c0 = tc * TCH
                for ct in range(NCT):
                    @block.scalar
                    def _(s, ct=ct, c0=c0):
                        s.wait_ge(cm.h, cm.n)
                        if ct > 0:
                            s.wait_ge(tp.h, tp.n)
                        av.inc(s.activation(out=sqs[:],
                                            in_=x1fm[:, ct * TPC + c0:ct * TPC + c0 + TCH],
                                            func=AF.Square))

                    @block.tensor
                    def _(t, ct=ct, c0=c0):
                        t.wait_ge(av.h, av.n)
                        t.matmul(ps_sum, ones_c[:],
                                 x1fm[:, ct * TPC + c0:ct * TPC + c0 + TCH],
                                 start=(ct == 0), stop=(ct == NCT - 1))
                        tp.inc(t.matmul(ps_sq, ones_c[:], sqs[:],
                                        start=(ct == 0), stop=(ct == NCT - 1)))

                @block.vector
                def _(v, c0=c0):
                    v.wait_ge(tp.h, tp.n)
                    v.tensor_scalar(out=r_mu, in0=ps_sum, scalar1=1.0 / DM,
                                    scalar2=None, op0=ALU.mult)
                    v.tensor_scalar(out=r_va, in0=ps_sq, scalar1=1.0 / DM,
                                    scalar2=None, op0=ALU.mult)
                    v.tensor_mul(out=r_t1, in0=r_mu, in1=r_mu)
                    v.tensor_sub(out=r_va, in0=r_va, in1=r_t1)
                    cm.inc(v.tensor_scalar_add(out=r_va, in0=r_va, scalar1=1e-5))

                @block.scalar
                def _(s):
                    s.wait_ge(cm.h, cm.n)
                    s.activation(out=r_t1, in_=r_va, func=AF.Ln)
                    av.inc(s.activation(out=r_va, in_=r_t1, func=AF.Exp, scale=-0.5))

                @block.tensor
                def _(t):
                    t.wait_ge(av.h, av.n)
                    t.matmul(ps_mu, ones_r[:], r_mu, start=True, stop=True)
                    tp.inc(t.matmul(ps_rs, ones_r[:], r_va, start=True, stop=True))

                @block.vector
                def _(v, c0=c0):
                    v.wait_ge(tp.h, tp.n)
                    last = None
                    for ct in range(NCT):
                        sl = x1fm[:, ct * TPC + c0:ct * TPC + c0 + TCH]
                        v.tensor_sub(out=sl, in0=sl, in1=ps_mu)
                        last = v.tensor_mul(out=sl, in0=sl, in1=ps_rs)
                    cm.inc(last)
            ln_done = cm.n

            if "xn1" in dout:
                @block.gpsimd
                def _(g):
                    g.wait_ge(cm.h, ln_done)
                    st.dma(g.dma_start(out=dout["xn1"], in_=x1fm[:]))

            esP2 = contextlib.ExitStack()
            E2 = esP2.enter_context
            xtkb = E2(nc.sbuf_tensor(lpfx + "xtkb", [128, DM], BF16))
            wgl = E2(nc.sbuf_tensor(lpfx + "wgl", [128, 8 * 128], F32))
            lfm = E2(nc.sbuf_tensor(lpfx + "lfm", [32, TPC], F32))
            ltk = E2(nc.sbuf_tensor(lpfx + "ltk", [128, 48], F32))
            gcol = E2(nc.sbuf_tensor(lpfx + "gcol", [128, 8], F32))
            xnbw = E2(nc.sbuf_tensor(lpfx + "xnbw", [128, NCT * TPC], BF16))

            @block.vector
            def _(v):
                v.wait_ge(cm.h, ln_done)
                last = None
                for ct in range(NCT):
                    last = v.tensor_copy(out=xnbw[:, ct * TPC:(ct + 1) * TPC],
                                         in_=x1fm[:, ct * TPC:(ct + 1) * TPC])
                cm.inc(last)

            for tt in range(NTT):
                @block.tensor
                def _(t, tt=tt):
                    t.wait_ge(cm.h, cm.n)
                    for ct in range(NCT):
                        tp.inc(t.transpose(ps[:, ct * 512:ct * 512 + 128],
                                           x1fm[:, ct * TPC + tt * 128:ct * TPC + (tt + 1) * 128],
                                           ident[:]))

                @block.vector
                def _(v, tt=tt):
                    v.wait_ge(tp.h, tp.n)
                    v.wait_ge(st.h, st.n)
                    last = None
                    for ct in range(NCT):
                        last = v.tensor_copy(out=xtkb[:, ct * 128:(ct + 1) * 128],
                                             in_=ps[:, ct * 512:ct * 512 + 128])
                    cm.inc(last)

                @block.gpsimd
                def _(g, tt=tt):
                    g.wait_ge(cm.h, cm.n)
                    st.dma(g.dma_start(out=xn1_bo.ap()[tt * 128:(tt + 1) * 128, :],
                                       in_=xtkb[:]))

            xn1bo_st = st.n

            @block.gpsimd
            def _(g):
                gm.inc(g.memset(wgl[:], 0.0))

            for kt in range(8):
                @block.sync
                def _(s, kt=kt):
                    if kt == 0:
                        s.wait_ge(gm.h, gm.n)
                    else:
                        s.wait_ge(cm.h, cm.n)
                    ld.dma(s.dma_start(out=xt[:, 0:16], in_=wg_d[kt * 128:(kt + 1) * 128, :]))

                @block.vector
                def _(v, kt=kt):
                    v.wait_ge(ld.h, ld.n)
                    cm.inc(v.tensor_copy(out=wgl[:, kt * 128:kt * 128 + 16], in_=xt[:, 0:16]))
            wgl_done = cm.n

            for tc in range(NCH):
                @block.tensor
                def _(t, tc=tc):
                    t.wait_ge(cm.h, cm.n)
                    last = None
                    for kt in range(8):
                        last = t.matmul(ps[:, 0:TCH],
                                        wgl[:, kt * 128:(kt + 1) * 128],
                                        x1fm[:, kt * TPC + tc * TCH:kt * TPC + tc * TCH + TCH],
                                        start=(kt == 0), stop=(kt == 7))
                    tp.inc(last)

                @block.vector
                def _(v, tc=tc):
                    v.wait_ge(tp.h, tp.n)
                    cm.inc(v.tensor_copy(out=lfm[0:32, tc * TCH:(tc + 1) * TCH],
                                         in_=ps[0:32, 0:TCH]))
            lfm_done = cm.n

            for tt in range(NTT):
                @block.tensor
                def _(t, tt=tt):
                    t.wait_ge(cm.h, cm.n)
                    tp.inc(t.transpose(ps[:, 3072:3072 + 32],
                                       lfm[0:32, tt * 128:(tt + 1) * 128],
                                       ident[0:32, 0:32]))

                @block.vector
                def _(v, tt=tt):
                    v.wait_ge(tp.h, tp.n)
                    v.wait_ge(st.h, st.n)
                    v.tensor_copy(out=ltk[:, 0:32], in_=ps[:, 3072:3072 + 32])
                    v.tensor_reduce(out=gcol[:, 0:1], in_=ltk[:, 0:16], axis=AX.X, op=ALU.max)
                    sf.inc(v.tensor_scalar_mul(out=gcol[:, 1:2], in0=gcol[:, 0:1], scalar1=-1.0))

                @block.scalar
                def _(s, tt=tt):
                    s.wait_ge(sf.h, sf.n)
                    av.inc(s.activation(out=ltk[:, 16:32], in_=ltk[:, 0:16],
                                        func=AF.Exp, bias=gcol[:, 1:2]))

                @block.vector
                def _(v, tt=tt):
                    v.wait_ge(av.h, av.n)
                    sf.inc(v.tensor_reduce(out=gcol[:, 2:3], in_=ltk[:, 16:32],
                                           axis=AX.X, op=ALU.sum if hasattr(ALU, "sum") else ALU.add))
                    v.wait_ge(sf.h, sf.n)
                    sf.inc(v.reciprocal(out=gcol[:, 3:4], in_=gcol[:, 2:3]))
                    v.wait_ge(sf.h, sf.n)
                    sf.inc(v.tensor_scalar(out=ltk[:, 32:48], in0=ltk[:, 16:32],
                                           scalar1=gcol[:, 3:4], scalar2=None, op0=ALU.mult))
                    v.wait_ge(sf.h, sf.n)
                    sf.inc(v.tensor_reduce(out=gcol[:, 4:5], in_=ltk[:, 32:48],
                                           axis=AX.X, op=ALU.max))
                    v.wait_ge(sf.h, sf.n)
                    sf.inc(v.tensor_scalar(out=ltk[:, 16:32], in0=ltk[:, 32:48],
                                           scalar1=gcol[:, 4:5], scalar2=None, op0=ALU.is_ge))
                    v.wait_ge(sf.h, sf.n)
                    sf.inc(v.scalar_tensor_tensor(out=ltk[:, 0:16], in0=ltk[:, 16:32],
                                                  scalar=-65536.0, in1=io16[:],
                                                  op0=ALU.mult, op1=ALU.add))
                    v.wait_ge(sf.h, sf.n)
                    cm.inc(v.tensor_reduce(out=gcol[:, 5:6], in_=ltk[:, 0:16],
                                           axis=AX.X, op=ALU.min))

                @block.gpsimd
                def _(g, tt=tt):
                    g.wait_ge(cm.h, cm.n)
                    st.dma(g.dma_start(out=idx_bo.ap()[tt * 128:(tt + 1) * 128, :],
                                       in_=gcol[:, 5:6]))
                    st.dma(g.dma_start(out=gsv_bo.ap()[tt * 128:(tt + 1) * 128, :],
                                       in_=gcol[:, 4:5]))

            @block.gpsimd
            def _(g):
                g.wait_ge(st.h, st.n)
                cc[1].inc(g.collective_compute(
                    "AllGather", ALU.bypass, replica_groups=GROUPS8,
                    ins=[idx_bo.ap().opt()], outs=[idx_fu.ap().opt()]))
                cc[1].inc(g.collective_compute(
                    "AllGather", ALU.bypass, replica_groups=GROUPS8,
                    ins=[gsv_bo.ap().opt()], outs=[gsv_fu.ap().opt()]))

            if "gate" in dout:
                @block.gpsimd
                def _(g):
                    g.wait_ge(cc[1].h, 1)
                    st.dma(g.dma_start(out=dout["gate"], in_=idx_bo.ap()))
                    if "gsv" in dout:
                        st.dma(g.dma_start(out=dout["gsv"], in_=gsv_bo.ap()))
            esP2.close()
            if stop_after == "gate":
                @block.gpsimd
                def _(g):
                    g.wait_ge(cc[0].h, 1)
                    g.wait_ge(cc[1].h, 2)
                    g.wait_ge(st.h, st.n)
                return _finish(nc, es, esAx, esA)

            esP3 = contextlib.ExitStack()
            E3 = esP3.enter_context
            xnbw3 = E3(nc.sbuf_tensor(lpfx + "xnbw3", [128, NCT * TPC], BF16))
            wstr = E3(nc.sbuf_tensor(lpfx + "wstr", [128, 2 * 1024], BF16))
            wld3 = E3(nc.sbuf_tensor(lpfx + "wld3", [128, 2 * 1024], F32))
            xzc = E3(nc.sbuf_tensor(lpfx + "xzc", [128, 2 * TCH], F32))

            @block.vector
            def _(v):
                v.wait_ge(cm.h, cm.n)
                last = None
                for ct in range(NCT):
                    last = v.tensor_copy(out=xnbw3[:, ct * TPC:(ct + 1) * TPC],
                                         in_=x1fm[:, ct * TPC:(ct + 1) * TPC])
                cm.inc(last)
            xnb3_done = cm.n

            p3_ld, p3_cast, p3_mm, p3_ev, p3_st = [], [], [], [], []
            for ft in range(32):
                @block.sync
                def _(s, ft=ft,
                      wfree=(p3_cast[ft - 2] if ft >= 2 else 0)):
                    if ft == 0:
                        s.wait_ge(cm.h, cm.n)
                    elif wfree:
                        s.wait_ge(cm.h, wfree)
                    for kt in range(8):
                        ld.dma(s.dma_start(
                            out=wld3[:, (ft % 2) * 1024 + kt * 128:(ft % 2) * 1024 + (kt + 1) * 128],
                            in_=win_d[kt * 128:(kt + 1) * 128, ft * 128:(ft + 1) * 128]))
                p3_ld.append(ld.n)

                @block.vector
                def _(v, ft=ft, ldn=p3_ld[-1],
                      sfree=(p3_mm[(ft - 2) * NCH + NCH - 1] if ft >= 2 else 0)):
                    v.wait_ge(ld.h, ldn)
                    if sfree:
                        v.wait_ge(tp.h, sfree)
                    cm.inc(v.tensor_copy(out=wstr[:, (ft % 2) * 1024:(ft % 2 + 1) * 1024],
                                         in_=wld3[:, (ft % 2) * 1024:(ft % 2 + 1) * 1024]))
                p3_cast.append(cm.n)

                for tc in range(NCH):
                    it3 = ft * NCH + tc

                    @block.tensor
                    def _(t, ft=ft, tc=tc, cmn=p3_cast[-1],
                          bfree=(p3_ev[it3 - 2] if it3 >= 2 else 0)):
                        t.wait_ge(cm.h, cmn)
                        if bfree:
                            t.wait_ge(cm.h, bfree)
                        last = None
                        for kt in range(8):
                            last = t.matmul(ps[:, (it3 % 2) * 512:(it3 % 2) * 512 + TCH],
                                            wstr[:, (ft % 2) * 1024 + kt * 128:
                                                 (ft % 2) * 1024 + (kt + 1) * 128],
                                            xnbw3[:, kt * TPC + tc * TCH:kt * TPC + tc * TCH + TCH],
                                            start=(kt == 0), stop=(kt == 7))
                        tp.inc(last)
                    p3_mm.append(tp.n)

                    @block.vector
                    def _(v, ft=ft, tc=tc, it3=it3, tpn=p3_mm[-1],
                          xfree=(p3_st[it3 - 2] if it3 >= 2 else 0)):
                        v.wait_ge(tp.h, tpn)
                        if xfree:
                            v.wait_ge(st.h, xfree)
                        else:
                            v.wait_ge(st.h, st.n)
                        cm.inc(v.tensor_copy(out=xzc[:, (it3 % 2) * TCH:(it3 % 2 + 1) * TCH],
                                             in_=ps[:, (it3 % 2) * 512:(it3 % 2) * 512 + TCH]))
                    p3_ev.append(cm.n)

                    @block.gpsimd
                    def _(g, ft=ft, tc=tc, it3=it3, cmn=p3_ev[-1]):
                        g.wait_ge(cm.h, cmn)
                        dst = xa_i if ft < 16 else za_i
                        fr = ft if ft < 16 else ft - 16
                        st.dma(g.dma_start(
                            out=dst.ap()[fr * 128:(fr + 1) * 128, tc * TCH:(tc + 1) * TCH],
                            in_=xzc[:, (it3 % 2) * TCH:(it3 % 2 + 1) * TCH]))
                    p3_st.append(st.n)

                if ft == 15:
                    @block.gpsimd
                    def _(g):
                        g.wait_ge(st.h, st.n)
                        with allow_ici_all_to_all():
                            cc[2].inc(g.collective_compute(
                                "AllToAll", ALU.bypass, replica_groups=GROUPS8,
                                ins=[xa_i.ap().opt()], outs=[xa_o.ap().opt()]))

            @block.gpsimd
            def _(g):
                g.wait_ge(st.h, st.n)
                with allow_ici_all_to_all():
                    cc[3].inc(g.collective_compute(
                        "AllToAll", ALU.bypass, replica_groups=GROUPS8,
                        ins=[za_i.ap().opt()], outs=[za_o.ap().opt()]))
                g.wait_ge(st.h, xn1bo_st)
                cc[0].inc(g.collective_compute(
                    "AllGather", ALU.bypass, replica_groups=GROUPS8,
                    ins=[xn1_bo.ap().opt()], outs=[xn1_fu.ap().opt()]))
            esP3.close()

            if stop_after == "p3":
                @block.gpsimd
                def _(g):
                    g.wait_ge(cc[2].h, 1)
                    g.wait_ge(cc[3].h, 1)
                    g.wait_ge(st.h, st.n)
                return _finish(nc, es, esAx, esA)

            esAx.close()
            esP4 = contextlib.ExitStack()
            E4 = esP4.enter_context
            preb = E4(nc.sbuf_tensor(lpfx + "preb", [128, 8192], F32))
            wxpb = E4(nc.sbuf_tensor(lpfx + "wxpb", [128, 2 * 160], BF16))
            wdtb = E4(nc.sbuf_tensor(lpfx + "wdtb", [128, 2 * 128], BF16))
            t1b = E4(nc.sbuf_tensor(lpfx + "t1b", [128, TCH], F32))
            t2b = E4(nc.sbuf_tensor(lpfx + "t2b", [128, TCH], F32))
            esP4x = contextlib.ExitStack()
            xinb = esP4x.enter_context(nc.sbuf_tensor(lpfx + "xinb", [128, 8200], F32))
            esP4a = contextlib.ExitStack()
            E4a = esP4a.enter_context
            ubf = E4a(nc.sbuf_tensor(lpfx + "ubf", [128, 8192], BF16))
            xdba = E4a(nc.sbuf_tensor(lpfx + "xdba", [128, 64 * 160], F32))

            @block.scalar
            def _(s):
                s.wait_ge(cm.h, cm.n)
                load_act_set(nc, s, "exp_and_others")
                av.inc(s.nop())

            @block.sync
            def _(s):
                ld.dma(s.dma_start(out=xinb[:, 0:160], in_=wxp_d[0:128, :]))
                ld.dma(s.dma_start(out=xinb[:, 160:320], in_=wxp_d[128:256, :]))
                ld.dma(s.dma_start(out=xinb[:, 320:448], in_=wdt_d[:, 0:128]))
                ld.dma(s.dma_start(out=xinb[:, 448:576], in_=wdt_d[:, 128:256]))

            @block.vector
            def _(v):
                v.wait_ge(ld.h, ld.n)
                v.tensor_copy(out=wxpb[:, 0:160], in_=xinb[:, 0:160])
                v.tensor_copy(out=wxpb[:, 160:320], in_=xinb[:, 160:320])
                v.tensor_copy(out=wdtb[:, 0:128], in_=xinb[:, 320:448])
                cm.inc(v.tensor_copy(out=wdtb[:, 128:256], in_=xinb[:, 448:576]))
            p4w_done = cm.n

            for b in range(2):
                for dt_ in range(2):
                    pix = 2 * b + dt_

                    @block.gpsimd
                    def _(g, b=b, dt_=dt_):
                        g.wait_ge(cm.h, cm.n)
                        gm.inc(g.memset(xinb[:, 0:3], 0.0))

                    @block.sync
                    def _(s, b=b, dt_=dt_):
                        s.wait_ge(gm.h, gm.n)
                        if b == 0 and dt_ == 0:
                            s.wait_ge(cc[2].h, 1)
                        for jj in range(4):
                            ld.dma(s.dma_start(
                                out=xinb[:, 3 + jj * 2048:3 + (jj + 1) * 2048],
                                in_=xa_o.ap()[(4 * b + jj) * 256 + dt_ * 128:
                                              (4 * b + jj) * 256 + dt_ * 128 + 128, :]))

                    @block.vector
                    def _(v, b=b, dt_=dt_):
                        v.wait_ge(ld.h, ld.n)
                        v.wait_ge(st.h, st.n)
                        ckb = PRM_CK + dt_ * 4
                        sf.inc(v.tensor_scalar(out=preb[:], in0=xinb[:, 0:8192],
                                               scalar1=prm[:, ckb:ckb + 1], scalar2=None,
                                               op0=ALU.mult))
                        for k in range(1, 4):
                            v.wait_ge(sf.h, sf.n)
                            sf.inc(v.scalar_tensor_tensor(out=preb[:], in0=xinb[:, k:k + 8192],
                                                          scalar=prm[:, ckb + k:ckb + k + 1],
                                                          in1=preb[:], op0=ALU.mult, op1=ALU.add))
                        v.wait_ge(sf.h, sf.n)
                        cm.inc(v.tensor_scalar(out=preb[:], in0=preb[:],
                                               scalar1=prm[:, PRM_CB + dt_:PRM_CB + dt_ + 1],
                                               scalar2=None, op0=ALU.add))

                    for chh in range(4):
                        @block.scalar
                        def _(s, chh=chh):
                            s.wait_ge(cm.h, cm.n)
                            av.inc(s.activation(out=xinb[:, chh * 2048:(chh + 1) * 2048],
                                                in_=preb[:, chh * 2048:(chh + 1) * 2048],
                                                func=AF.Tanh, scale=0.5))

                    @block.vector
                    def _(v, b=b, dt_=dt_):
                        v.wait_ge(av.h, av.n)
                        sf.inc(v.tensor_scalar(out=xinb[:, 0:8192], in0=xinb[:, 0:8192],
                                                scalar1=0.5, scalar2=0.5,
                                                op0=ALU.mult, op1=ALU.add))
                        v.wait_ge(sf.h, sf.n)
                        sf.inc(v.tensor_mul(out=preb[:], in0=preb[:], in1=xinb[:, 0:8192]))
                        v.wait_ge(sf.h, sf.n)
                        v.tensor_copy(out=ubf[:, 0:4096], in_=preb[:, 0:4096])
                        cm.inc(v.tensor_copy(out=ubf[:, 4096:8192], in_=preb[:, 4096:8192]))

                    @block.gpsimd
                    def _(g, pix=pix):
                        g.wait_ge(cm.h, cm.n)
                        st.dma(g.dma_start(out=u_dr.ap()[pix * 128:(pix + 1) * 128, :],
                                           in_=preb[:]))
                        if "u" in dout:
                            st.dma(g.dma_start(out=dout["u"][:, pix * 8192:(pix + 1) * 8192],
                                               in_=preb[:]))

                    for tt in range(64):
                        @block.tensor
                        def _(t, tt=tt, dt_=dt_):
                            t.wait_ge(cm.h, cm.n)
                            if tt > 0:
                                t.wait_ge(cm.h, cm.n)
                            tp.inc(t.matmul(ps[:, (tt % 2) * 512:(tt % 2) * 512 + 160],
                                            ubf[:, tt * 128:(tt + 1) * 128],
                                            wxpb[:, dt_ * 160:(dt_ + 1) * 160],
                                            start=True, stop=True))

                        @block.vector
                        def _(v, tt=tt, dt_=dt_):
                            v.wait_ge(tp.h, tp.n)
                            if dt_ == 0 and tt == 0:
                                v.wait_ge(st.h, st.n)
                            if dt_ == 0:
                                ins = v.tensor_copy(out=xdba[:, tt * 160:(tt + 1) * 160],
                                                    in_=ps[:, (tt % 2) * 512:(tt % 2) * 512 + 160])
                            else:
                                ins = v.tensor_add(out=xdba[:, tt * 160:(tt + 1) * 160],
                                                   in0=xdba[:, tt * 160:(tt + 1) * 160],
                                                   in1=ps[:, (tt % 2) * 512:(tt % 2) * 512 + 160])
                            cm.inc(ins)

                if True:
                    @block.gpsimd
                    def _(g, b=b):
                        g.wait_ge(cm.h, cm.n)
                        last = None
                        for tt in range(64):
                            last = st.dma(g.dma_start(
                                out=xdb_bo.ap()[b * 8192 + tt * 128:b * 8192 + (tt + 1) * 128, :],
                                in_=xdba[:, tt * 160:(tt + 1) * 160]))

                @block.gpsimd
                def _(g, b=b):
                    g.wait_ge(st.h, st.n)
                    cc[4].inc(g.collective_compute(
                        "AllReduce", ALU.add, replica_groups=GROUPS8,
                        ins=[xdb_bo.ap()[b * 8192:(b + 1) * 8192, :].opt()],
                        outs=[xdb_fu.ap()[b * 8192:(b + 1) * 8192, :].opt()]))
            esP4a.close()

            if "xdb" in dout:
                @block.gpsimd
                def _(g):
                    g.wait_ge(cc[4].h, 2)
                    st.dma(g.dma_start(out=dout["xdb"], in_=xdb_fu.ap()))

            pb_ld, pb_tp, pb_cm, pb_st = [], [], [], []
            for b in range(2):
                for tt in range(64):
                    i4 = b * 64 + tt
                    s4 = i4 % 2

                    @block.sync
                    def _(s, b=b, tt=tt, s4=s4,
                          xfree=(pb_tp[i4 - 2] if i4 >= 2 else 0)):
                        if tt == 0:
                            s.wait_ge(cc[4].h, b + 1)
                        if tt == 0 and b == 0:
                            s.wait_ge(cm.h, cm.n)
                        if xfree:
                            s.wait_ge(tp.h, xfree)
                        ld.dma(s.dma_start(out=xinb[:, s4 * 2048:s4 * 2048 + 160],
                                           in_=xdb_fu.ap()[b * 8192 + tt * 128:
                                                           b * 8192 + (tt + 1) * 128, :]))
                    pb_ld.append(ld.n)

                    @block.tensor
                    def _(t, s4=s4, ldn=pb_ld[-1],
                          pfree=(pb_cm[i4 - 2] if i4 >= 2 else 0)):
                        t.wait_ge(ld.h, ldn)
                        if pfree:
                            t.wait_ge(cm.h, pfree)
                        t.transpose(ps[:, s4 * 1024:s4 * 1024 + 128],
                                    xinb[:, s4 * 2048:s4 * 2048 + 128], ident[:])
                        tp.inc(t.transpose(ps[:, s4 * 1024 + 512:s4 * 1024 + 640],
                                           xinb[:, s4 * 2048 + 32:s4 * 2048 + 160], ident[:]))
                    pb_tp.append(tp.n)

                    @block.vector
                    def _(v, s4=s4, tpn=pb_tp[-1],
                          tfree=(pb_st[i4 - 2] if i4 >= 2 else 0)):
                        v.wait_ge(tp.h, tpn)
                        if tfree:
                            v.wait_ge(st.h, tfree)
                        else:
                            v.wait_ge(st.h, st.n)
                        v.tensor_copy(out=t1b[:, s4 * 128:s4 * 128 + 128],
                                      in_=ps[:, s4 * 1024:s4 * 1024 + 128])
                        cm.inc(v.tensor_copy(out=t2b[:, s4 * 128:s4 * 128 + 128],
                                             in_=ps[:, s4 * 1024 + 512:s4 * 1024 + 640]))
                    pb_cm.append(cm.n)

                    @block.gpsimd
                    def _(g, b=b, tt=tt, s4=s4, cmn=pb_cm[-1]):
                        g.wait_ge(cm.h, cmn)
                        st.dma(g.dma_start(out=dtl_dr.ap()[b * 128:(b + 1) * 128,
                                                           tt * 128:(tt + 1) * 128],
                                           in_=t1b[:, s4 * 128:s4 * 128 + 128]))
                        st.dma(g.dma_start(out=bc_dr.ap()[b * 32:(b + 1) * 32,
                                                          tt * 128:(tt + 1) * 128],
                                           in_=t2b[96:128, s4 * 128:s4 * 128 + 128]))
                    pb_st.append(st.n)

            yz_hist = []
            row_ldB, row_ld, dta_sf = [], [], []
            dbu_sf, main1_sf, gp_gm, avA = [], [], [], []
            esP4x.close()
            esP4c = contextlib.ExitStack()
            E4c = esP4c.enter_context
            dtf = E4c(nc.sbuf_tensor(lpfx + "dtf", [128, 8192], F32))
            dlt = E4c(nc.sbuf_tensor(lpfx + "dlt", [128, 8192], F32))
            wch = E4c(nc.sbuf_tensor(lpfx + "wch", [128, 2048], F32))
            tA = E4c(nc.sbuf_tensor(lpfx + "tA", [128, 2048], F32))
            tB = E4c(nc.sbuf_tensor(lpfx + "tB", [128, 2048], F32))
            tC = E4c(nc.sbuf_tensor(lpfx + "tC", [128, 2 * 2048], F32))
            ych = E4c(nc.sbuf_tensor(lpfx + "ych", [128, 2048], F32))
            zch = E4c(nc.sbuf_tensor(lpfx + "zch", [128, 2048], F32))
            sbB = E4c(nc.sbuf_tensor(lpfx + "sbB", [128, 2048], F32))
            sbC = E4c(nc.sbuf_tensor(lpfx + "sbC", [128, 2048], F32))
            dltb = E4c(nc.sbuf_tensor(lpfx + "dltb", [128, 512], BF16))
            carry = E4c(nc.sbuf_tensor(lpfx + "carry", [128, NST], F32))

            for b in range(2):
                for dt_ in range(2):
                    pix = 2 * b + dt_

                    @block.sync
                    def _(s, b=b, dt_=dt_, pix=pix):
                        s.wait_ge(cm.h, cm.n)
                        s.wait_ge(st.h, st.n)
                        for hh in range(2):
                            ld.dma(s.dma_start(out=preb[:, hh * 4096:(hh + 1) * 4096],
                                               in_=u_dr.ap()[pix * 128:(pix + 1) * 128,
                                                             hh * 4096:(hh + 1) * 4096]))
                            ld.dma(s.dma_start(out=dlt[:, hh * 4096:(hh + 1) * 4096],
                                               in_=dtl_dr.ap()[b * 128:(b + 1) * 128,
                                                               hh * 4096:(hh + 1) * 4096]))

                    @block.gpsimd
                    def _(g):
                        g.wait_ge(cm.h, cm.n)
                        gm.inc(g.memset(carry[:], 0.0))
                    pass_gm = gm.n

                    @block.scalar
                    def _(s):
                        load_act_set(nc, s, "natural_log_exp_and_others")
                        av.inc(s.nop())

                    for tch in range(16):
                        @block.vector
                        def _(v, tch=tch):
                            v.wait_ge(ld.h, ld.n)
                            v.wait_ge(tp.h, tp.n)
                            cm.inc(v.tensor_copy(out=dltb[:],
                                                 in_=dlt[:, tch * 512:(tch + 1) * 512]))

                        @block.tensor
                        def _(t, dt_=dt_, tch=tch):
                            t.wait_ge(cm.h, cm.n)
                            tp.inc(t.matmul(ps[:, (tch % 2) * 512:(tch % 2) * 512 + TCH],
                                            wdtb[:, dt_ * 128:(dt_ + 1) * 128],
                                            dltb[:],
                                            start=True, stop=True))

                        @block.vector
                        def _(v, dt_=dt_, tch=tch):
                            v.wait_ge(tp.h, tp.n)
                            cm.inc(v.tensor_scalar(out=t1b[:], in0=ps[:, (tch % 2) * 512:(tch % 2) * 512 + TCH],
                                                   scalar1=prm[:, PRM_BDT + dt_:PRM_BDT + dt_ + 1],
                                                   scalar2=None, op0=ALU.add))

                        @block.scalar
                        def _(s, tch=tch):
                            s.wait_ge(cm.h, cm.n)
                            s.activation(out=t2b[:], in_=t1b[:], func=AF.Abs)
                            s.activation(out=sqs[:], in_=t2b[:], func=AF.Exp, scale=-1.0)
                            s.activation(out=t2b[:], in_=sqs[:], func=AF.Ln, bias=1.0)
                            av.inc(s.activation(out=sqs[:], in_=t1b[:], func=AF.Relu))

                        @block.vector
                        def _(v, tch=tch):
                            v.wait_ge(av.h, av.n)
                            cm.inc(v.tensor_add(out=dtf[:, tch * 512:(tch + 1) * 512],
                                                in0=t2b[:], in1=sqs[:]))

                    if "dtf" in dout and b == 0:
                        @block.gpsimd
                        def _(g, dt_=dt_):
                            g.wait_ge(cm.h, cm.n)
                            st.dma(g.dma_start(out=dout["dtf"][:, dt_ * 8192:(dt_ + 1) * 8192]
                                               if dt_ < 2 else None, in_=dtf[:]))

                    @block.scalar
                    def _(s):
                        s.wait_ge(cm.h, cm.n)
                        load_act_set(nc, s, "exp_and_others")
                        av.inc(s.nop())

                    for ch in range(4):
                        @block.vector
                        def _(v, b=b, dt_=dt_, ch=ch):
                            v.wait_ge(cm.h, cm.n)
                            v.wait_ge(av.h, av.n)
                            v.memset(ych[:], 0.0)
                            cm.inc(v.tensor_mul(out=wch[:],
                                                in0=dtf[:, ch * 2048:(ch + 1) * 2048],
                                                in1=preb[:, ch * 2048:(ch + 1) * 2048]))

                        chhead_cm = cm.n

                        def emit_pre(k, ch=ch, dt_=dt_):
                            @block.vector
                            def _(v, k=k, avn=(avA[k - 2] if k >= 2 else 0)):
                                if avn:
                                    v.wait_ge(av.h, avn)
                                acol = PRM_A + dt_ * 16 + k
                                dst = tA if k % 2 == 0 else tB
                                sf.inc(v.tensor_scalar(out=dst[:],
                                                       in0=dtf[:, ch * 2048:(ch + 1) * 2048],
                                                       scalar1=prm[:, acol:acol + 1],
                                                       scalar2=None, op0=ALU.mult))
                            while len(dta_sf) <= k:
                                dta_sf.append(None)
                            dta_sf[k] = sf.n

                            @block.scalar
                            def _(s, k=k, sfn=dta_sf[k]):
                                s.wait_ge(sf.h, sfn)
                                src = tA if k % 2 == 0 else tB
                                av.inc(s.activation(
                                    out=dlt[:, 4096 + (k % 2) * 2048:4096 + (k % 2 + 1) * 2048],
                                    in_=src[:], func=AF.Exp))
                            while len(avA) <= k:
                                avA.append(None)
                            avA[k] = av.n

                        for n in range(NST):
                            sl = n % 2

                            @block.sync
                            def _(s, b=b, ch=ch, n=n,
                                  bfree=(dbu_sf[-1] if dbu_sf else 0),
                                  cfree=(gp_gm[-1] if gp_gm else 0)):
                                if bfree:
                                    s.wait_ge(sf.h, bfree)
                                ld.dma(s.dma_start(
                                    out=sbB[:],
                                    in_=bc_dr.ap()[b * 32 + n:b * 32 + n + 1,
                                                   ch * 2048:(ch + 1) * 2048].partition_broadcast(128)))
                                if cfree:
                                    s.wait_ge(gm.h, cfree)
                                ld.dma(s.dma_start(
                                    out=sbC[:],
                                    in_=bc_dr.ap()[b * 32 + 16 + n:b * 32 + 17 + n,
                                                   ch * 2048:(ch + 1) * 2048].partition_broadcast(128)))
                            row_ldB.append(ld.n - 16)
                            row_ld.append(ld.n)

                            if n == 0:
                                emit_pre(0)
                            if n + 1 < NST:
                                emit_pre(n + 1)

                            @block.vector
                            def _(v, ch=ch, n=n, sl=sl, avn=avA[n], ldn=row_ldB[-1],
                                  tfree=(gp_gm[-2] if len(gp_gm) >= 2 else 0),
                                  pgm=(pass_gm if (ch == 0 and n == 0) else 0)):
                                v.wait_ge(av.h, avn)
                                v.wait_ge(ld.h, ldn)
                                if tfree:
                                    v.wait_ge(gm.h, tfree)
                                if pgm:
                                    v.wait_ge(gm.h, pgm)
                                dst = tA if sl == 0 else tB
                                sf.inc(v.tensor_mul(out=dst[:], in0=wch[:], in1=sbB[:]))
                                v.wait_ge(sf.h, sf.n)
                                v.tensor_tensor_scan(out=tC[:, sl * 2048:(sl + 1) * 2048],
                                                     data0=dlt[:, 4096 + sl * 2048:4096 + (sl + 1) * 2048],
                                                     data1=dst[:], initial=carry[:, n:n + 1],
                                                     op0=ALU.mult, op1=ALU.add)
                                sf.inc(v.tensor_copy(out=sqs[:, 0:1], in_=ones_c[:]))
                                v.wait_ge(sf.h, sf.n)
                                sf.inc(v.tensor_copy(out=carry[:, n:n + 1],
                                                     in_=tC[:, sl * 2048 + 2047:sl * 2048 + 2048]))
                            dbu_sf.append(sf.n - 2)
                            main1_sf.append(sf.n)

                            @block.vector
                            def _(v, n=n, sl=sl, ldn=row_ld[-1]):
                                v.wait_ge(ld.h, ldn)
                                sf.inc(v.tensor_mul(out=zch[:], in0=tC[:, sl * 2048:(sl + 1) * 2048],
                                                    in1=sbC[:]))
                                v.wait_ge(sf.h, sf.n)
                                gm.inc(v.tensor_add(out=ych[:], in0=ych[:], in1=zch[:]))
                            gp_gm.append(gm.n)

                        @block.sync
                        def _(s, b=b, dt_=dt_, ch=ch, gmn=gp_gm[-1]):
                            s.wait_ge(cm.h, cm.n)
                            s.wait_ge(gm.h, gmn)
                            if b == 0 and dt_ == 0 and ch == 0:
                                s.wait_ge(cc[3].h, 1)
                            ld.dma(s.dma_start(out=zch[:],
                                               in_=za_o.ap()[(4 * b + ch) * 256 + dt_ * 128:
                                                             (4 * b + ch) * 256 + dt_ * 128 + 128, :]))

                        @block.scalar
                        def _(s, ch=ch):
                            s.wait_ge(ld.h, ld.n)
                            av.inc(s.activation(out=tB[:], in_=zch[:], func=AF.Tanh, scale=0.5))

                        @block.vector
                        def _(v, b=b, dt_=dt_, ch=ch, gmn=gp_gm[-1]):
                            v.wait_ge(av.h, av.n)
                            v.wait_ge(gm.h, gmn)
                            if len(yz_hist) >= 2:
                                v.wait_ge(st.h, yz_hist[-2])
                            sf.inc(v.scalar_tensor_tensor(out=ych[:],
                                                          in0=preb[:, ch * 2048:(ch + 1) * 2048],
                                                          scalar=prm[:, PRM_D + dt_:PRM_D + dt_ + 1],
                                                          in1=ych[:], op0=ALU.mult, op1=ALU.add))
                            sf.inc(v.tensor_scalar(out=tB[:], in0=tB[:], scalar1=0.5, scalar2=0.5,
                                                   op0=ALU.mult, op1=ALU.add))
                            v.wait_ge(sf.h, sf.n)
                            sf.inc(v.tensor_mul(out=tA[:], in0=zch[:], in1=tB[:]))
                            v.wait_ge(sf.h, sf.n)
                            cm.inc(v.tensor_mul(out=dlt[:, (ch % 2) * 2048:(ch % 2 + 1) * 2048],
                                                in0=ych[:], in1=tA[:]))

                        @block.gpsimd
                        def _(g, b=b, dt_=dt_, ch=ch):
                            g.wait_ge(cm.h, cm.n)
                            st.dma(g.dma_start(
                                out=yz_i.ap()[(4 * b + ch) * 256 + dt_ * 128:
                                              (4 * b + ch) * 256 + dt_ * 128 + 128, :],
                                in_=dlt[:, (ch % 2) * 2048:(ch % 2 + 1) * 2048]))
                            if "yz" in dout:
                                st.dma(g.dma_start(
                                    out=dout["yz"][:, (2 * b + dt_) * 8192 + ch * 2048:
                                                   (2 * b + dt_) * 8192 + (ch + 1) * 2048],
                                    in_=dlt[:, (ch % 2) * 2048:(ch % 2 + 1) * 2048]))
                        yz_hist.append(st.n)

            @block.gpsimd
            def _(g):
                g.wait_ge(st.h, st.n)
                with allow_ici_all_to_all():
                    cc[5].inc(g.collective_compute(
                        "AllToAll", ALU.bypass, replica_groups=GROUPS8,
                        ins=[yz_i.ap().opt()], outs=[yz_o.ap().opt()]))
            esP4c.close()
            esP4.close()
            if stop_after == "scan":
                @block.gpsimd
                def _(g):
                    g.wait_ge(cc[5].h, 1)
                    g.wait_ge(st.h, st.n)
                return _finish(nc, es, esA)

            esP5 = contextlib.ExitStack()
            E5 = esP5.enter_context
            oh5 = E5(nc.sbuf_tensor(lpfx + "oh5", [16, 4096], F32))
            cs5 = E5(nc.sbuf_tensor(lpfx + "cs5", [16, 4096], F32))
            prow = E5(nc.sbuf_tensor(lpfx + "prow", [1, 4096], F32))
            prow2 = E5(nc.sbuf_tensor(lpfx + "prow2", [1, 4096], F32))
            vrow = E5(nc.sbuf_tensor(lpfx + "vrow", [1, 4096], F32))
            grow = E5(nc.sbuf_tensor(lpfx + "grow", [1, 4096], F32))
            car16 = E5(nc.sbuf_tensor(lpfx + "car16", [16, 1], F32))
            ccol = E5(nc.sbuf_tensor(lpfx + "ccol", [128, 2], F32))
            pcol = E5(nc.sbuf_tensor(lpfx + "pcol", [128, 8], F32))
            mh5 = E5(nc.sbuf_tensor(lpfx + "mh5", [128, 16], F32))
            mlo5 = E5(nc.sbuf_tensor(lpfx + "mlo5", [128, 128], F32))
            tb32 = E5(nc.sbuf_tensor(lpfx + "tb32", [32, 128], F32))
            ttab = E5(nc.sbuf_tensor(lpfx + "ttab", [128, 32], F32))

            @block.sync
            def _(s):
                s.wait_ge(cm.h, cm.n)
                ld.dma(s.dma_start(out=prow[0:1, 0:4], in_=ccst_d))

            @block.tensor
            def _(t):
                t.wait_ge(ld.h, ld.n)
                tp.inc(t.matmul(ps[:, 0:4], ones_r[:], prow[0:1, 0:4], start=True, stop=True))

            @block.vector
            def _(v):
                v.wait_ge(tp.h, tp.n)
                cm.inc(v.tensor_copy(out=ccol[:, 0:2], in_=ps[:, 0:2]))

            @block.gpsimd
            def _(g):
                g.wait_ge(cm.h, cm.n)
                gm.inc(g.memset(car16[:], 0.0))

            for ch5 in range(4):
                @block.sync
                def _(s, ch5=ch5):
                    s.wait_ge(gm.h, gm.n)
                    s.wait_ge(cc[1].h, 2)
                    s.wait_ge(cm.h, cm.n)
                    s.wait_ge(st.h, st.n)
                    ld.dma(s.dma_start(out=prow[:],
                                       in_=idx_fu.ap()[ch5 * 4096:(ch5 + 1) * 4096, :]))
                    ld.dma(s.dma_start(out=grow[:],
                                       in_=gsv_fu.ap()[ch5 * 4096:(ch5 + 1) * 4096, :]))

                @block.tensor
                def _(t, ch5=ch5):
                    t.wait_ge(ld.h, ld.n)
                    last = None
                    for q in range(8):
                        last = t.matmul(ps[0:16, q * 512:(q + 1) * 512],
                                        ones_r[0:1, 0:16], prow[0:1, q * 512:(q + 1) * 512],
                                        start=True, stop=True)
                    tp.inc(last)

                @block.vector
                def _(v, ch5=ch5):
                    v.wait_ge(tp.h, tp.n)
                    sf.inc(v.tensor_scalar(out=oh5[:], in0=ps[0:16, 0:4096],
                                           scalar1=iotaC[0:16, 0:1], scalar2=None,
                                           op0=ALU.is_equal))
                    v.wait_ge(sf.h, sf.n)
                    v.tensor_tensor_scan(out=cs5[:], data0=oh5[:], data1=oh5[:],
                                         initial=car16[:, 0:1], op0=ALU.add, op1=ALU.bypass)
                    sf.inc(v.tensor_copy(out=sqs[:, 0:1], in_=ones_c[:]))
                    v.wait_ge(sf.h, sf.n)
                    sf.inc(v.tensor_copy(out=car16[:, 0:1], in_=cs5[:, 4095:4096]))
                    v.wait_ge(sf.h, sf.n)
                    cm.inc(v.tensor_mul(out=oh5[:], in0=oh5[:], in1=cs5[:]))

                @block.tensor
                def _(t, ch5=ch5):
                    t.wait_ge(cm.h, cm.n)
                    last = None
                    for q in range(8):
                        last = t.matmul(ps[0:1, 2048 + q * 256:2048 + q * 256 + 512]
                                        if False else ps[0:1, q * 512:q * 512 + 512],
                                        ones_c[0:16, 0:1], oh5[:, q * 512:(q + 1) * 512],
                                        start=True, stop=True)
                    tp.inc(last)

                @block.vector
                def _(v, ch5=ch5):
                    v.wait_ge(tp.h, tp.n)
                    sf.inc(v.tensor_scalar_add(out=prow2[:], in0=ps[0:1, 0:4096], scalar1=-1.0))
                    v.wait_ge(sf.h, sf.n)
                    sf.inc(v.tensor_scalar(out=vrow[:], in0=prow2[:], scalar1=1023.0,
                                           scalar2=None, op0=ALU.is_le))
                    v.wait_ge(sf.h, sf.n)
                    sf.inc(v.tensor_mul(out=grow[:], in0=grow[:], in1=vrow[:]))
                    sf.inc(v.tensor_scalar(out=prow2[:], in0=prow2[:], scalar1=1023.0,
                                           scalar2=None, op0=ALU.min))
                    v.wait_ge(sf.h, sf.n)
                    sf.inc(v.scalar_tensor_tensor(out=prow2[:], in0=prow[:], scalar=1024.0,
                                                  in1=prow2[:], op0=ALU.mult, op1=ALU.add))
                    sf.inc(v.tensor_scalar(out=vrow[:], in0=vrow[:], scalar1=-200000.0,
                                           scalar2=200000.0, op0=ALU.mult, op1=ALU.add))
                    v.wait_ge(sf.h, sf.n)
                    cm.inc(v.tensor_add(out=vrow[:], in0=vrow[:], in1=prow2[:]))

                @block.gpsimd
                def _(g, ch5=ch5):
                    g.wait_ge(cm.h, cm.n)
                    st.dma(g.dma_start(out=pcl_bo.ap()[ch5 * 4096:(ch5 + 1) * 4096, :],
                                       in_=prow2[:]))
                    st.dma(g.dma_start(out=gcl_bo.ap()[ch5 * 4096:(ch5 + 1) * 4096, :],
                                       in_=grow[:]))
                    st.dma(g.dma_start(out=p_dr.ap()[ch5 * 4096:(ch5 + 1) * 4096, :],
                                       in_=vrow[:]))
                    if "ptab" in dout:
                        st.dma(g.dma_start(out=dout["ptab"][ch5 * 4096:(ch5 + 1) * 4096, :],
                                           in_=vrow[:]))
                    if "gst" in dout:
                        st.dma(g.dma_start(out=dout["gst"][ch5 * 4096:(ch5 + 1) * 4096, :],
                                           in_=grow[:]))

            @block.gpsimd
            def _(g):
                g.wait_ge(st.h, st.n)
                with allow_ici_all_to_all():
                    cc[6].inc(g.collective_compute(
                        "AllToAll", ALU.bypass, replica_groups=GROUPS8,
                        ins=[pcl_bo.ap().opt()], outs=[pcl_sl.ap().opt()]))
                    cc[6].inc(g.collective_compute(
                        "AllToAll", ALU.bypass, replica_groups=GROUPS8,
                        ins=[gcl_bo.ap().opt()], outs=[gcl_sl.ap().opt()]))

            for tt5 in range(128):
                @block.sync
                def _(s, tt5=tt5):
                    s.wait_ge(cm.h, cm.n)
                    ld.dma(s.dma_start(out=pcol[:, 0:1],
                                       in_=p_dr.ap()[tt5 * 128:(tt5 + 1) * 128, :]))

                @block.vector
                def _(v, tt5=tt5):
                    v.wait_ge(ld.h, ld.n)
                    sf.inc(v.tensor_scalar(out=pcol[:, 1:2], in0=pcol[:, 0:1],
                                           scalar1=ccol[:, 0:1], scalar2=None, op0=ALU.subtract))
                    v.wait_ge(sf.h, sf.n)
                    sf.inc(v.tensor_scalar(out=pcol[:, 2:3], in0=pcol[:, 1:2], scalar1=0.0,
                                           scalar2=None, op0=ALU.is_lt))
                    v.wait_ge(sf.h, sf.n)
                    sf.inc(v.scalar_tensor_tensor(out=pcol[:, 1:2], in0=pcol[:, 2:3],
                                                  scalar=300000.0, in1=pcol[:, 1:2],
                                                  op0=ALU.mult, op1=ALU.add))
                    v.wait_ge(sf.h, sf.n)
                    sf.inc(v.tensor_scalar(out=pcol[:, 4:5], in0=pcol[:, 1:2],
                                           scalar1=1.0 / 128.0, scalar2=-0.49609375,
                                           op0=ALU.mult, op1=ALU.add))
                    v.wait_ge(sf.h, sf.n)
                    sf.inc(v.tensor_scalar_add(out=pcol[:, 4:5], in0=pcol[:, 4:5],
                                               scalar1=12582912.0))
                    v.wait_ge(sf.h, sf.n)
                    sf.inc(v.tensor_scalar_add(out=pcol[:, 4:5], in0=pcol[:, 4:5],
                                               scalar1=-12582912.0))
                    v.wait_ge(sf.h, sf.n)
                    sf.inc(v.scalar_tensor_tensor(out=pcol[:, 3:4], in0=pcol[:, 4:5],
                                                  scalar=-128.0, in1=pcol[:, 1:2],
                                                  op0=ALU.mult, op1=ALU.add))
                    v.wait_ge(sf.h, sf.n)
                    v.tensor_scalar(out=mh5[:], in0=iotaF[:, 0:16],
                                    scalar1=pcol[:, 4:5], scalar2=None, op0=ALU.is_equal)
                    sf.inc(v.tensor_scalar(out=mlo5[:], in0=iotaF[:],
                                           scalar1=pcol[:, 3:4], scalar2=None, op0=ALU.is_equal))
                    sf.inc(v.tensor_scalar_add(out=pcol[:, 5:6], in0=iotaC[:],
                                               scalar1=float(tt5 * 128 + 1)))
                    v.wait_ge(sf.h, sf.n)
                    cm.inc(v.tensor_scalar(out=mh5[:], in0=mh5[:],
                                           scalar1=pcol[:, 5:6], scalar2=None, op0=ALU.mult))

                @block.tensor
                def _(t, tt5=tt5):
                    t.wait_ge(cm.h, cm.n)
                    tp.inc(t.matmul(ps[:, 3584:3584 + 16], mlo5[:], mh5[:],
                                    start=(tt5 == 0), stop=(tt5 == 127)))

            @block.vector
            def _(v):
                v.wait_ge(tp.h, tp.n)
                sf.inc(v.tensor_copy(out=ttab[:, 0:16], in_=ps[:, 3584:3584 + 16]))
                v.wait_ge(sf.h, sf.n)
                sf.inc(v.tensor_scalar(out=ttab[:, 16:32], in0=ttab[:, 0:16], scalar1=0.0,
                                       scalar2=None, op0=ALU.is_equal))
                sf.inc(v.tensor_scalar_add(out=ttab[:, 0:16], in0=ttab[:, 0:16], scalar1=-1.0))
                v.wait_ge(sf.h, sf.n)
                cm.inc(v.scalar_tensor_tensor(out=ttab[:, 0:16], in0=ttab[:, 16:32],
                                              scalar=100000.0, in1=ttab[:, 0:16],
                                              op0=ALU.mult, op1=ALU.add))

            if "ttb" in dout:
                @block.gpsimd
                def _(g):
                    g.wait_ge(cm.h, cm.n)
                    st.dma(g.dma_start(out=dout["ttb"], in_=ttab[:]))

            @block.scalar
            def _(s):
                s.wait_ge(cm.h, cm.n)
                av.inc(s.copy(out=tu32[:], in_=ttab[:, 0:16]))
            ttab_done = av.n

            esP5.close()
            esP5b = contextlib.ExitStack()
            E5b = esP5b.enter_context
            dtok = E5b(nc.sbuf_tensor(lpfx + "dtok", [128, DM], BF16))
            dtokf = E5b(nc.sbuf_tensor(lpfx + "dtokf", [128, DM], F32))
            dfm = E5b(nc.sbuf_tensor(lpfx + "dfm", [128, 16 * 1024], BF16))
            wld5 = E5b(nc.sbuf_tensor(lpfx + "wld5", [128, 2 * 1024], F32))
            wst5 = E5b(nc.sbuf_tensor(lpfx + "wst5", [128, 2 * 1024], BF16))
            hbm = E5b(nc.sbuf_tensor(lpfx + "hbm", [128, 32 * 1024], BF16))
            eof = E5b(nc.sbuf_tensor(lpfx + "eof", [128, 1024], F32))
            eotk = E5b(nc.sbuf_tensor(lpfx + "eotk", [128, 8 * DM], BF16))

            for g5 in range(16):
                @block.gpsimd
                def _(g, g5=g5):
                    if g5 == 0:
                        g.wait_ge(av.h, ttab_done)
                        g.wait_ge(cc[0].h, 1)
                        r = g.alloc_register()
                        g.reg_mov(r, 16383)
                        nc._moe_breg = r
                    g.wait_ge(cm.h, cm.n)
                    gm.inc(g.memset(dtok[:], 0.0))

                @block.gpsimd
                def _(g, g5=g5):
                    g.wait_ge(gm.h, gm.n)
                    st.dma(g.indirect_dma_start(
                        out=dtok[:], out_offset=None,
                        in_=xn1_fu.ap(), in_offset=IndirectOffsetOnAxis(tu32[:, g5:g5 + 1], 0),
                        bounds_check=nc._moe_breg, oob_is_err=False))

                @block.vector
                def _(v, g5=g5):
                    v.wait_ge(st.h, st.n)
                    cm.inc(v.tensor_copy(out=dtokf[:], in_=dtok[:]))

                @block.tensor
                def _(t, g5=g5):
                    t.wait_ge(cm.h, cm.n)
                    for ct in range(NCT):
                        tp.inc(t.transpose(ps[:, ct * 512:ct * 512 + 128],
                                           dtokf[:, ct * 128:(ct + 1) * 128], ident[:]))

                @block.vector
                def _(v, g5=g5):
                    v.wait_ge(tp.h, tp.n)
                    last = None
                    for ct in range(NCT):
                        last = v.tensor_copy(out=dfm[:, ct * 2048 + g5 * 128:ct * 2048 + (g5 + 1) * 128],
                                             in_=ps[:, ct * 512:ct * 512 + 128])
                    cm.inc(last)
            disp_done = cm.n

            cm_cast = {}
            tp_mm = {}
            av_re = {}
            ld_ld = {}
            ev_cm = {}
            li = 0
            for e5 in range(2):
                e_first = li
                for phase in range(2):
                    for sub5 in range(32):
                        if phase == 0:
                            fi = sub5
                        else:
                            ot, kq = sub5 // 4, sub5 % 4
                        bb = li % 2

                        @block.sync
                        def _(s, e5=e5, phase=phase, sub5=sub5, li=li, bb=bb):
                            if li >= 2:
                                s.wait_ge(cm.h, cm_cast[li - 2])
                            else:
                                s.wait_ge(cm.h, cm.n)
                                s.wait_ge(tp.h, tp.n)
                            for kt in range(8):
                                if phase == 0:
                                    src = w1s_d[e5 * 1024 + kt * 128:e5 * 1024 + (kt + 1) * 128,
                                                sub5 * 128:(sub5 + 1) * 128]
                                else:
                                    ot, kq = sub5 // 4, sub5 % 4
                                    src = w2s_d[e5 * 4096 + kq * 1024 + kt * 128:
                                                e5 * 4096 + kq * 1024 + (kt + 1) * 128,
                                                ot * 128:(ot + 1) * 128]
                                ld.dma(s.dma_start(
                                    out=wld5[:, bb * 1024 + kt * 128:bb * 1024 + (kt + 1) * 128],
                                    in_=src))
                        ld_ld[li] = ld.n

                        @block.vector
                        def _(v, li=li, bb=bb):
                            v.wait_ge(ld.h, ld_ld[li])
                            if li >= 2:
                                v.wait_ge(tp.h, tp_mm[li - 2])
                            cm.inc(v.tensor_copy(out=wst5[:, bb * 1024:(bb + 1) * 1024],
                                                 in_=wld5[:, bb * 1024:(bb + 1) * 1024]))
                        cm_cast[li] = cm.n

                        @block.tensor
                        def _(t, e5=e5, phase=phase, sub5=sub5, li=li, bb=bb, e_first=e_first):
                            t.wait_ge(cm.h, cm_cast[li])
                            if phase == 0:
                                fi = sub5
                                if fi >= 2:
                                    t.wait_ge(av.h, av_re[e_first + fi - 2])
                                for kt in range(8):
                                    for hh in range(2):
                                        ins = t.matmul(
                                            ps[:, (fi % 2) * 1024 + hh * 512:
                                               (fi % 2) * 1024 + hh * 512 + 512],
                                            wst5[:, bb * 1024 + kt * 128:bb * 1024 + (kt + 1) * 128],
                                            dfm[:, kt * 2048 + e5 * 1024 + hh * 512:
                                                kt * 2048 + e5 * 1024 + hh * 512 + 512],
                                            start=(kt == 0), stop=(kt == 7))
                                tp.inc(ins)
                            else:
                                ot, kq = sub5 // 4, sub5 % 4
                                if sub5 == 0:
                                    t.wait_ge(av.h, av_re[e_first + 31])
                                if kq == 0 and ot >= 1:
                                    t.wait_ge(cm.h, ev_cm[(e5, ot - 1)])
                                for kt in range(8):
                                    for hh in range(2):
                                        ins = t.matmul(
                                            ps[:, 2048 + hh * 512:2048 + hh * 512 + 512],
                                            wst5[:, bb * 1024 + kt * 128:bb * 1024 + (kt + 1) * 128],
                                            hbm[:, (kq * 8 + kt) * 1024 + hh * 512:
                                                (kq * 8 + kt) * 1024 + hh * 512 + 512],
                                            start=(kq == 0 and kt == 0),
                                            stop=(kq == 3 and kt == 7))
                                tp.inc(ins)
                        tp_mm[li] = tp.n

                        if phase == 0:
                            fi = sub5

                            @block.scalar
                            def _(s, e5=e5, fi=fi, li=li):
                                s.wait_ge(tp.h, tp_mm[li])
                                if e5 == 1 and fi == 0:
                                    s.wait_ge(tp.h, tp_mm[li - 1])
                                last = None
                                for hh in range(2):
                                    last = s.activation(
                                        out=hbm[:, fi * 1024 + hh * 512:fi * 1024 + hh * 512 + 512],
                                        in_=ps[:, (fi % 2) * 1024 + hh * 512:
                                               (fi % 2) * 1024 + hh * 512 + 512],
                                        func=AF.Relu)
                                av.inc(last)
                            av_re[li] = av.n
                        elif sub5 % 4 == 3:
                            ot = sub5 // 4

                            @block.vector
                            def _(v, e5=e5, ot=ot, li=li):
                                v.wait_ge(tp.h, tp_mm[li])
                                if e5 == 1 and ot == 0:
                                    v.wait_ge(st.h, st.n)
                                cm.inc(v.tensor_copy(out=eof[:], in_=ps[:, 2048:3072]))

                            @block.tensor
                            def _(t, ot=ot):
                                t.wait_ge(cm.h, cm.n)
                                for sub in range(8):
                                    tp.inc(t.transpose(
                                        ps[:, 3072 + sub * 128:3072 + (sub + 1) * 128],
                                        eof[:, sub * 128:(sub + 1) * 128], ident[:]))

                            @block.vector
                            def _(v, ot=ot):
                                v.wait_ge(tp.h, tp.n)
                                last = None
                                for sub in range(8):
                                    last = v.tensor_copy(
                                        out=eotk[:, sub * DM + ot * 128:sub * DM + (ot + 1) * 128],
                                        in_=ps[:, 3072 + sub * 128:3072 + (sub + 1) * 128])
                                cm.inc(last)
                            ev_cm[(e5, ot)] = cm.n
                        li += 1

                @block.gpsimd
                def _(g, e5=e5):
                    g.wait_ge(cm.h, ev_cm[(e5, 7)])
                    for sub in range(8):
                        st.dma(g.dma_start(
                            out=eo_bo.ap()[e5 * 1024 + sub * 128:e5 * 1024 + (sub + 1) * 128, :],
                            in_=eotk[:, sub * DM:(sub + 1) * DM]))

            @block.gpsimd
            def _(g):
                g.wait_ge(st.h, st.n)
                cc[7].inc(g.collective_compute(
                    "AllGather", ALU.bypass, replica_groups=GROUPS8,
                    ins=[eo_bo.ap().opt()], outs=[eo_fu.ap().opt()]))

            if "eo" in dout:
                @block.gpsimd
                def _(g):
                    g.wait_ge(cc[7].h, 1)
                    st.dma(g.dma_start(out=dout["eo"], in_=eo_bo.ap()))
            esP5b.close()
            if stop_after == "moe":
                @block.gpsimd
                def _(g):
                    g.wait_ge(cc[6].h, 2)
                    g.wait_ge(cc[7].h, 1)
                    g.wait_ge(st.h, st.n)
                return _finish(nc, es, esA)

            esP6 = contextlib.ExitStack()
            E6 = esP6.enter_context
            wob = E6(nc.sbuf_tensor(lpfx + "wob", [128, 16 * 1024], BF16))
            wld6 = E6(nc.sbuf_tensor(lpfx + "wld6", [128, 1024], F32))
            yzl = E6(nc.sbuf_tensor(lpfx + "yzl", [128, 2 * 16 * 128], F32))
            yzb = E6(nc.sbuf_tensor(lpfx + "yzb", [128, 2 * 16 * 128], BF16))
            s1t = E6(nc.sbuf_tensor(lpfx + "s1t", [128, 2 * DM], F32))
            eor = E6(nc.sbuf_tensor(lpfx + "eor", [128, 2 * DM], BF16))
            g1t = E6(nc.sbuf_tensor(lpfx + "g1t", [128, 2 * DM], F32))
            x1t = E6(nc.sbuf_tensor(lpfx + "x1t", [128, 2 * DM], F32))
            pcol6 = E6(nc.sbuf_tensor(lpfx + "pcol6", [128, 2 * 2], F32))
            pu32 = E6(nc.sbuf_tensor(lpfx + "pu32", [128, 2 * 1], U32))
            if "x1late" in dout:
                x1t2 = E6(nc.sbuf_tensor(lpfx + "x1t2", [128, DM], F32))

            for dt6 in range(16):
                @block.sync
                def _(s, dt6=dt6):
                    s.wait_ge(cm.h, cm.n)
                    ld.dma(s.dma_start(out=wld6[:], in_=wout_d[dt6 * 128:(dt6 + 1) * 128, :]))

                @block.vector
                def _(v, dt6=dt6):
                    v.wait_ge(ld.h, ld.n)
                    cm.inc(v.tensor_copy(out=wob[:, dt6 * 1024:(dt6 + 1) * 1024], in_=wld6[:]))

            p6_ld, p6_cast, p6_mm, p6_g, p6_comb, p6_st, p6_gst, p6_sil = [], [], [], [], [], [], [], []
            for tt6 in range(NTT):
                s6 = tt6 % 2
                O = s6 * DM
                OY = s6 * 2048

                @block.sync
                def _(s, tt6=tt6, s6=s6, O=O, OY=OY,
                      xfree=(p6_st[tt6 - 2] if tt6 >= 2 else 0),
                      yfree=(p6_cast[tt6 - 2] if tt6 >= 2 else 0),
                      pfree=(p6_sil[tt6 - 2] if tt6 >= 2 else 0)):
                    if tt6 == 0:
                        s.wait_ge(cm.h, cm.n)
                        s.wait_ge(st.h, st.n)
                        s.wait_ge(cc[5].h, 1)
                        s.wait_ge(cc[6].h, 2)
                    if xfree:
                        s.wait_ge(st.h, xfree)
                    if yfree:
                        s.wait_ge(cm.h, yfree)
                    if pfree:
                        s.wait_ge(cm.h, pfree)
                    for dt6 in range(16):
                        ld.dma(s.dma_start(
                            out=yzl[:, OY + dt6 * 128:OY + (dt6 + 1) * 128],
                            in_=yz_o.ap()[dt6 * 128:(dt6 + 1) * 128,
                                          tt6 * 128:(tt6 + 1) * 128]))
                    ld.dma(s.dma_start(out=x1t[:, O:O + DM], in_=x1_d[tt6 * 128:(tt6 + 1) * 128, :]))
                    ld.dma(s.dma_start(out=pcol6[:, s6 * 2:s6 * 2 + 1],
                                       in_=pcl_sl.ap()[tt6 * 128:(tt6 + 1) * 128, :]))
                    ld.dma(s.dma_start(out=pcol6[:, s6 * 2 + 1:s6 * 2 + 2],
                                       in_=gcl_sl.ap()[tt6 * 128:(tt6 + 1) * 128, :]))
                p6_ld.append(ld.n)

                @block.vector
                def _(v, tt6=tt6, OY=OY, ldn=p6_ld[-1],
                      bfree=(p6_mm[tt6 - 2] if tt6 >= 2 else 0)):
                    v.wait_ge(ld.h, ldn)
                    if bfree:
                        v.wait_ge(tp.h, bfree)
                    last = None
                    for dt6 in range(16):
                        last = v.tensor_copy(out=yzb[:, OY + dt6 * 128:OY + (dt6 + 1) * 128],
                                             in_=yzl[:, OY + dt6 * 128:OY + (dt6 + 1) * 128])
                    if "x1late" in dout:
                        last = v.tensor_copy(out=x1t2[:], in_=x1t[:, (OY // 2):(OY // 2) + DM])
                    cm.inc(last)
                p6_cast.append(cm.n)

                @block.scalar
                def _(s, tt6=tt6, s6=s6, ldn=p6_ld[-1],
                      gdone=(p6_gst[tt6 - 1] if tt6 >= 1 else 0)):
                    s.wait_ge(ld.h, ldn)
                    if gdone:
                        s.wait_ge(st.h, gdone)
                    av.inc(s.copy(out=pu32[:, s6:s6 + 1], in_=pcol6[:, s6 * 2:s6 * 2 + 1]))
                p6_pu = av.n

                @block.gpsimd
                def _(g, tt6=tt6, s6=s6, O=O, avn=p6_pu,
                      efree=(p6_g[tt6 - 2] if tt6 >= 2 else 0)):
                    g.wait_ge(av.h, avn)
                    if efree:
                        g.wait_ge(sf.h, efree)
                    if tt6 == 0:
                        g.wait_ge(cc[7].h, 1)
                        g.memset(eor[:], 0.0)
                    st.dma(g.indirect_dma_start(
                        out=eor[:, O:O + DM], out_offset=None,
                        in_=eo_fu.ap(), in_offset=IndirectOffsetOnAxis(pu32[:, s6:s6 + 1], 0),
                        bounds_check=nc._moe_breg, oob_is_err=False))
                p6_gst.append(st.n)

                @block.tensor
                def _(t, tt6=tt6, OY=OY, s6=s6, cmn=p6_cast[-1],
                      sfree=(p6_comb[tt6 - 2] if tt6 >= 2 else 0)):
                    t.wait_ge(cm.h, cmn)
                    if sfree:
                        t.wait_ge(cm.h, sfree)
                    for chn in range(2):
                        last = None
                        for dt6 in range(16):
                            last = t.matmul(ps[:, s6 * 1024 + chn * 512:s6 * 1024 + chn * 512 + 512],
                                            yzb[:, OY + dt6 * 128:OY + (dt6 + 1) * 128],
                                            wob[:, dt6 * 1024 + chn * 512:dt6 * 1024 + (chn + 1) * 512],
                                            start=(dt6 == 0), stop=(dt6 == 15))
                        tp.inc(last)
                p6_mm.append(tp.n)

                @block.vector
                def _(v, tt6=tt6, s6=s6, O=O, tpn=p6_mm[-1]):
                    v.wait_ge(tp.h, tpn)
                    v.tensor_copy(out=s1t[:, O:O + 512], in_=ps[:, s6 * 1024:s6 * 1024 + 512])
                    cm.inc(v.tensor_copy(out=s1t[:, O + 512:O + 1024],
                                         in_=ps[:, s6 * 1024 + 512:s6 * 1024 + 1024]))
                p6_comb.append(cm.n)

                @block.vector
                def _(v, tt6=tt6, s6=s6, O=O, gstn=p6_gst[-1]):
                    v.wait_ge(st.h, gstn)
                    sf.inc(v.tensor_scalar(out=g1t[:, O:O + DM], in0=eor[:, O:O + DM],
                                           scalar1=pcol6[:, s6 * 2 + 1:s6 * 2 + 2],
                                           scalar2=None, op0=ALU.mult))
                p6_g.append(sf.n)

                @block.scalar
                def _(s, tt6=tt6, O=O, sfn=p6_g[-1]):
                    s.wait_ge(sf.h, sfn)
                    av.inc(s.activation(out=sqs[:, 0:512],
                                        in_=g1t[:, O:O + 512], func=AF.Tanh, scale=0.5))
                @block.scalar
                def _(s, tt6=tt6, O=O):
                    av.inc(s.activation(out=t1b[:, 0:512],
                                        in_=g1t[:, O + 512:O + 1024], func=AF.Tanh, scale=0.5))

                @block.vector
                def _(v, tt6=tt6, s6=s6, O=O):
                    v.wait_ge(av.h, av.n)
                    v.wait_ge(cm.h, p6_comb[-1])
                    v.tensor_scalar(out=sqs[:, 0:512], in0=sqs[:, 0:512], scalar1=0.5,
                                    scalar2=0.5, op0=ALU.mult, op1=ALU.add)
                    v.tensor_scalar(out=t1b[:, 0:512], in0=t1b[:, 0:512], scalar1=0.5,
                                    scalar2=0.5, op0=ALU.mult, op1=ALU.add)
                    v.tensor_mul(out=g1t[:, O:O + 512], in0=g1t[:, O:O + 512], in1=sqs[:, 0:512])
                    v.tensor_mul(out=g1t[:, O + 512:O + 1024], in0=g1t[:, O + 512:O + 1024],
                                 in1=t1b[:, 0:512])
                    v.tensor_mul(out=g1t[:, O:O + DM], in0=s1t[:, O:O + DM], in1=g1t[:, O:O + DM])
                    cm.inc(v.tensor_add(out=x1t[:, O:O + DM], in0=x1t[:, O:O + DM],
                                        in1=g1t[:, O:O + DM]))
                p6_sil.append(cm.n)

                @block.gpsimd
                def _(g, tt6=tt6, O=O, cmn=p6_sil[-1]):
                    g.wait_ge(cm.h, cmn)
                    st.dma(g.dma_start(out=x2_d[tt6 * 128:(tt6 + 1) * 128, :], in_=x1t[:, O:O + DM]))
                    if "s1" in dout:
                        st.dma(g.dma_start(out=dout["s1"][tt6 * 128:(tt6 + 1) * 128, :],
                                           in_=s1t[:, O:O + DM]))
                    if "g1" in dout:
                        st.dma(g.dma_start(out=dout["g1"][tt6 * 128:(tt6 + 1) * 128, :],
                                           in_=g1t[:, O:O + DM]))
                    if "x1late" in dout:
                        st.dma(g.dma_start(out=dout["x1late"][tt6 * 128:(tt6 + 1) * 128, :],
                                           in_=x1t2[:]))
                p6_st.append(st.n)

            @block.gpsimd
            def _(g):
                g.wait_ge(st.h, st.n)
            esP6.close()
    return nc



def l1_in_map(inp, core, x1):
    m = dict(
        win_d=inp["L1_attn_W_in"],
        ck_d=np.ascontiguousarray(inp["L1_attn_conv_k"].reshape(4, DI)[:, core * DSL:(core + 1) * DSL]),
        cb_d=np.ascontiguousarray(inp["L1_attn_conv_b"].reshape(16, 128)[2 * core:2 * core + 2]),
        wxp_d=np.ascontiguousarray(inp["L1_attn_W_xproj"][core * DSL:(core + 1) * DSL]),
        wdt_d=np.ascontiguousarray(inp["L1_attn_W_dt"][:, core * DSL:(core + 1) * DSL]),
        bdt_d=np.ascontiguousarray(inp["L1_attn_b_dt"].reshape(16, 128)[2 * core:2 * core + 2]),
        alog_d=np.ascontiguousarray(inp["L1_attn_A_log"][core * DSL:(core + 1) * DSL]),
        dv_d=np.ascontiguousarray(inp["L1_attn_D"].reshape(16, 128)[2 * core:2 * core + 2]),
        wout_d=inp["L1_attn_W_out"],
        wg_d=inp["L1_ffn_Wg"],
        w1s_d=np.ascontiguousarray(inp["L1_ffn_w1"][2 * core:2 * core + 2].reshape(2 * DM, 4096)),
        w2s_d=np.ascontiguousarray(inp["L1_ffn_w2"][2 * core:2 * core + 2].reshape(2 * 4096, DM)),
        ccst_d=np.array([[2048.0 * core, 2048.0 * core, 0.0, 0.0]], np.float32),
    )
    if x1 is not None:
        m["x1"] = np.ascontiguousarray(x1)
    return m


def build_fused(races=True, dbg0=(), dbg1=()):
    nc = bass.Bass("TRN2", target_bir_lowering=False, detect_race_conditions=races)
    es0 = contextlib.ExitStack()
    with es0:
        fz = dict(nc=nc, E=es0.enter_context,
                  bar=Sem(es0.enter_context(nc.semaphore("bar"))))
        build_l0(fuse=fz, dbg=dbg0)
        build_l1(fuse=fz, dbg=dbg1)
    return nc


def kernel(**inputs):
    inp = {k: np.asarray(v) for k, v in inputs.items()}
    nc = build_fused()
    maps = []
    for c in range(NCORES):
        m = l0_in_map(inp, c)
        m.update(l1_in_map(inp, c, None))
        maps.append(m)
    res = run_bass_kernel_spmd(nc, maps, core_ids=list(range(NCORES)))
    out = np.concatenate([np.asarray(res.results[c]["x2"], np.float32)
                          for c in range(NCORES)], 0)
    return out.reshape(2, 8192, DM)
